# version 02
# baseline (speedup 1.0000x reference)
import math
from contextlib import ExitStack

import numpy as np
import ml_dtypes
import concourse.bass as bass
import concourse.mybir as mybir
from concourse.bass_utils import run_bass_kernel_spmd

F32 = mybir.dt.float32
BF16 = mybir.dt.bfloat16
AF = mybir.ActivationFunctionType
OP = mybir.AluOpType
AX = mybir.AxisListType

D = 2048
NDC = 16
TLOC = 2048
TT = 1024
NTT = TLOC // TT
INTER = 1408
NIC = 11
NE = 8
NSH = 2816
NHC = 22
HEADS = 16
DH = 128
SEQ = 8192
GRP = 4
NKC = SEQ // 128
ISQ = 1.0 / math.sqrt(128.0)
M_STAT = 33
CONSTS = (0.5, 0.25, -0.5, -1.0, 1e-5)


class Res:
    __slots__ = ("g",)

    def __init__(self):
        self.g = {}

    def rd(self, sem, val):
        if self.g.get(sem, 0) < val:
            self.g[sem] = val

    def wait_w(self, b, eng):
        for s, v in self.g.items():
            eng.wait_ge(b.sem(s), v)
        self.g = {}


class Builder:
    def __init__(self, nc, dbg=(), phases=("ffn1", "qkv", "gather", "attn", "ffn2"),
                 lim=None):
        self.nc = nc
        self.dbg = set(dbg)
        self.phases = set(phases)
        self.lim = lim or {}

    def _mksem(self, name):
        self.sems[name] = [self.ctx.enter_context(self.nc.semaphore(name)), 0]

    def sem(self, n):
        return self.sems[n][0]

    def val(self, n):
        return self.sems[n][1]

    def inc(self, n, instr, k=1):
        s = self.sems[n]
        instr.then_inc(s[0], k)
        s[1] += k
        return s[1]

    def snap(self):
        return {n: v[1] for n, v in self.sems.items()}

    def wait_snap(self, eng, snapshot):
        for n, v in snapshot.items():
            if v > 0:
                eng.wait_ge(self.sem(n), v)

    def phase_barrier(self):
        snap = self.phase_snap
        for sec in (self.T, self.V, self.A, self.S, self.G):
            def _f(eng, snap=snap):
                self.wait_snap(eng, snap)
            sec(_f)

    def qdrain(self, eng, qsem):
        if self.val(qsem) > 0:
            eng.wait_ge(self.sem(qsem), self.val(qsem))

    def T(self, f): self.block.tensor(f)
    def V(self, f): self.block.vector(f)
    def A(self, f): self.block.scalar(f)
    def G(self, f): self.block.gpsimd(f)
    def S(self, f): self.block.sync(f)

    def build(self):
        nc = self.nc
        EI = dict(kind="ExternalInput")
        self.xT = nc.dram_tensor("xT", [D, TLOC], F32, **EI).ap()
        self.wq_p = nc.dram_tensor("wq_p", [NDC, 128, NDC * 128], F32, **EI).ap()
        self.wk_p = nc.dram_tensor("wk_p", [NDC, 128, NDC * 128], F32, **EI).ap()
        self.wv_p = nc.dram_tensor("wv_p", [NDC, 128, D], F32, **EI).ap()
        self.wo_p = nc.dram_tensor("wo_p", [NDC, 128, NDC * 128], F32, **EI).ap()
        self.biases = nc.dram_tensor("biases", [128, 4 * NDC], F32, **EI).ap()
        self.bv_r = nc.dram_tensor("bv_r", [1, D], F32, **EI).ap()
        self.bq_r = nc.dram_tensor("bq_r", [1, D], F32, **EI).ap()
        self.bk_r = nc.dram_tensor("bk_r", [1, D], F32, **EI).ap()
        self.helix_t = nc.dram_tensor("helix_t", [128, NDC], F32, **EI).ap()
        self.helix_r = nc.dram_tensor("helix_r", [1, D], F32, **EI).ap()
        self.spiral_t = nc.dram_tensor("spiral_t", [128, NDC], F32, **EI).ap()
        self.spiral_r = nc.dram_tensor("spiral_r", [1, D], F32, **EI).ap()
        self.gwT_p = nc.dram_tensor("gwT_p", [NDC, 128, NE], F32, **EI).ap()
        self.rb_d = nc.dram_tensor("rb", [NE, 1], F32, **EI).ap()
        self.w1_p = nc.dram_tensor("w1_p", [NE, NIC, 128, NDC * 128], F32, **EI).ap()
        self.w3_p = nc.dram_tensor("w3_p", [NE, NIC, 128, NDC * 128], F32, **EI).ap()
        self.w2_p = nc.dram_tensor("w2_p", [NE, NDC, 128, NIC * 128], F32, **EI).ap()
        self.sw1_p = nc.dram_tensor("sw1_p", [NHC, 128, NDC * 128], F32, **EI).ap()
        self.sw3_p = nc.dram_tensor("sw3_p", [NHC, 128, NDC * 128], F32, **EI).ap()
        self.sw2_p = nc.dram_tensor("sw2_p", [NDC, 128, NHC * 128], F32, **EI).ap()
        self.identf_d = nc.dram_tensor("identf", [128, 128], F32, **EI).ap()
        self.consts_d = nc.dram_tensor("consts", [1, 8], F32, **EI).ap()

        self.outT = nc.dram_tensor("outT", [D, TLOC], F32, kind="ExternalOutput").ap()

        self.x_res = nc.dram_tensor("x_res", [D, TLOC], F32).ap()
        self.xnA = nc.dram_tensor("xnA", [D, TLOC], BF16).ap()
        self.q_dr = nc.dram_tensor("q_dr", [D, TLOC], BF16).ap()
        self.g_dr = nc.dram_tensor("g_dr", [NE, TT], BF16).ap()
        self.row_dr = nc.dram_tensor("row_dr", [4, D], F32).ap()
        self.kc_in = nc.dram_tensor("kc_in", [D, TLOC], BF16)
        self.vc_in = nc.dram_tensor("vc_in", [TLOC, D], BF16)
        self.kg = nc.dram_tensor("kg", [GRP * D, TLOC], BF16)
        self.vg = nc.dram_tensor("vg", [GRP * TLOC, D], BF16)

        self.dbg_t = {}
        for name in self.dbg:
            shp = [TLOC, D] if name == "dbg_v" else [D, TLOC]
            dt = BF16 if name in ("dbg_xnA", "dbg_q", "dbg_k", "dbg_v") else F32
            self.dbg_t[name] = nc.dram_tensor(name, shp, dt, kind="ExternalOutput").ap()

        with ExitStack() as ctx:
            self.ctx = ctx
            self.block = ctx.enter_context(nc.Block())
            self.sems = {}
            for n in ("sw", "sx", "smm", "sac", "sdv", "scc"):
                self._mksem(n)

            sb = lambda name, shape, dt: ctx.enter_context(nc.sbuf_tensor(name, shape, dt))
            self.combo = sb("combo", [128, NDC * M_STAT], F32)
            self.onesf = sb("onesf", [1, 128], F32)
            self.onesbr = sb("onesbr", [1, 128], BF16)
            self.ones_bf = sb("ones_bf", [128, 1], BF16)
            self.bias_t = sb("bias_t", [128, 4 * NDC], F32)
            self.rb8 = sb("rb8", [NE, 1], F32)
            self.gsum = sb("gsum", [NE, 1], F32)
            self.idf = sb("idf", [128, 128], F32)
            self.ones8f = sb("ones8f", [1, NE], F32)
            self.ones8c = sb("ones8c", [NE, 1], F32)
            self.consts = sb("consts_sb", [1, 8], F32)
            for ci, cv in enumerate(CONSTS):
                nc.const_aps.register(F32, cv, self.consts[0:1, ci:ci + 1])

            self.prologue()
            if "ffn1" in self.phases:
                self.ffn_call(1)
            if "qkv" in self.phases:
                self.qkv_phase()
            if "gather" in self.phases:
                self.gather_phase()
            if "attn" in self.phases:
                self.attn_phase()
            if "kdump" in self.phases:
                kdo = nc.dram_tensor("kdump", [GRP * D, TLOC], BF16, kind="ExternalOutput").ap()
                vdo = nc.dram_tensor("vdump", [GRP * TLOC, D], BF16, kind="ExternalOutput").ap()
                @self.S
                def _(sy):
                    self.qdrain(sy, 'sx')
                    sy.wait_ge(self.sem("scc"), self.val("scc"))
                    self.inc("sx", sy.dma_start(out=kdo, in_=self.kg.ap()), 16)
                    self.inc("sx", sy.dma_start(out=vdo, in_=self.vg.ap()), 16)
            if "xdump" in self.phases:
                @self.S
                def _(sy):
                    self.qdrain(sy, 'sx')
                    self.wait_snap(sy, self.phase_snap)
                    sy.wait_ge(self.sem("sw"), self.val("sw"))
                    self.inc("sx", sy.dma_start(out=self.outT, in_=self.x_res), 16)
            if "ffn2" in self.phases:
                self.ffn_call(2)
            self.final_wait()
        return nc

    def prologue(self):
        nc = self.nc
        with ExitStack() as pctx:
            trow = pctx.enter_context(nc.sbuf_tensor("trow", [1, 6 * D], F32))
            gps = pctx.enter_context(nc.psum_tensor("gsum_ps", [NE, 8], F32))

            @self.S
            def _(sy):
                self.qdrain(sy, 'sx')
                for dst, src in ((self.rb8[:, :], self.rb_d),
                                 (self.bias_t[:, :], self.biases),
                                 (self.idf[:, :], self.identf_d),
                                 (self.consts[:, :], self.consts_d),
                                 (trow[0:1, 0:D], self.helix_r),
                                 (trow[0:1, 4 * D:5 * D], self.bq_r),
                                 (trow[0:1, 5 * D:6 * D], self.bk_r)):
                    self.inc("sx", sy.dma_start(out=dst, in_=src), 16)
            ld0 = self.val("sx")

            @self.G
            def _(g):
                self.qdrain(g, 'sw')
                for dc in range(NDC):
                    self.inc("sw", g.dma_start(out=self.combo[:, dc * M_STAT: dc * M_STAT + NE],
                                               in_=self.gwT_p[dc]), 16)
                self.inc("sw", g.dma_start(out=trow[0:1, D:2 * D], in_=self.spiral_r), 16)
            ldw0 = self.val("sw")

            @self.V
            def _(v):
                v.memset(self.onesf[:, :], 1.0)
                v.memset(self.onesbr[:, :], 1.0)
                v.memset(self.ones_bf[:, :], 1.0)
                v.memset(self.ones8f[:, :], 1.0)
                v.memset(self.ones8c[:, :], 1.0)
                for dc in range(NDC):
                    v.memset(self.combo[:, dc * M_STAT + NE: dc * M_STAT + 32], 0.0)
                ii = None
                for dc in range(NDC):
                    ii = v.memset(self.combo[:, dc * M_STAT + 32: dc * M_STAT + 33], 1.0)
                self.inc("sdv", ii)
            mz = self.val("sdv")

            @self.A
            def _(a):
                a.wait_ge(self.sem("sx"), ld0)
                a.wait_ge(self.sem("sw"), ldw0)
                a.activation(out=trow[0:1, 2 * D:3 * D], in_=trow[0:1, 0:D], func=AF.Sin, scale=0.5)
                ii = a.activation(out=trow[0:1, 3 * D:4 * D], in_=trow[0:1, 0:D], func=AF.Sin, scale=0.25)
                self.inc("sac", ii)
            sin_done = self.val("sac")

            @self.V
            def _(v):
                v.wait_ge(self.sem("sac"), sin_done)
                v.wait_ge(self.sem("sdv"), mz)
                r_cq = trow[0:1, 0:D]
                r_sw = trow[0:1, D:2 * D]
                r_s2 = trow[0:1, 2 * D:3 * D]
                r_s4 = trow[0:1, 3 * D:4 * D]
                v.tensor_tensor(r_cq, r_s2, r_s2, OP.mult)
                v.tensor_scalar(r_cq, r_cq, -2.0, 1.0, OP.mult, OP.add)
                v.tensor_tensor(r_cq, r_cq, r_sw, OP.mult)
                v.tensor_scalar_mul(r_cq, r_cq, ISQ)
                v.tensor_tensor(r_sw, r_s4, r_s4, OP.mult)
                v.tensor_scalar(r_sw, r_sw, -2.0, 1.0, OP.mult, OP.add)
                v.tensor_tensor(r_s2, r_s2, r_sw, OP.mult)
                v.tensor_scalar_mul(trow[0:1, D:2 * D], r_s2, 2.0)
                v.tensor_tensor(trow[0:1, 2 * D:3 * D], trow[0:1, 4 * D:5 * D], r_cq, OP.mult)
                self.inc("sdv", v.tensor_tensor(trow[0:1, 3 * D:4 * D], trow[0:1, 5 * D:6 * D],
                                                trow[0:1, D:2 * D], OP.mult))
            trig_dv = self.val("sdv")

            @self.S
            def _(sy):
                self.qdrain(sy, 'sx')
                sy.wait_ge(self.sem("sdv"), trig_dv)
                self.inc("sx", sy.dma_start(out=self.row_dr.rearrange("a d -> (a d)").rearrange("(o n) -> o n", o=1),
                                            in_=trow[0:1, 0:4 * D]), 16)
            self.rows_stored = self.val("sx")

            @self.T
            def _(t):
                t.wait_ge(self.sem("sw"), ldw0)
                t.wait_ge(self.sem("sdv"), mz)
                ii = None
                for dc in range(NDC):
                    ii = t.matmul(gps[0:NE, 0:1], self.combo[:, dc * M_STAT: dc * M_STAT + NE],
                                  self.combo[:, dc * M_STAT + 32: dc * M_STAT + 33],
                                  start=(dc == 0), stop=(dc == NDC - 1))
                self.inc("smm", ii)
            gmm = self.val("smm")

            @self.V
            def _(v):
                v.wait_ge(self.sem("smm"), gmm)
                self.inc("sdv", v.tensor_copy(self.gsum[:, :], gps[0:NE, 0:1]))
            self.phase_snap = self.snap()

    def ln_stats(self, x_sb, xsq, ps_sum, ps_sq, rows, Rxsq):
        sq_done = []
        mm_after = []
        for dc in range(NDC):
            @self.V
            def _(v):
                if dc >= 2:
                    v.wait_ge(self.sem("smm"), mm_after[dc - 2])
                dst = xsq[:, (dc % 2) * TT:(dc % 2 + 1) * TT]
                ii = v.tensor_tensor(dst, x_sb[:, dc * TT:(dc + 1) * TT],
                                     x_sb[:, dc * TT:(dc + 1) * TT], OP.mult)
                sq_done.append(self.inc("sdv", ii))

            @self.T
            def _(t):
                lhs = self.combo[:, dc * M_STAT:(dc + 1) * M_STAT]
                for ts in range(2):
                    t.matmul(ps_sum[0:M_STAT, ts * 512:(ts + 1) * 512], lhs,
                             x_sb[:, dc * TT + ts * 512: dc * TT + (ts + 1) * 512],
                             start=(dc == 0), stop=(dc == NDC - 1))
                t.wait_ge(self.sem("sdv"), sq_done[dc])
                ii = None
                for ts in range(2):
                    ii = t.matmul(ps_sq[0:M_STAT, ts * 512:(ts + 1) * 512], lhs,
                                  xsq[:, (dc % 2) * TT + ts * 512:(dc % 2) * TT + (ts + 1) * 512],
                                  start=(dc == 0), stop=(dc == NDC - 1))
                mm_after.append(self.inc("smm", ii))
        stats_mm = self.val("smm")
        Rxsq.rd("smm", stats_mm)

        @self.V
        def _(v):
            v.wait_ge(self.sem("smm"), stats_mm)
            v.tensor_scalar_mul(rows["m"], ps_sum[32:33, :], 1.0 / D)
            v.tensor_scalar_mul(rows["v"], ps_sq[32:33, :], 1.0 / D)
            v.tensor_tensor(rows["a"], rows["m"], rows["m"], OP.mult)
            self.inc("sdv", v.tensor_sub(rows["v"], rows["v"], rows["a"]))
        vd = self.val("sdv")

        @self.A
        def _(a):
            a.wait_ge(self.sem("sdv"), vd)
            a.activation(out=rows["a"], in_=rows["v"], func=AF.Ln, bias=1e-5)
            self.inc("sac", a.activation(out=rows["a"], in_=rows["a"], func=AF.Exp, scale=-0.5))
        ad = self.val("sac")

        @self.V
        def _(v):
            v.wait_ge(self.sem("sac"), ad)
            self.inc("sdv", v.scalar_tensor_tensor(rows["b"], rows["m"], -1.0, rows["a"],
                                                   OP.mult, OP.mult))
        return self.val("sdv"), stats_mm

    def bcast2(self, row_a, row_b, Pa, Ra, Pb, Rb, out_a, out_b, rdy_sem, rdy_val):
        @self.T
        def _(t):
            t.wait_ge(self.sem(rdy_sem), rdy_val)
            Ra.wait_w(self, t)
            Rb.wait_w(self, t)
            ii = None
            for P0, row in ((Pa, row_a), (Pb, row_b)):
                for ts in range(2):
                    ii = t.matmul(P0[0:128, ts * 512:(ts + 1) * 512],
                                  self.onesf[:, :], row[0:1, ts * 512:(ts + 1) * 512],
                                  start=True, stop=True)
            self.inc("smm", ii)
        mmv = self.val("smm")

        @self.A
        def _(a):
            a.wait_ge(self.sem("smm"), mmv)
            a.activation(out=out_a, in_=Pa[:, 0:TT], func=AF.Copy)
            self.inc("sac", a.activation(out=out_b, in_=Pb[:, 0:TT], func=AF.Copy))
        Ra.rd("sac", self.val("sac"))
        Rb.rd("sac", self.val("sac"))
        return self.val("sac")

    def ffn_call(self, call_id):
        nc = self.nc
        self.phase_barrier()
        src = self.xT if call_id == 1 else self.x_res
        with ExitStack() as cctx:
            sb = lambda name, shape, dt: cctx.enter_context(
                nc.sbuf_tensor(f"{name}_{call_id}", shape, dt))
            bf = {}
            bf["x_sb"] = sb("x", [128, NDC * TT], F32)
            bf["xn_sb"] = sb("xn", [128, NDC * TT], BF16)
            bf["act_sb"] = sb("act", [128, NIC * TT], BF16)
            bf["w1b"] = sb("w1b", [128, 2 * NDC * 128], BF16)
            bf["w3b"] = sb("w3b", [128, 2 * NDC * 128], BF16)
            bf["w2b"] = sb("w2b", [128, 2 * NHC * 128], BF16)
            bf["gb_sb"] = sb("gb", [128, NE * TT], BF16)
            bf["ab_sb"] = sb("ab", [128, TT], F32)
            bf["bb_sb"] = sb("bb", [128, TT], F32)
            bf["xsq"] = sb("xsq", [128, 2 * TT], F32)
            rowst = sb("rows", [1, 4 * TT], F32)
            bf["ws1"] = sb("ws1", [128, TT], F32)
            bf["ws2"] = sb("ws2", [128, TT], F32)
            bf["g8"] = sb("g8", [NE, TT], BF16)
            bf["rows"] = {n: rowst[0:1, i * TT:(i + 1) * TT]
                          for i, n in enumerate(("m", "v", "a", "b"))}
            bf["growf"] = rowst[0:1, 0:2 * TT]

            @self.V
            def _(v):
                v.memset(bf["ws1"][:, :], 0.0)
                v.memset(bf["ws2"][:, :], 0.0)
                v.memset(bf["gb_sb"][:, :], 0.0)
                self.inc("sdv", v.memset(bf["g8"][:, :], 0.0))

            with ExitStack() as px:
                P = [px.enter_context(nc.psum_tensor(f"P{i}_{call_id}", [128, TT], F32))
                     for i in range(4)]
                bf["P"] = P
                bf["R"] = dict(P=[Res() for _ in range(4)],
                               w1b=[Res(), Res()], w3b=[Res(), Res()], w2b=[Res(), Res()],
                               x=Res(), xn=Res(), act=Res(), ws=Res(), xsq=Res(),
                               g8=Res(), gdr=Res())
                for tt in range(self.lim.get("ntt", NTT)):
                    self._ffn_tile(call_id, tt, src, bf)
            self.phase_snap = self.snap()

    def _ffn_tile(self, call_id, tt, src, bf):
        t0 = tt * TT
        P, R = bf["P"], bf["R"]
        x_sb, xn_sb = bf["x_sb"], bf["xn_sb"]
        xsq, rows = bf["xsq"], bf["rows"]

        @self.S
        def _(sy):
            self.qdrain(sy, 'sx')
            if tt == 0:
                self.wait_snap(sy, self.phase_snap)
            R["x"].wait_w(self, sy)
            for dc in range(NDC):
                self.inc("sx", sy.dma_start(out=x_sb[:, dc * TT:(dc + 1) * TT],
                                            in_=src[dc * 128:(dc + 1) * 128, t0:t0 + TT]), 16)
        x_ld = self.val("sx")

        @self.V
        def _(v):
            v.wait_ge(self.sem("sx"), x_ld)
            R["xsq"].wait_w(self, v)

        @self.T
        def _(t):
            t.wait_ge(self.sem("sx"), x_ld)
            R["P"][2].wait_w(self, t)
            R["P"][3].wait_w(self, t)

        ab_dv, stats_mm = self.ln_stats(x_sb, xsq, P[2], P[3], rows, R["xsq"])

        ab_ready = self.bcast2(rows["a"], rows["b"], P[0], R["P"][0], P[1], R["P"][1],
                               bf["ab_sb"][:, :], bf["bb_sb"][:, :], "sdv", ab_dv)

        if self.lim.get("router", True):
            self._router(tt, bf, ab_ready)
        else:
            self.gb_ready = self.val("sac")
            self.grow_read_mm = self.val("smm")

        @self.V
        def _(v):
            v.wait_ge(self.sem("sac"), ab_ready)
            R["xn"].wait_w(self, v)
            R["xsq"].wait_w(self, v)
            ii = None
            for dc in range(NDC):
                v.tensor_tensor(xsq[:, 0:TT], x_sb[:, dc * TT:(dc + 1) * TT],
                                bf["ab_sb"][:, :], OP.mult)
                ii = v.tensor_tensor(xn_sb[:, dc * TT:(dc + 1) * TT], xsq[:, 0:TT],
                                     bf["bb_sb"][:, :], OP.add)
            self.inc("sdv", ii)
        self.xn_ready = self.val("sdv")

        self.x_acc_dv = self.xn_ready
        for e in range(self.lim.get("ne", NE)):
            self._expert(e, bf)

        for half in range(2 if self.lim.get("shared", True) else 0):
            self._shared(half, bf)

        if self.lim.get("epi", True):
            self._tile_epilogue(call_id, tt, bf, t0)
        else:
            @self.S
            def _(sy):
                self.qdrain(sy, 'sx')
                sy.wait_ge(self.sem("sdv"), self.x_acc_dv)
                for dc in range(NDC):
                    self.inc("sx", sy.dma_start(
                        out=self.x_res[dc * 128:(dc + 1) * 128, t0:t0 + TT],
                        in_=bf["x_sb"][:, dc * TT:(dc + 1) * TT]), 16)
                if "dbg_xres" in self.dbg_t:
                    for dc in range(NDC):
                        self.inc("sx", sy.dma_start(
                            out=self.dbg_t["dbg_xres"][dc * 128:(dc + 1) * 128, t0:t0 + TT],
                            in_=bf["x_sb"][:, dc * TT:(dc + 1) * TT]), 16)
            bf["R"]["x"].rd("sx", self.val("sx"))

    def _router(self, tt, bf, ab_ready):
        P, R = bf["P"], bf["R"]
        ws1, ws2, g8, xsq = bf["ws1"], bf["ws2"], bf["g8"], bf["xsq"]
        gsb = bass.AP(self.gsum, 0, [[1, NE], [0, TT]])
        rbb = bass.AP(self.rb8, 0, [[1, NE], [0, TT]])

        @self.V
        def _(v):
            v.wait_ge(self.sem("sac"), ab_ready)
            R["ws"].wait_w(self, v)
            v.tensor_tensor(ws1[0:NE, :], P[2][0:NE, :], bf["ab_sb"][0:NE, :], OP.mult)
            v.tensor_tensor(ws2[0:NE, :], bf["bb_sb"][0:NE, :], gsb, OP.mult)
            self.inc("sdv", v.tensor_add(ws1[0:NE, :], ws1[0:NE, :], ws2[0:NE, :]))
        R["P"][2].rd("sdv", self.val("sdv"))
        logit_dv = self.val("sdv")

        @self.A
        def _(a):
            a.wait_ge(self.sem("sdv"), logit_dv)
            self.inc("sac", a.activation(out=ws2[0:NE, :], in_=ws1[0:NE, :], func=AF.Exp, scale=-1.0))
        ed = self.val("sac")

        @self.V
        def _(v):
            v.wait_ge(self.sem("sac"), ed)
            v.tensor_scalar_add(ws2[0:NE, :], ws2[0:NE, :], 1.0)
            v.reciprocal(ws2[0:NE, :], ws2[0:NE, :])
            v.tensor_tensor(ws1[32:32 + NE, :], ws2[0:NE, :], rbb, OP.add)
            self.inc("sdv", v.tensor_copy(ws1[0:NE, :], ws2[0:NE, :]))
        sel_dv = self.val("sdv")
        if self.lim.get("rstage", 9) <= 1:
            self.gb_ready = self.val("sac"); self.grow_read_mm = self.val("smm"); return

        rows = bf["rows"]
        sco = ws1[0:NE, :]
        xscr = xsq
        if self.lim.get("rstage", 9) <= 2:
            self.gb_ready = self.val("sac"); self.grow_read_mm = self.val("smm"); return

        def ptree_max(first_wait_dv, src_base, out_row):
            @self.S
            def _(sy):
                self.qdrain(sy, 'sx')
                sy.wait_ge(self.sem("sdv"), first_wait_dv)
                self.inc("sx", sy.dma_start(out=xscr[0:4, 0:TT], in_=ws1[src_base + 4:src_base + 8, :]), 16)
                self.inc("sx", sy.dma_start(out=ws2[0:4, :], in_=ws1[src_base:src_base + 4, :]), 16)
            d1 = self.val("sx")

            @self.V
            def _(v):
                v.wait_ge(self.sem("sx"), d1)
                self.inc("sdv", v.tensor_max(ws2[0:4, :], ws2[0:4, :], xscr[0:4, 0:TT]))
            v1 = self.val("sdv")

            @self.S
            def _(sy):
                self.qdrain(sy, 'sx')
                sy.wait_ge(self.sem("sdv"), v1)
                self.inc("sx", sy.dma_start(out=xscr[0:2, 0:TT], in_=ws2[2:4, :]), 16)
            d2 = self.val("sx")

            @self.V
            def _(v):
                v.wait_ge(self.sem("sx"), d2)
                self.inc("sdv", v.tensor_max(ws2[0:2, :], ws2[0:2, :], xscr[0:2, 0:TT]))
            v2 = self.val("sdv")

            @self.S
            def _(sy):
                self.qdrain(sy, 'sx')
                sy.wait_ge(self.sem("sdv"), v2)
                self.inc("sx", sy.dma_start(out=xscr[0:1, 0:TT], in_=ws2[1:2, :]), 16)
            d3 = self.val("sx")

            @self.V
            def _(v):
                v.wait_ge(self.sem("sx"), d3)
                self.inc("sdv", v.tensor_max(out_row, ws2[0:1, :], xscr[0:1, 0:TT]))
            return self.val("sdv")

        def bcast8(row_ap, rdy_sem, rdy, out8):
            @self.T
            def _(t):
                t.wait_ge(self.sem(rdy_sem), rdy)
                R["P"][1].wait_w(self, t)
                ii = None
                for ts in range(2):
                    ii = t.matmul(P[1][0:NE, ts * 512:(ts + 1) * 512], self.ones8f[0:1, :],
                                  row_ap[0:1, ts * 512:(ts + 1) * 512], start=True, stop=True)
                self.inc("smm", ii)
            mv = self.val("smm")

            @self.A
            def _(a):
                a.wait_ge(self.sem("smm"), mv)
                self.inc("sac", a.activation(out=out8, in_=P[1][0:NE, :], func=AF.Copy))
            ev = self.val("sac")
            R["P"][1].rd("sac", ev)
            return ev

        def psum8(in8_ap, rdy, out_row):
            @self.T
            def _(t):
                t.wait_ge(self.sem("sdv"), rdy)
                R["P"][1].wait_w(self, t)
                ii = None
                for ts in range(2):
                    ii = t.matmul(P[1][0:1, ts * 512:(ts + 1) * 512], self.ones8c[:, 0:1],
                                  in8_ap[:, ts * 512:(ts + 1) * 512], start=True, stop=True)
                self.inc("smm", ii)
            mv = self.val("smm")

            @self.A
            def _(a):
                a.wait_ge(self.sem("smm"), mv)
                self.inc("sac", a.activation(out=out_row, in_=P[1][0:1, :], func=AF.Copy))
            ev = self.val("sac")
            R["P"][1].rd("sac", ev)
            return ev

        m1_dv = ptree_max(sel_dv, 32, rows["b"])
        m1b_ev = bcast8(rows["b"], "sdv", m1_dv, ws2[32:32 + NE, :])

        @self.V
        def _(v):
            v.wait_ge(self.sem("sac"), m1b_ev)
            v.tensor_sub(ws1[64:64 + NE, :], ws1[32:32 + NE, :], ws2[32:32 + NE, :])
            v.tensor_single_scalar(ws1[64:64 + NE, :], ws1[64:64 + NE, :], 0.0, OP.is_ge)
            v.tensor_copy(ws2[0:NE, :], ws1[64:64 + NE, :])
            v.tensor_copy(xscr[32:32 + NE, 0:TT], ws1[64:64 + NE, :])
            self.inc("sdv", v.tensor_mul(xscr[0:NE, 0:TT], ws2[0:NE, :], sco))
        mask1_dv = self.val("sdv")
        s1_ev = psum8(xscr[0:NE, 0:TT], mask1_dv, rows["m"])

        @self.V
        def _(v):
            v.wait_ge(self.sem("sac"), s1_ev)
            self.inc("sdv", v.scalar_tensor_tensor(ws1[32:32 + NE, :], xscr[32:32 + NE, 0:TT],
                                                   -1e9, ws1[32:32 + NE, :], OP.mult, OP.add))
        sel2_dv = self.val("sdv")
        m2_dv = ptree_max(sel2_dv, 32, rows["b"])
        m2b_ev = bcast8(rows["b"], "sdv", m2_dv, ws2[32:32 + NE, :])

        @self.V
        def _(v):
            v.wait_ge(self.sem("sac"), m2b_ev)
            v.tensor_sub(ws1[32:32 + NE, :], ws1[32:32 + NE, :], ws2[32:32 + NE, :])
            v.tensor_single_scalar(ws1[32:32 + NE, :], ws1[32:32 + NE, :], 0.0, OP.is_ge)
            v.tensor_copy(xscr[32:32 + NE, 0:TT], sco)
            self.inc("sdv", v.tensor_mul(xscr[0:NE, 0:TT], ws1[32:32 + NE, :],
                                         xscr[32:32 + NE, 0:TT]))
        mask2_dv = self.val("sdv")
        s2_ev = psum8(xscr[0:NE, 0:TT], mask2_dv, rows["v"])

        @self.V
        def _(v):
            v.wait_ge(self.sem("sac"), s2_ev)
            v.tensor_add(rows["a"], rows["m"], rows["v"])
            v.tensor_scalar_max(rows["a"], rows["a"], 1e-12)
            v.reciprocal(rows["a"], rows["a"])
            v.tensor_scalar_mul(rows["a"], rows["a"], 0.5)
            v.tensor_mul(rows["b"], rows["m"], rows["a"])
            self.inc("sdv", v.tensor_mul(rows["a"], rows["v"], rows["a"]))
        w_dv = self.val("sdv")
        w1b_ev = bcast8(rows["b"], "sdv", w_dv, ws2[64:64 + NE, :])

        @self.V
        def _(v):
            v.wait_ge(self.sem("sac"), w1b_ev)
            self.inc("sdv", v.tensor_mul(ws2[0:NE, :], ws1[64:64 + NE, :],
                                         ws2[64:64 + NE, :]))
        g1_dv = self.val("sdv")
        w2b_ev = bcast8(rows["a"], "sdv", g1_dv, ws2[32:32 + NE, :])

        @self.V
        def _(v):
            v.wait_ge(self.sem("sac"), w2b_ev)
            R["g8"].wait_w(self, v)
            v.tensor_mul(xscr[0:NE, 0:TT], ws1[32:32 + NE, :], ws2[32:32 + NE, :])
            self.inc("sdv", v.tensor_add(g8[0:NE, :], ws2[0:NE, :], xscr[0:NE, 0:TT]))
        g8_dv = self.val("sdv")
        R["ws"].rd("sdv", g8_dv)
        R["xsq"].rd("sdv", g8_dv)

        @self.S
        def _(sy):
            self.qdrain(sy, 'sx')
            sy.wait_ge(self.sem("sdv"), g8_dv)
            R["gdr"].wait_w(self, sy)
            self.inc("sx", sy.dma_start(out=self.g_dr, in_=g8[:, :]), 16)
        grow_ld = self.val("sx")
        R["g8"].rd("sx", grow_ld)
        if self.lim.get("rstage", 9) <= 5:
            self.gb_ready = self.val("sac"); self.grow_read_mm = self.val("smm"); return

        grow_cp = {}
        growf = bf["growf"]
        for e in range(NE):
            Pi = e % 2
            rowf = growf[0:1, (e % 2) * TT:(e % 2 + 1) * TT]

            @self.G
            def _(g):
                self.qdrain(g, 'sw')
                if e == 0:
                    g.wait_ge(self.sem("sx"), grow_ld)
                if e >= 2:
                    g.wait_ge(self.sem("smm"), grow_cp[e - 2])
                self.inc("sw", g.dma_start(out=rowf, in_=self.g_dr[e:e + 1, :]), 16)
            cpv = self.val("sw")

            @self.T
            def _(t):
                t.wait_ge(self.sem("sw"), cpv)
                R["P"][Pi].wait_w(self, t)
                ii = None
                for ts in range(2):
                    ii = t.matmul(P[Pi][0:128, ts * 512:(ts + 1) * 512], self.onesf[:, :],
                                  rowf[0:1, ts * 512:(ts + 1) * 512],
                                  start=True, stop=True)
                self.inc("smm", ii)
            grow_cp[e] = self.val("smm")
            mmv = self.val("smm")

            @self.A
            def _(a):
                a.wait_ge(self.sem("smm"), mmv)
                self.inc("sac", a.activation(out=bf["gb_sb"][:, e * TT:(e + 1) * TT],
                                             in_=P[Pi][:, 0:TT], func=AF.Copy))
            R["P"][Pi].rd("sac", self.val("sac"))
        R["gdr"].rd("sw", self.val("sw"))
        self.gb_ready = self.val("sac")
        self.grow_read_mm = self.val("smm")
        if self.lim.get("dump") and not hasattr(self, "_dumped"):
            self._dumped = True
            dmp_g8 = self.nc.dram_tensor("dmp_g8", [NE, TT], BF16, kind="ExternalOutput").ap()
            dmp_ws1 = self.nc.dram_tensor("dmp_ws1", [128, TT], F32, kind="ExternalOutput").ap()
            dmp_ws2 = self.nc.dram_tensor("dmp_ws2", [128, TT], F32, kind="ExternalOutput").ap()
            dmp_gb = self.nc.dram_tensor("dmp_gb", [128, 2 * TT], BF16, kind="ExternalOutput").ap()
            dmp_gf = self.nc.dram_tensor("dmp_gf", [1, 2 * TT], F32, kind="ExternalOutput").ap()
            dmp_wsm = self.nc.dram_tensor("dmp_wsm", [128, 16], F32, kind="ExternalOutput").ap()
            gbv = self.gb_ready

            @self.S
            def _(sy):
                self.qdrain(sy, 'sx')
                sy.wait_ge(self.sem("sac"), gbv)
                sy.wait_ge(self.sem("sw"), self.val("sw"))
                self.inc("sx", sy.dma_start(out=dmp_g8, in_=g8[:, :]), 16)
                self.inc("sx", sy.dma_start(out=dmp_ws1, in_=ws1[:, :]), 16)
                self.inc("sx", sy.dma_start(out=dmp_ws2, in_=ws2[:, :]), 16)
                self.inc("sx", sy.dma_start(out=dmp_gb, in_=bf["gb_sb"][:, 0:2 * TT]), 16)
                self.inc("sx", sy.dma_start(out=dmp_gf, in_=growf[0:1, :]), 16)
                self.inc("sx", sy.dma_start(out=dmp_wsm, in_=wsm[:, :]), 16)

    def _expert(self, e, bf):
        P, R = bf["P"], bf["R"]
        act_sb, xn_sb, x_sb = bf["act_sb"], bf["xn_sb"], bf["x_sb"]
        w1b, w3b, w2b = bf["w1b"], bf["w3b"], bf["w2b"]

        w_ready = {}

        def load_w13(ic):
            par = ic % 2
            @self.G
            def _(g):
                self.qdrain(g, 'sw')
                R["w1b"][par].wait_w(self, g)
                R["w3b"][par].wait_w(self, g)
                self.inc("sw", g.dma_start(out=w1b[:, par * 2048:(par + 1) * 2048],
                                           in_=self.w1_p[e, ic]), 16)
                self.inc("sw", g.dma_start(out=w3b[:, par * 2048:(par + 1) * 2048],
                                           in_=self.w3_p[e, ic]), 16)
            w_ready[ic] = self.val("sw")

        load_w13(0)
        load_w13(1)

        act_evac = {}
        for ic in range(NIC):
            par = ic % 2
            @self.T
            def _(t):
                t.wait_ge(self.sem("sw"), w_ready[ic])
                if ic == 0:
                    t.wait_ge(self.sem("sdv"), self.xn_ready)
                R["P"][par].wait_w(self, t)
                R["P"][2 + par].wait_w(self, t)
                for dc in range(NDC):
                    lhs1 = w1b[:, par * 2048 + dc * 128: par * 2048 + (dc + 1) * 128]
                    for ts in range(2):
                        t.matmul(P[par][:, ts * 512:(ts + 1) * 512], lhs1,
                                 xn_sb[:, dc * TT + ts * 512: dc * TT + (ts + 1) * 512],
                                 start=(dc == 0), stop=(dc == NDC - 1))
                ii = None
                for dc in range(NDC):
                    lhs3 = w3b[:, par * 2048 + dc * 128: par * 2048 + (dc + 1) * 128]
                    for ts in range(2):
                        ii = t.matmul(P[2 + par][:, ts * 512:(ts + 1) * 512], lhs3,
                                      xn_sb[:, dc * TT + ts * 512: dc * TT + (ts + 1) * 512],
                                      start=(dc == 0), stop=(dc == NDC - 1))
                self.inc("smm", ii)
            h_mm = self.val("smm")
            R["w1b"][par].rd("smm", h_mm)
            R["w3b"][par].rd("smm", h_mm)

            if ic + 2 < NIC:
                load_w13(ic + 2)

            @self.A
            def _(a):
                a.wait_ge(self.sem("smm"), h_mm)
                if ic == 0:
                    a.wait_ge(self.sem("sac"), self.gb_ready)
                R["act"].wait_w(self, a)
                self.inc("sac", a.activation(out=act_sb[:, ic * TT:(ic + 1) * TT],
                                             in_=P[par][:, :], func=AF.Silu))
            sl = self.val("sac")
            R["P"][par].rd("sac", sl)

            @self.V
            def _(v):
                v.wait_ge(self.sem("sac"), sl)
                v.tensor_tensor(act_sb[:, ic * TT:(ic + 1) * TT],
                                act_sb[:, ic * TT:(ic + 1) * TT], P[2 + par][:, :], OP.mult)
                self.inc("sdv", v.tensor_tensor(act_sb[:, ic * TT:(ic + 1) * TT],
                                                act_sb[:, ic * TT:(ic + 1) * TT],
                                                bf["gb_sb"][:, e * TT:(e + 1) * TT], OP.mult))
            act_evac[ic] = self.val("sdv")
            R["P"][2 + par].rd("sdv", act_evac[ic])

        w2_ready = {}

        def load_w2(do):
            par = do % 2
            @self.G
            def _(g):
                self.qdrain(g, 'sw')
                R["w2b"][par].wait_w(self, g)
                self.inc("sw", g.dma_start(
                    out=w2b[:, par * NHC * 128: par * NHC * 128 + NIC * 128],
                    in_=self.w2_p[e, do]), 16)
            w2_ready[do] = self.val("sw")

        load_w2(0)
        load_w2(1)

        for do in range(NDC):
            par = do % 2
            @self.T
            def _(t):
                t.wait_ge(self.sem("sw"), w2_ready[do])
                if do == 0:
                    t.wait_ge(self.sem("sdv"), act_evac[NIC - 1])
                R["P"][par].wait_w(self, t)
                ii = None
                for ic in range(NIC):
                    lhs = w2b[:, par * NHC * 128 + ic * 128: par * NHC * 128 + (ic + 1) * 128]
                    for ts in range(2):
                        ii = t.matmul(P[par][:, ts * 512:(ts + 1) * 512], lhs,
                                      act_sb[:, ic * TT + ts * 512: ic * TT + (ts + 1) * 512],
                                      start=(ic == 0), stop=(ic == NIC - 1))
                self.inc("smm", ii)
            y_mm = self.val("smm")
            R["w2b"][par].rd("smm", y_mm)
            if do == NDC - 1:
                R["act"].rd("smm", y_mm)

            if do + 2 < NDC:
                load_w2(do + 2)

            @self.V
            def _(v):
                v.wait_ge(self.sem("smm"), y_mm)
                self.inc("sdv", v.tensor_tensor(x_sb[:, do * TT:(do + 1) * TT],
                                                x_sb[:, do * TT:(do + 1) * TT],
                                                P[par][:, :], OP.add))
            R["P"][par].rd("sdv", self.val("sdv"))
        self.x_acc_dv = self.val("sdv")

    def _shared(self, half, bf):
        P, R = bf["P"], bf["R"]
        act_sb, xn_sb, x_sb = bf["act_sb"], bf["xn_sb"], bf["x_sb"]
        w1b, w3b, w2b = bf["w1b"], bf["w3b"], bf["w2b"]
        hofs = half * 512

        w_ready = {}

        def load_sw13(hc):
            par = hc % 2
            @self.G
            def _(g):
                self.qdrain(g, 'sw')
                R["w1b"][par].wait_w(self, g)
                R["w3b"][par].wait_w(self, g)
                self.inc("sw", g.dma_start(out=w1b[:, par * 2048:(par + 1) * 2048],
                                           in_=self.sw1_p[hc]), 16)
                self.inc("sw", g.dma_start(out=w3b[:, par * 2048:(par + 1) * 2048],
                                           in_=self.sw3_p[hc]), 16)
            w_ready[hc] = self.val("sw")

        load_sw13(0)
        load_sw13(1)

        act_evac = {}
        for hc in range(NHC):
            par = hc % 2
            @self.T
            def _(t):
                t.wait_ge(self.sem("sw"), w_ready[hc])
                R["P"][par].wait_w(self, t)
                R["P"][2 + par].wait_w(self, t)
                for dc in range(NDC):
                    t.matmul(P[par][:, 0:512],
                             w1b[:, par * 2048 + dc * 128: par * 2048 + (dc + 1) * 128],
                             xn_sb[:, dc * TT + hofs: dc * TT + hofs + 512],
                             start=(dc == 0), stop=(dc == NDC - 1))
                ii = None
                for dc in range(NDC):
                    ii = t.matmul(P[2 + par][:, 0:512],
                                  w3b[:, par * 2048 + dc * 128: par * 2048 + (dc + 1) * 128],
                                  xn_sb[:, dc * TT + hofs: dc * TT + hofs + 512],
                                  start=(dc == 0), stop=(dc == NDC - 1))
                self.inc("smm", ii)
            h_mm = self.val("smm")
            R["w1b"][par].rd("smm", h_mm)
            R["w3b"][par].rd("smm", h_mm)

            if hc + 2 < NHC:
                load_sw13(hc + 2)

            @self.A
            def _(a):
                a.wait_ge(self.sem("smm"), h_mm)
                R["act"].wait_w(self, a)
                self.inc("sac", a.activation(out=act_sb[:, hc * 512:(hc + 1) * 512],
                                             in_=P[par][:, 0:512], func=AF.Silu))
            sl = self.val("sac")
            R["P"][par].rd("sac", sl)

            @self.V
            def _(v):
                v.wait_ge(self.sem("sac"), sl)
                self.inc("sdv", v.scalar_tensor_tensor(
                    act_sb[:, hc * 512:(hc + 1) * 512],
                    act_sb[:, hc * 512:(hc + 1) * 512], 0.5,
                    P[2 + par][:, 0:512], OP.mult, OP.mult))
            act_evac[hc] = self.val("sdv")
            R["P"][2 + par].rd("sdv", act_evac[hc])

        w2_ready = {}

        def load_sw2(do):
            par = do % 2
            @self.G
            def _(g):
                self.qdrain(g, 'sw')
                R["w2b"][par].wait_w(self, g)
                self.inc("sw", g.dma_start(out=w2b[:, par * NHC * 128:(par + 1) * NHC * 128],
                                           in_=self.sw2_p[do]), 16)
            w2_ready[do] = self.val("sw")

        load_sw2(0)
        load_sw2(1)

        for do in range(NDC):
            par = do % 2
            @self.T
            def _(t):
                t.wait_ge(self.sem("sw"), w2_ready[do])
                if do == 0:
                    t.wait_ge(self.sem("sdv"), act_evac[NHC - 1])
                R["P"][par].wait_w(self, t)
                ii = None
                for hc in range(NHC):
                    ii = t.matmul(P[par][:, 0:512],
                                  w2b[:, par * NHC * 128 + hc * 128: par * NHC * 128 + (hc + 1) * 128],
                                  act_sb[:, hc * 512:(hc + 1) * 512],
                                  start=(hc == 0), stop=(hc == NHC - 1))
                self.inc("smm", ii)
            y_mm = self.val("smm")
            R["w2b"][par].rd("smm", y_mm)
            if do == NDC - 1:
                R["act"].rd("smm", y_mm)

            if do + 2 < NDC:
                load_sw2(do + 2)

            @self.V
            def _(v):
                v.wait_ge(self.sem("smm"), y_mm)
                self.inc("sdv", v.tensor_tensor(
                    x_sb[:, do * TT + hofs: do * TT + hofs + 512],
                    x_sb[:, do * TT + hofs: do * TT + hofs + 512],
                    P[par][:, 0:512], OP.add))
            R["P"][par].rd("sdv", self.val("sdv"))
        self.x_acc_dv = self.val("sdv")

    def _tile_epilogue(self, call_id, tt, bf, t0):
        P, R = bf["P"], bf["R"]
        x_sb, xn_sb, xsq, rows = bf["x_sb"], bf["xn_sb"], bf["xsq"], bf["rows"]
        x_done = self.x_acc_dv

        if call_id == 1:
            @self.V
            def _(v):
                v.wait_ge(self.sem("sdv"), x_done)
                v.wait_ge(self.sem("smm"), self.grow_read_mm)
                R["xsq"].wait_w(self, v)

            @self.T
            def _(t):
                t.wait_ge(self.sem("sdv"), x_done)
                R["P"][2].wait_w(self, t)
                R["P"][3].wait_w(self, t)
            ab_dv, _sm = self.ln_stats(x_sb, xsq, P[2], P[3], rows, R["xsq"])
            R["P"][2].rd("sdv", ab_dv)
            R["P"][3].rd("sdv", ab_dv)
            ab2 = self.bcast2(rows["a"], rows["b"], P[0], R["P"][0], P[1], R["P"][1],
                              bf["ab_sb"][:, :], bf["bb_sb"][:, :], "sdv", ab_dv)

            @self.V
            def _(v):
                v.wait_ge(self.sem("sac"), ab2)
                R["xn"].wait_w(self, v)
                R["xsq"].wait_w(self, v)
                ii = None
                for dc in range(NDC):
                    v.tensor_tensor(xsq[:, 0:TT], x_sb[:, dc * TT:(dc + 1) * TT],
                                    bf["ab_sb"][:, :], OP.mult)
                    ii = v.tensor_tensor(xn_sb[:, dc * TT:(dc + 1) * TT], xsq[:, 0:TT],
                                         bf["bb_sb"][:, :], OP.add)
                self.inc("sdv", ii)
            xn2 = self.val("sdv")

            @self.S
            def _(sy):
                self.qdrain(sy, 'sx')
                sy.wait_ge(self.sem("sdv"), xn2)
                for dc in range(NDC):
                    self.inc("sx", sy.dma_start(
                        out=self.xnA[dc * 128:(dc + 1) * 128, t0:t0 + TT],
                        in_=xn_sb[:, dc * TT:(dc + 1) * TT]), 16)
                    self.inc("sx", sy.dma_start(
                        out=self.x_res[dc * 128:(dc + 1) * 128, t0:t0 + TT],
                        in_=x_sb[:, dc * TT:(dc + 1) * TT]), 16)
                for nm, sbuf in (("dbg_xres", x_sb), ("dbg_xnA", xn_sb)):
                    if nm in self.dbg_t:
                        for dc in range(NDC):
                            self.inc("sx", sy.dma_start(
                                out=self.dbg_t[nm][dc * 128:(dc + 1) * 128, t0:t0 + TT],
                                in_=sbuf[:, dc * TT:(dc + 1) * TT]), 16)
            st = self.val("sx")
            R["x"].rd("sx", st)
            R["xn"].rd("sx", st)
        else:
            @self.S
            def _(sy):
                self.qdrain(sy, 'sx')
                sy.wait_ge(self.sem("sdv"), x_done)
                for dc in range(NDC):
                    self.inc("sx", sy.dma_start(
                        out=self.outT[dc * 128:(dc + 1) * 128, t0:t0 + TT],
                        in_=x_sb[:, dc * TT:(dc + 1) * TT]), 16)
            R["x"].rd("sx", self.val("sx"))

    def qkv_phase(self):
        nc = self.nc
        self.phase_barrier()
        barrier = self.phase_snap
        with ExitStack() as cctx:
            sb = lambda name, shape, dt: cctx.enter_context(nc.sbuf_tensor(name, shape, dt))
            xn_sb = sb("xnq", [128, NDC * TLOC], BF16)
            wbuf = sb("wqk", [128, 4 * 2048], BF16)
            wvb = sb("wvb", [128, NDC * 2048], BF16)
            cqb = sb("cqb", [128, D], F32)
            ckb = sb("ckb", [128, D], F32)
            bvb = sb("bvb", [128, D], F32)
            rws = sb("rws", [1, 3 * D], F32)
            stg = sb("qstg", [128, 2 * TLOC], BF16)

            with ExitStack() as px:
                PP = [px.enter_context(nc.psum_tensor(f"QP{i}", [128, TLOC], F32))
                      for i in range(2)]
                Rp = [Res(), Res()]
                Rw = [Res() for _ in range(4)]
                Rstg = [Res(), Res()]

                @self.S
                def _(sy):
                    self.qdrain(sy, 'sx')
                    self.wait_snap(sy, barrier)
                    sy.wait_ge(self.sem("sx"), self.rows_stored)
                    for dc in range(NDC):
                        self.inc("sx", sy.dma_start(out=xn_sb[:, dc * TLOC:(dc + 1) * TLOC],
                                                    in_=self.xnA[dc * 128:(dc + 1) * 128, :]), 16)
                    self.inc("sx", sy.dma_start(out=rws[0:1, 0:2 * D],
                                                in_=self.row_dr[0:2, :].rearrange("a d -> (a d)").rearrange("(o n) -> o n", o=1)), 16)
                    self.inc("sx", sy.dma_start(out=rws[0:1, 2 * D:3 * D], in_=self.bv_r), 16)
                    for bi in range(2):
                        self.inc("sx", sy.dma_start(
                            out=self.bias_t[:, bi * NDC:(bi + 1) * NDC],
                            in_=bass.AP(self.row_dr.tensor, (2 + bi) * D, [[1, 128], [128, NDC]]),
                            allow_slow_non_contiguous=True), 16)
                xn_ld = self.val("sx")

                @self.T
                def _(t):
                    self.wait_snap(t, barrier)
                    t.wait_ge(self.sem("sx"), xn_ld)
                    ii = None
                    for half in range(2):
                        for ts in range(D // 512):
                            ii = t.matmul(PP[half][:, ts * 512:(ts + 1) * 512],
                                          self.onesf[:, :],
                                          rws[0:1, half * D + ts * 512: half * D + (ts + 1) * 512],
                                          start=True, stop=True)
                    self.inc("smm", ii)
                cmm = self.val("smm")

                @self.A
                def _(a):
                    a.wait_ge(self.sem("smm"), cmm)
                    a.activation(out=cqb[:, :], in_=PP[0][:, 0:D], func=AF.Copy)
                    self.inc("sac", a.activation(out=ckb[:, :], in_=PP[1][:, 0:D], func=AF.Copy))
                Rp[0].rd("sac", self.val("sac"))
                Rp[1].rd("sac", self.val("sac"))

                @self.T
                def _(t):
                    Rp[0].wait_w(self, t)
                    ii = None
                    for ts in range(D // 512):
                        ii = t.matmul(PP[0][:, ts * 512:(ts + 1) * 512], self.onesf[:, :],
                                      rws[0:1, 2 * D + ts * 512: 2 * D + (ts + 1) * 512],
                                      start=True, stop=True)
                    self.inc("smm", ii)
                bvm = self.val("smm")

                @self.A
                def _(a):
                    a.wait_ge(self.sem("smm"), bvm)
                    self.inc("sac", a.activation(out=bvb[:, :], in_=PP[0][:, 0:D], func=AF.Copy))
                Rp[0].rd("sac", self.val("sac"))
                cb_done = self.val("sac")

                @self.G
                def _(g):
                    self.qdrain(g, 'sw')
                    for dc in range(NDC):
                        self.inc("sw", g.dma_start(out=wvb[:, dc * 2048:(dc + 1) * 2048],
                                                   in_=self.wv_p[dc]), 16)
                wv_ld = self.val("sw")

                w_ready = {}

                def load_w(pi, jc):
                    slot = pi * 2 + jc % 2
                    @self.G
                    def _(g):
                        self.qdrain(g, 'sw')
                        Rw[slot].wait_w(self, g)
                        src = self.wq_p if pi == 0 else self.wk_p
                        self.inc("sw", g.dma_start(out=wbuf[:, slot * 2048:(slot + 1) * 2048],
                                                   in_=src[jc]), 16)
                    w_ready[(pi, jc)] = (self.val("sw"), slot)

                for pi in range(2):
                    load_w(pi, 0)
                    load_w(pi, 1)

                for pi in range(2):
                    scale = cqb if pi == 0 else ckb
                    bcol = pi
                    dst = self.q_dr if pi == 0 else self.kc_in.ap()
                    dbgn = "dbg_q" if pi == 0 else "dbg_k"
                    for jc in range(NDC):
                        wv, slot = w_ready[(pi, jc)]

                        @self.V
                        def _(v):
                            v.wait_ge(self.sem("sw"), wv)
                            v.wait_ge(self.sem("sac"), cb_done)
                            ii = None
                            for dc in range(NDC):
                                ii = v.tensor_tensor(
                                    wbuf[:, slot * 2048 + dc * 128: slot * 2048 + (dc + 1) * 128],
                                    wbuf[:, slot * 2048 + dc * 128: slot * 2048 + (dc + 1) * 128],
                                    scale[:, jc * 128:(jc + 1) * 128], OP.mult)
                            self.inc("sdv", ii)
                        wsc = self.val("sdv")

                        spar = jc % 2
                        @self.T
                        def _(t):
                            t.wait_ge(self.sem("sdv"), wsc)
                            if pi == 0 and jc == 0:
                                t.wait_ge(self.sem("sx"), xn_ld)
                            Rp[spar].wait_w(self, t)
                            ii = None
                            for dc in range(NDC):
                                lhs = wbuf[:, slot * 2048 + dc * 128: slot * 2048 + (dc + 1) * 128]
                                for ts in range(TLOC // 512):
                                    ii = t.matmul(PP[spar][:, ts * 512:(ts + 1) * 512], lhs,
                                                  xn_sb[:, dc * TLOC + ts * 512: dc * TLOC + (ts + 1) * 512],
                                                  start=(dc == 0), stop=(dc == NDC - 1))
                            self.inc("smm", ii)
                        pm = self.val("smm")
                        Rw[slot].rd("smm", pm)

                        if jc + 2 < NDC:
                            load_w(pi, jc + 2)

                        @self.A
                        def _(a):
                            a.wait_ge(self.sem("smm"), pm)
                            Rstg[spar].wait_w(self, a)
                            self.inc("sac", a.activation(
                                out=stg[:, spar * TLOC:(spar + 1) * TLOC],
                                in_=PP[spar][:, :], func=AF.Identity,
                                bias=self.bias_t[:, bcol * NDC + jc: bcol * NDC + jc + 1]))
                        ev = self.val("sac")
                        Rp[spar].rd("sac", ev)

                        @self.S
                        def _(sy):
                            self.qdrain(sy, 'sx')
                            sy.wait_ge(self.sem("sac"), ev)
                            self.inc("sx", sy.dma_start(
                                out=dst[jc * 128:(jc + 1) * 128, :],
                                in_=stg[:, spar * TLOC:(spar + 1) * TLOC]), 16)
                            if dbgn in self.dbg_t:
                                self.inc("sx", sy.dma_start(
                                    out=self.dbg_t[dbgn][jc * 128:(jc + 1) * 128, :],
                                    in_=stg[:, spar * TLOC:(spar + 1) * TLOC]), 16)
                        Rstg[spar].rd("sx", self.val("sx"))

                if self.lim.get("wdump"):
                    wdo = nc.dram_tensor("wdump", [128, NDC * 2048], BF16, kind="ExternalOutput").ap()
                    cdo = nc.dram_tensor("cdump", [128, 2 * D], F32, kind="ExternalOutput").ap()
                    @self.S
                    def _(sy):
                        self.qdrain(sy, 'sx')
                        sy.wait_ge(self.sem("sw"), wv_ld)
                        sy.wait_ge(self.sem("sac"), cb_done)
                        self.inc("sx", sy.dma_start(out=wdo, in_=wvb[:, :]), 16)
                        self.inc("sx", sy.dma_start(out=cdo[:, 0:D], in_=cqb[:, :]), 16)
                        self.inc("sx", sy.dma_start(out=cdo[:, D:2 * D], in_=ckb[:, :]), 16)

                vdst = self.vc_in.ap()
                for tc in range(NDC):
                    spar = tc % 2
                    @self.T
                    def _(t):
                        if tc < 2:
                            t.wait_ge(self.sem("sw"), wv_ld)
                        Rp[spar].wait_w(self, t)
                        ii = None
                        for dc in range(NDC):
                            lhs = xn_sb[:, dc * TLOC + tc * 128: dc * TLOC + (tc + 1) * 128]
                            for js in range(D // 512):
                                ii = t.matmul(PP[spar][:, js * 512:(js + 1) * 512], lhs,
                                              wvb[:, dc * 2048 + js * 512: dc * 2048 + (js + 1) * 512],
                                              start=(dc == 0), stop=(dc == NDC - 1))
                        self.inc("smm", ii)
                    pm = self.val("smm")

                    @self.V
                    def _(v):
                        v.wait_ge(self.sem("smm"), pm)
                        Rstg[spar].wait_w(self, v)
                        self.inc("sdv", v.tensor_add(
                            stg[:, spar * TLOC:(spar + 1) * TLOC],
                            PP[spar][:, :], bvb[:, :]))
                    ev = self.val("sdv")
                    Rp[spar].rd("sdv", ev)

                    @self.S
                    def _(sy):
                        self.qdrain(sy, 'sx')
                        sy.wait_ge(self.sem("sdv"), ev)
                        self.inc("sx", sy.dma_start(
                            out=vdst[tc * 128:(tc + 1) * 128, :],
                            in_=stg[:, spar * TLOC:(spar + 1) * TLOC]), 16)
                        if "dbg_v" in self.dbg_t:
                            self.inc("sx", sy.dma_start(
                                out=self.dbg_t["dbg_v"][tc * 128:(tc + 1) * 128, :],
                                in_=stg[:, spar * TLOC:(spar + 1) * TLOC]), 16)
                    Rstg[spar].rd("sx", self.val("sx"))
            self.phase_snap = self.snap()

    def gather_phase(self):
        kv_stored = self.val("sx")

        @self.G
        def _(g):
            self.qdrain(g, 'sw')
            g.wait_ge(self.sem("sx"), kv_stored)
            self.inc("scc", g.collective_compute(
                "AllGather", OP.bypass,
                replica_groups=[[0, 1, 2, 3], [4, 5, 6, 7]],
                ins=[self.kc_in.ap().opt()], outs=[self.kg.ap().opt()]))
            self.inc("scc", g.collective_compute(
                "AllGather", OP.bypass,
                replica_groups=[[0, 1, 2, 3], [4, 5, 6, 7]],
                ins=[self.vc_in.ap().opt()], outs=[self.vg.ap().opt()]))
        self.phase_snap = self.snap()

    def attn_phase(self):
        nc = self.nc
        self.phase_barrier()
        barrier = self.phase_snap
        kgap, vgap = self.kg.ap(), self.vg.ap()
        with ExitStack() as cctx:
            sb = lambda name, shape, dt: cctx.enter_context(nc.sbuf_tensor(name, shape, dt))
            k_sb = sb("k_sb", [128, 2 * SEQ], BF16)
            vt_sb = sb("vt_sb", [128, 2 * SEQ], BF16)
            o_sb = sb("o_sb", [128, NDC * TLOC], BF16)
            q_sb = sb("q_sb", [128, 4 * 512], BF16)
            pt_sb = sb("pt_sb", [128, 2 * 1024], BF16)
            lrow = sb("lrow", [1, 1024], F32)
            lb_sb = sb("lb_sb", [128, 1024], F32)
            wo_b = sb("wo_b", [128, 2 * 2048], BF16)
            ost = sb("ost", [128, 2 * 512], F32)

            with ExitStack() as px:
                S2 = [px.enter_context(nc.psum_tensor(f"S{i}", [128, 1024], F32))
                      for i in range(2)]
                O2 = [px.enter_context(nc.psum_tensor(f"O{i}", [128, 512], F32))
                      for i in range(2)]
                L2 = [px.enter_context(nc.psum_tensor(f"L{i}", [1, 512], F32))
                      for i in range(2)]
                RS = [Res(), Res()]
                RO = [Res(), Res()]
                RL = [Res(), Res()]
                Rk = [Res(), Res()]
                Rvt = [Res(), Res()]
                Rq = [Res() for _ in range(4)]
                Rpt = [Res(), Res()]
                Rlb = Res()
                Ro = Res()

                kv_ready = {}

                def load_kv(h):
                    par = h % 2
                    @self.S
                    def _(sy):
                        self.qdrain(sy, 'sx')
                        if h == 0:
                            self.wait_snap(sy, barrier)
                            sy.wait_ge(self.sem("scc"), self.val("scc"))
                        Rk[par].wait_w(self, sy)
                        Rvt[par].wait_w(self, sy)
                        for r in range(GRP):
                            self.inc("sx", sy.dma_start(
                                out=k_sb[:, par * SEQ + r * TLOC: par * SEQ + (r + 1) * TLOC],
                                in_=kgap[r * D + h * 128: r * D + (h + 1) * 128, :]), 16)
                        self.inc("sx", sy.dma_start(
                            out=vt_sb[:, par * SEQ: (par + 1) * SEQ]
                            .rearrange("p (kc j) -> p kc j", kc=NKC),
                            in_=vgap[:, h * 128:(h + 1) * 128]
                            .rearrange("(kc p) j -> p kc j", p=128)), 16)
                    kv_ready[h] = self.val("sx")

                load_kv(0)

                q_ready = {}

                def load_q(h, qt):
                    slot = qt % 4
                    @self.S
                    def _(sy):
                        self.qdrain(sy, 'sx')
                        Rq[slot].wait_w(self, sy)
                        self.inc("sx", sy.dma_start(
                            out=q_sb[:, slot * 512:(slot + 1) * 512],
                            in_=self.q_dr[h * 128:(h + 1) * 128, qt * 512:(qt + 1) * 512]), 16)
                    q_ready[(h, qt)] = self.val("sx")

                load_q(0, 0)
                load_q(0, 1)

                NH = self.lim.get("nheads", HEADS)
                for h in range(NH):
                    par = h % 2
                    if h + 1 < NH:
                        load_kv(h + 1)
                    for qp in range(2):
                        qt0, qt1 = 2 * qp, 2 * qp + 1
                        s0, s1 = qt0 % 4, qt1 % 4
                        if not (h == 0 and qp == 0):
                            load_q(h, qt0)
                            load_q(h, qt1)
                        NKCL = self.lim.get("nkc", NKC)
                        for kc in range(NKCL):
                            sp = kc % 2
                            @self.T
                            def _(t):
                                if kc == 0:
                                    t.wait_ge(self.sem("sx"), q_ready[(h, qt0)])
                                    t.wait_ge(self.sem("sx"), q_ready[(h, qt1)])
                                    t.wait_ge(self.sem("sx"), kv_ready[h])
                                    RO[0].wait_w(self, t)
                                    RO[1].wait_w(self, t)
                                    RL[0].wait_w(self, t)
                                    RL[1].wait_w(self, t)
                                RS[sp].wait_w(self, t)
                                lhsk = k_sb[:, par * SEQ + kc * 128: par * SEQ + (kc + 1) * 128]
                                t.matmul(S2[sp][:, 0:512], lhsk, q_sb[:, s0 * 512:(s0 + 1) * 512],
                                         start=True, stop=True)
                                ii = t.matmul(S2[sp][:, 512:1024], lhsk,
                                              q_sb[:, s1 * 512:(s1 + 1) * 512],
                                              start=True, stop=True)
                                self.inc("smm", ii)
                            smv = self.val("smm")

                            @self.A
                            def _(a):
                                a.wait_ge(self.sem("smm"), smv)
                                Rpt[kc % 2].wait_w(self, a)
                                self.inc("sac", a.activation(
                                    out=pt_sb[:, (kc % 2) * 1024:(kc % 2 + 1) * 1024],
                                    in_=S2[sp][:, :], func=AF.Exp))
                            ev = self.val("sac")
                            RS[sp].rd("sac", ev)

                            @self.T
                            def _(t):
                                t.wait_ge(self.sem("sac"), ev)
                                po = (kc % 2) * 1024
                                lhsv = vt_sb[:, par * SEQ + kc * 128: par * SEQ + (kc + 1) * 128]
                                t.matmul(O2[0][:, :], lhsv, pt_sb[:, po: po + 512],
                                         start=(kc == 0), stop=(kc == NKCL - 1))
                                t.matmul(O2[1][:, :], lhsv, pt_sb[:, po + 512: po + 1024],
                                         start=(kc == 0), stop=(kc == NKCL - 1))
                                t.matmul(L2[0][0:1, :], self.ones_bf[:, :], pt_sb[:, po: po + 512],
                                         start=(kc == 0), stop=(kc == NKCL - 1))
                                ii = t.matmul(L2[1][0:1, :], self.ones_bf[:, :],
                                              pt_sb[:, po + 512: po + 1024],
                                              start=(kc == 0), stop=(kc == NKCL - 1))
                                self.inc("smm", ii)
                            av = self.val("smm")
                            Rpt[kc % 2].rd("smm", av)
                            Rq[s0].rd("smm", av)
                            Rq[s1].rd("smm", av)
                            Rk[par].rd("smm", av)
                            Rvt[par].rd("smm", av)
                        av_all = self.val("smm")

                        @self.V
                        def _(v):
                            v.wait_ge(self.sem("smm"), av_all)
                            v.tensor_copy(lrow[0:1, 0:512], L2[0][0:1, :])
                            v.tensor_copy(lrow[0:1, 512:1024], L2[1][0:1, :])
                            self.inc("sdv", v.reciprocal(lrow[0:1, :], lrow[0:1, :]))
                        lr = self.val("sdv")
                        RL[0].rd("sdv", lr)
                        RL[1].rd("sdv", lr)

                        @self.T
                        def _(t):
                            t.wait_ge(self.sem("sdv"), lr)
                            RS[0].wait_w(self, t)
                            ii = None
                            for ts in range(2):
                                ii = t.matmul(S2[0][:, ts * 512:(ts + 1) * 512], self.onesf[:, :],
                                              lrow[0:1, ts * 512:(ts + 1) * 512],
                                              start=True, stop=True)
                            self.inc("smm", ii)
                        lbm = self.val("smm")

                        @self.A
                        def _(a):
                            a.wait_ge(self.sem("smm"), lbm)
                            Rlb.wait_w(self, a)
                            self.inc("sac", a.activation(out=lb_sb[:, :], in_=S2[0][:, :],
                                                         func=AF.Copy))
                        lbe = self.val("sac")
                        RS[0].rd("sac", lbe)

                        @self.V
                        def _(v):
                            v.wait_ge(self.sem("sac"), lbe)
                            if self.lim.get("nol"):
                                v.tensor_copy(
                                    o_sb[:, h * TLOC + qt0 * 512: h * TLOC + (qt0 + 1) * 512],
                                    O2[0][:, :])
                                self.inc("sdv", v.tensor_copy(
                                    o_sb[:, h * TLOC + qt1 * 512: h * TLOC + (qt1 + 1) * 512],
                                    O2[1][:, :]))
                            else:
                                v.tensor_tensor(
                                    o_sb[:, h * TLOC + qt0 * 512: h * TLOC + (qt0 + 1) * 512],
                                    O2[0][:, :], lb_sb[:, 0:512], OP.mult)
                                self.inc("sdv", v.tensor_tensor(
                                    o_sb[:, h * TLOC + qt1 * 512: h * TLOC + (qt1 + 1) * 512],
                                    O2[1][:, :], lb_sb[:, 512:1024], OP.mult))
                        oev = self.val("sdv")
                        RO[0].rd("sdv", oev)
                        RO[1].rd("sdv", oev)
                        Rlb.rd("sdv", oev)
                o_done = self.val("sdv")

                if self.lim.get("ptdump"):
                    pdo = nc.dram_tensor("ptdump", [128, 2 * 1024], BF16, kind="ExternalOutput").ap()
                    sdo = nc.dram_tensor("sdump", [128, 1024], F32, kind="ExternalOutput").ap()
                    @self.V
                    def _(v):
                        v.wait_ge(self.sem("sdv"), o_done)
                        self.inc("sdv", v.tensor_copy(lb_sb[:, :], S2[0][:, :]))
                    sde = self.val("sdv")
                    @self.S
                    def _(sy):
                        self.qdrain(sy, 'sx')
                        sy.wait_ge(self.sem("sdv"), sde)
                        self.inc("sx", sy.dma_start(out=pdo, in_=pt_sb[:, :]), 16)
                        self.inc("sx", sy.dma_start(out=sdo, in_=lb_sb[:, :]), 16)

                if self.lim.get("odump"):
                    odo = nc.dram_tensor("odump", [D, TLOC], BF16, kind="ExternalOutput").ap()
                    @self.S
                    def _(sy):
                        self.qdrain(sy, 'sx')
                        sy.wait_ge(self.sem("sdv"), o_done)
                        for jc in range(NDC):
                            self.inc("sx", sy.dma_start(
                                out=odo[jc * 128:(jc + 1) * 128, :],
                                in_=o_sb[:, jc * TLOC:(jc + 1) * TLOC]), 16)

                Rwo = [Res(), Res()]
                Rost = [Res(), Res()]
                wo_ready = {}

                def load_wo(do):
                    par = do % 2
                    @self.G
                    def _(g):
                        self.qdrain(g, 'sw')
                        Rwo[par].wait_w(self, g)
                        self.inc("sw", g.dma_start(out=wo_b[:, par * 2048:(par + 1) * 2048],
                                                   in_=self.wo_p[do]), 16)
                    wo_ready[do] = self.val("sw")

                load_wo(0)
                load_wo(1)

                for do in range(NDC):
                    par = do % 2
                    for qt in range(4):
                        opar = qt % 2
                        @self.T
                        def _(t):
                            t.wait_ge(self.sem("sw"), wo_ready[do])
                            if do == 0 and qt == 0:
                                t.wait_ge(self.sem("sdv"), o_done)
                            RO[opar].wait_w(self, t)
                            ii = None
                            for jc in range(NDC):
                                lhs = wo_b[:, par * 2048 + jc * 128: par * 2048 + (jc + 1) * 128]
                                ii = t.matmul(O2[opar][:, :], lhs,
                                              o_sb[:, jc * TLOC + qt * 512: jc * TLOC + (qt + 1) * 512],
                                              start=(jc == 0), stop=(jc == NDC - 1))
                            self.inc("smm", ii)
                        wmm = self.val("smm")
                        Ro.rd("smm", wmm)
                        Rwo[par].rd("smm", wmm)

                        sslot = (do * 4 + qt) % 2
                        @self.A
                        def _(a):
                            a.wait_ge(self.sem("smm"), wmm)
                            Rost[sslot].wait_w(self, a)
                            self.inc("sac", a.activation(
                                out=ost[:, sslot * 512:(sslot + 1) * 512],
                                in_=O2[opar][:, :], func=AF.Identity,
                                bias=self.bias_t[:, 3 * NDC + do: 3 * NDC + do + 1]))
                        wev = self.val("sac")
                        RO[opar].rd("sac", wev)

                        @self.G
                        def _(g):
                            self.qdrain(g, 'sw')
                            g.wait_ge(self.sem("sac"), wev)
                            self.inc("sw", g.dma_start(
                                out=self.x_res[do * 128:(do + 1) * 128, qt * 512:(qt + 1) * 512],
                                in_=ost[:, sslot * 512:(sslot + 1) * 512],
                                accum_op=OP.add), 16)
                        Rost[sslot].rd("sw", self.val("sw"))
                    if do + 2 < NDC:
                        load_wo(do + 2)
            self.phase_snap = self.snap()

    def final_wait(self):
        @self.S
        def _(sy):
            self.qdrain(sy, 'sx')
            sy.wait_ge(self.sem("sx"), self.val("sx"))
            sy.wait_ge(self.sem("sw"), self.val("sw"))


def _pack_inputs(inputs):
    f32 = np.float32
    i = inputs
    pk4 = lambda WT: np.ascontiguousarray(
        WT.reshape(NDC, 128, NDC, 128).transpose(2, 1, 0, 3).reshape(NDC, 128, NDC * 128))
    d = {}
    d["wq_p"] = pk4(np.asarray(i["L0_attn_Wq"], f32).T.copy())
    d["wk_p"] = pk4(np.asarray(i["L0_attn_Wk"], f32).T.copy())
    d["wv_p"] = np.ascontiguousarray(np.asarray(i["L0_attn_Wv"], f32).T.reshape(NDC, 128, D))
    d["wo_p"] = pk4(np.asarray(i["L0_attn_Wo"], f32).T.copy())
    bt = np.zeros((128, 4 * NDC), f32)
    for c, nm in enumerate(("L0_attn_bq", "L0_attn_bk", "L0_attn_bv", "L0_attn_bo")):
        bt[:, c * NDC:(c + 1) * NDC] = np.asarray(i[nm], f32).reshape(NDC, 128).T
    d["biases"] = bt
    d["bv_r"] = np.asarray(i["L0_attn_bv"], f32).reshape(1, D).copy()
    d["bq_r"] = np.asarray(i["L0_attn_bq"], f32).reshape(1, D).copy()
    d["bk_r"] = np.asarray(i["L0_attn_bk"], f32).reshape(1, D).copy()
    hl = np.asarray(i["L0_attn_helix_rotation"], f32)
    sp = np.asarray(i["L0_attn_spiral_weight"], f32)
    d["helix_t"] = np.ascontiguousarray(hl.reshape(NDC, 128).T)
    d["helix_r"] = hl.reshape(1, D).copy()
    d["spiral_t"] = np.ascontiguousarray(np.broadcast_to(sp[None, :], (128, NDC))).astype(f32)
    d["spiral_r"] = np.repeat(sp, 128).reshape(1, D).copy()
    d["gwT_p"] = np.ascontiguousarray(np.asarray(i["L0_ffn_gate_w"], f32).T.reshape(NDC, 128, NE))
    d["rb"] = np.asarray(i["L0_ffn_router_bias"], f32).reshape(NE, 1).copy()
    W1 = np.asarray(i["L0_ffn_W1"], f32)
    W3 = np.asarray(i["L0_ffn_W3"], f32)
    W2 = np.asarray(i["L0_ffn_W2"], f32)
    d["w1_p"] = np.ascontiguousarray(
        W1.reshape(NE, NDC, 128, NIC, 128).transpose(0, 3, 2, 1, 4).reshape(NE, NIC, 128, NDC * 128))
    d["w3_p"] = np.ascontiguousarray(
        W3.reshape(NE, NDC, 128, NIC, 128).transpose(0, 3, 2, 1, 4).reshape(NE, NIC, 128, NDC * 128))
    d["w2_p"] = np.ascontiguousarray(
        W2.reshape(NE, NIC, 128, NDC, 128).transpose(0, 3, 2, 1, 4).reshape(NE, NDC, 128, NIC * 128))
    sw1T = np.asarray(i["L0_ffn_sw1"], f32).T.copy()
    sw3T = np.asarray(i["L0_ffn_sw3"], f32).T.copy()
    sw2T = np.asarray(i["L0_ffn_sw2"], f32).T.copy()
    d["sw1_p"] = np.ascontiguousarray(
        sw1T.reshape(NDC, 128, NHC, 128).transpose(2, 1, 0, 3).reshape(NHC, 128, NDC * 128))
    d["sw3_p"] = np.ascontiguousarray(
        sw3T.reshape(NDC, 128, NHC, 128).transpose(2, 1, 0, 3).reshape(NHC, 128, NDC * 128))
    d["sw2_p"] = np.ascontiguousarray(
        sw2T.reshape(NHC, 128, NDC, 128).transpose(2, 1, 0, 3).reshape(NDC, 128, NHC * 128))
    d["identf"] = np.eye(128, dtype=np.float32)
    c = np.zeros((1, 8), np.float32); c[0, :len(CONSTS)] = CONSTS
    d["consts"] = c
    return d


_NC_CACHE = {}


def _get_nc(dbg=(), phases=("ffn1", "qkv", "gather", "attn", "ffn2"), lim=None):
    key = (tuple(sorted(dbg)), tuple(phases), tuple(sorted((lim or {}).items())))
    if key not in _NC_CACHE:
        _NC_CACHE[key] = Builder(bass.Bass("TRN2", target_bir_lowering=False,
                                           detect_race_conditions=False), dbg,
                                 phases, lim).build()
    return _NC_CACHE[key]


def kernel(**inputs):
    x = np.asarray(inputs["L0_attn_x"], np.float32)
    shared = _pack_inputs(inputs)
    nc = _get_nc()
    in_maps = []
    for c in range(8):
        seq, t0 = c // GRP, (c % GRP) * TLOC
        m = dict(shared)
        m["xT"] = np.ascontiguousarray(x[seq, t0:t0 + TLOC, :].T)
        in_maps.append(m)
    res = run_bass_kernel_spmd(nc, in_maps, core_ids=list(range(8)))
    out = np.empty((2, SEQ, D), np.float32)
    for c in range(8):
        seq, t0 = c // GRP, (c % GRP) * TLOC
        out[seq, t0:t0 + TLOC, :] = res.results[c]["outT"].T
    return out


# version 03
# speedup vs baseline: 1.0236x; 1.0236x over previous
import math
from contextlib import ExitStack

import numpy as np
import ml_dtypes
import concourse.bass as bass
import concourse.mybir as mybir
from concourse.bass_utils import run_bass_kernel_spmd

F32 = mybir.dt.float32
BF16 = mybir.dt.bfloat16
AF = mybir.ActivationFunctionType
OP = mybir.AluOpType
AX = mybir.AxisListType

D = 2048
NDC = 16
TLOC = 2048
TT = 1024
NTT = TLOC // TT
INTER = 1408
NIC = 11
NE = 8
NSH = 2816
NHC = 22
HEADS = 16
DH = 128
SEQ = 8192
GRP = 4
NKC = SEQ // 128
ISQ = 1.0 / math.sqrt(128.0)
M_STAT = 33
CONSTS = (0.5, 0.25, -0.5, -1.0, 1e-5)


class Res:
    __slots__ = ("g",)

    def __init__(self):
        self.g = {}

    def rd(self, sem, val):
        if self.g.get(sem, 0) < val:
            self.g[sem] = val

    def wait_w(self, b, eng):
        for s, v in self.g.items():
            eng.wait_ge(b.sem(s), v)
        self.g = {}


class Builder:
    def __init__(self, nc, dbg=(), phases=("ffn1", "qkv", "gather", "attn", "ffn2"),
                 lim=None):
        self.nc = nc
        self.dbg = set(dbg)
        self.phases = set(phases)
        self.lim = lim or {}

    def _mksem(self, name):
        self.sems[name] = [self.ctx.enter_context(self.nc.semaphore(name)), 0]

    def sem(self, n):
        return self.sems[n][0]

    def val(self, n):
        return self.sems[n][1]

    def inc(self, n, instr, k=1):
        s = self.sems[n]
        instr.then_inc(s[0], k)
        s[1] += k
        return s[1]

    def snap(self):
        return {n: v[1] for n, v in self.sems.items()}

    def wait_snap(self, eng, snapshot):
        for n, v in snapshot.items():
            if v > 0:
                eng.wait_ge(self.sem(n), v)

    def phase_barrier(self):
        snap = self.phase_snap
        for sec in (self.T, self.V, self.A, self.S, self.G):
            def _f(eng, snap=snap):
                self.wait_snap(eng, snap)
            sec(_f)

    def gq_begin(self, g):
        name = f"sw{self.swrot % 4}"
        self.swrot += 1
        if self.val(name) > 0:
            g.wait_ge(self.sem(name), self.val(name))
        return name

    def qdrain(self, eng, qsem):
        if self.val(qsem) > 0:
            eng.wait_ge(self.sem(qsem), self.val(qsem))

    def T(self, f): self.block.tensor(f)
    def V(self, f): self.block.vector(f)
    def A(self, f): self.block.scalar(f)
    def G(self, f): self.block.gpsimd(f)
    def S(self, f): self.block.sync(f)

    def build(self):
        nc = self.nc
        EI = dict(kind="ExternalInput")
        self.xT = nc.dram_tensor("xT", [D, TLOC], F32, **EI).ap()
        self.wq_p = nc.dram_tensor("wq_p", [NDC, 128, NDC * 128], F32, **EI).ap()
        self.wk_p = nc.dram_tensor("wk_p", [NDC, 128, NDC * 128], F32, **EI).ap()
        self.wv_p = nc.dram_tensor("wv_p", [NDC, 128, D], F32, **EI).ap()
        self.wo_p = nc.dram_tensor("wo_p", [NDC, 128, NDC * 128], F32, **EI).ap()
        self.biases = nc.dram_tensor("biases", [128, 4 * NDC], F32, **EI).ap()
        self.bv_r = nc.dram_tensor("bv_r", [1, D], F32, **EI).ap()
        self.bq_r = nc.dram_tensor("bq_r", [1, D], F32, **EI).ap()
        self.bk_r = nc.dram_tensor("bk_r", [1, D], F32, **EI).ap()
        self.helix_t = nc.dram_tensor("helix_t", [128, NDC], F32, **EI).ap()
        self.helix_r = nc.dram_tensor("helix_r", [1, D], F32, **EI).ap()
        self.spiral_t = nc.dram_tensor("spiral_t", [128, NDC], F32, **EI).ap()
        self.spiral_r = nc.dram_tensor("spiral_r", [1, D], F32, **EI).ap()
        self.gwT_p = nc.dram_tensor("gwT_p", [NDC, 128, NE], F32, **EI).ap()
        self.rb_d = nc.dram_tensor("rb", [NE, 1], F32, **EI).ap()
        self.w1_p = nc.dram_tensor("w1_p", [NE, NIC, 128, NDC * 128], F32, **EI).ap()
        self.w3_p = nc.dram_tensor("w3_p", [NE, NIC, 128, NDC * 128], F32, **EI).ap()
        self.w2_p = nc.dram_tensor("w2_p", [NE, NDC, 128, NIC * 128], F32, **EI).ap()
        self.sw1_p = nc.dram_tensor("sw1_p", [NHC, 128, NDC * 128], F32, **EI).ap()
        self.sw3_p = nc.dram_tensor("sw3_p", [NHC, 128, NDC * 128], F32, **EI).ap()
        self.sw2_p = nc.dram_tensor("sw2_p", [NDC, 128, NHC * 128], F32, **EI).ap()
        self.identf_d = nc.dram_tensor("identf", [128, 128], F32, **EI).ap()
        self.consts_d = nc.dram_tensor("consts", [1, 8], F32, **EI).ap()

        self.outT = nc.dram_tensor("outT", [D, TLOC], F32, kind="ExternalOutput").ap()

        self.x_res = nc.dram_tensor("x_res", [D, TLOC], F32).ap()
        self.xnA = nc.dram_tensor("xnA", [D, TLOC], BF16).ap()
        self.q_dr = nc.dram_tensor("q_dr", [D, TLOC], BF16).ap()
        self.g_dr = nc.dram_tensor("g_dr", [NE, TT], BF16).ap()
        self.row_dr = nc.dram_tensor("row_dr", [4, D], F32).ap()
        self.kc_in = nc.dram_tensor("kc_in", [D, TLOC], BF16)
        self.vc_in = nc.dram_tensor("vc_in", [TLOC, D], BF16)
        self.kg = nc.dram_tensor("kg", [GRP * D, TLOC], BF16)
        self.vg = nc.dram_tensor("vg", [GRP * TLOC, D], BF16)

        self.dbg_t = {}
        for name in self.dbg:
            shp = [TLOC, D] if name == "dbg_v" else [D, TLOC]
            dt = BF16 if name in ("dbg_xnA", "dbg_q", "dbg_k", "dbg_v") else F32
            self.dbg_t[name] = nc.dram_tensor(name, shp, dt, kind="ExternalOutput").ap()

        with ExitStack() as ctx:
            self.ctx = ctx
            self.block = ctx.enter_context(nc.Block())
            self.sems = {}
            for n in ("sw", "sx", "smm", "sac", "sdv", "scc",
                      "sw0", "sw1", "sw2", "sw3"):
                self._mksem(n)
            self.swrot = 0

            sb = lambda name, shape, dt: ctx.enter_context(nc.sbuf_tensor(name, shape, dt))
            self.combo = sb("combo", [128, NDC * M_STAT], F32)
            self.onesf = sb("onesf", [1, 128], F32)
            self.onesbr = sb("onesbr", [1, 128], BF16)
            self.ones_bf = sb("ones_bf", [128, 1], BF16)
            self.bias_t = sb("bias_t", [128, 4 * NDC], F32)
            self.rb8 = sb("rb8", [NE, 1], F32)
            self.gsum = sb("gsum", [NE, 1], F32)
            self.idf = sb("idf", [128, 128], F32)
            self.ones8f = sb("ones8f", [1, NE], F32)
            self.ones8c = sb("ones8c", [NE, 1], F32)
            self.consts = sb("consts_sb", [1, 8], F32)
            for ci, cv in enumerate(CONSTS):
                nc.const_aps.register(F32, cv, self.consts[0:1, ci:ci + 1])

            self.prologue()
            if "ffn1" in self.phases:
                self.ffn_call(1)
            if "qkv" in self.phases:
                self.qkv_phase()
            if "gather" in self.phases:
                self.gather_phase()
            if "attn" in self.phases:
                self.attn_phase()
            if "kdump" in self.phases:
                kdo = nc.dram_tensor("kdump", [GRP * D, TLOC], BF16, kind="ExternalOutput").ap()
                vdo = nc.dram_tensor("vdump", [GRP * TLOC, D], BF16, kind="ExternalOutput").ap()
                @self.S
                def _(sy):
                    self.qdrain(sy, 'sx')
                    sy.wait_ge(self.sem("scc"), self.val("scc"))
                    self.inc("sx", sy.dma_start(out=kdo, in_=self.kg.ap()), 16)
                    self.inc("sx", sy.dma_start(out=vdo, in_=self.vg.ap()), 16)
            if "xdump" in self.phases:
                @self.S
                def _(sy):
                    self.qdrain(sy, 'sx')
                    self.wait_snap(sy, self.phase_snap)
                    sy.wait_ge(self.sem("sw"), self.val("sw"))
                    self.inc("sx", sy.dma_start(out=self.outT, in_=self.x_res), 16)
            if "ffn2" in self.phases:
                self.ffn_call(2)
            self.final_wait()
        return nc

    def prologue(self):
        nc = self.nc
        with ExitStack() as pctx:
            trow = pctx.enter_context(nc.sbuf_tensor("trow", [1, 6 * D], F32))
            gps = pctx.enter_context(nc.psum_tensor("gsum_ps", [NE, 8], F32))

            @self.S
            def _(sy):
                self.qdrain(sy, 'sx')
                for dst, src in ((self.rb8[:, :], self.rb_d),
                                 (self.bias_t[:, :], self.biases),
                                 (self.idf[:, :], self.identf_d),
                                 (self.consts[:, :], self.consts_d),
                                 (trow[0:1, 0:D], self.helix_r),
                                 (trow[0:1, 4 * D:5 * D], self.bq_r),
                                 (trow[0:1, 5 * D:6 * D], self.bk_r)):
                    self.inc("sx", sy.dma_start(out=dst, in_=src), 16)
            ld0 = self.val("sx")

            @self.G
            def _(g):
                self.qdrain(g, 'sw')
                for dc in range(NDC):
                    self.inc("sw", g.dma_start(out=self.combo[:, dc * M_STAT: dc * M_STAT + NE],
                                               in_=self.gwT_p[dc]), 16)
                self.inc("sw", g.dma_start(out=trow[0:1, D:2 * D], in_=self.spiral_r), 16)
            ldw0 = self.val("sw")

            @self.V
            def _(v):
                v.memset(self.onesf[:, :], 1.0)
                v.memset(self.onesbr[:, :], 1.0)
                v.memset(self.ones_bf[:, :], 1.0)
                v.memset(self.ones8f[:, :], 1.0)
                v.memset(self.ones8c[:, :], 1.0)
                for dc in range(NDC):
                    v.memset(self.combo[:, dc * M_STAT + NE: dc * M_STAT + 32], 0.0)
                ii = None
                for dc in range(NDC):
                    ii = v.memset(self.combo[:, dc * M_STAT + 32: dc * M_STAT + 33], 1.0)
                self.inc("sdv", ii)
            mz = self.val("sdv")

            @self.A
            def _(a):
                a.wait_ge(self.sem("sx"), ld0)
                a.wait_ge(self.sem("sw"), ldw0)
                a.activation(out=trow[0:1, 2 * D:3 * D], in_=trow[0:1, 0:D], func=AF.Sin, scale=0.5)
                ii = a.activation(out=trow[0:1, 3 * D:4 * D], in_=trow[0:1, 0:D], func=AF.Sin, scale=0.25)
                self.inc("sac", ii)
            sin_done = self.val("sac")

            @self.V
            def _(v):
                v.wait_ge(self.sem("sac"), sin_done)
                v.wait_ge(self.sem("sdv"), mz)
                r_cq = trow[0:1, 0:D]
                r_sw = trow[0:1, D:2 * D]
                r_s2 = trow[0:1, 2 * D:3 * D]
                r_s4 = trow[0:1, 3 * D:4 * D]
                v.tensor_tensor(r_cq, r_s2, r_s2, OP.mult)
                v.tensor_scalar(r_cq, r_cq, -2.0, 1.0, OP.mult, OP.add)
                v.tensor_tensor(r_cq, r_cq, r_sw, OP.mult)
                v.tensor_scalar_mul(r_cq, r_cq, ISQ)
                v.tensor_tensor(r_sw, r_s4, r_s4, OP.mult)
                v.tensor_scalar(r_sw, r_sw, -2.0, 1.0, OP.mult, OP.add)
                v.tensor_tensor(r_s2, r_s2, r_sw, OP.mult)
                v.tensor_scalar_mul(trow[0:1, D:2 * D], r_s2, 2.0)
                v.tensor_tensor(trow[0:1, 2 * D:3 * D], trow[0:1, 4 * D:5 * D], r_cq, OP.mult)
                self.inc("sdv", v.tensor_tensor(trow[0:1, 3 * D:4 * D], trow[0:1, 5 * D:6 * D],
                                                trow[0:1, D:2 * D], OP.mult))
            trig_dv = self.val("sdv")

            @self.S
            def _(sy):
                self.qdrain(sy, 'sx')
                sy.wait_ge(self.sem("sdv"), trig_dv)
                self.inc("sx", sy.dma_start(out=self.row_dr.rearrange("a d -> (a d)").rearrange("(o n) -> o n", o=1),
                                            in_=trow[0:1, 0:4 * D]), 16)
            self.rows_stored = self.val("sx")

            @self.T
            def _(t):
                t.wait_ge(self.sem("sw"), ldw0)
                t.wait_ge(self.sem("sdv"), mz)
                ii = None
                for dc in range(NDC):
                    ii = t.matmul(gps[0:NE, 0:1], self.combo[:, dc * M_STAT: dc * M_STAT + NE],
                                  self.combo[:, dc * M_STAT + 32: dc * M_STAT + 33],
                                  start=(dc == 0), stop=(dc == NDC - 1))
                self.inc("smm", ii)
            gmm = self.val("smm")

            @self.V
            def _(v):
                v.wait_ge(self.sem("smm"), gmm)
                self.inc("sdv", v.tensor_copy(self.gsum[:, :], gps[0:NE, 0:1]))
            self.phase_snap = self.snap()

    def ln_stats(self, x_sb, xsq, ps_sum, ps_sq, rows, Rxsq):
        sq_done = []
        mm_after = []
        for dc in range(NDC):
            @self.V
            def _(v):
                if dc >= 2:
                    v.wait_ge(self.sem("smm"), mm_after[dc - 2])
                dst = xsq[:, (dc % 2) * TT:(dc % 2 + 1) * TT]
                ii = v.tensor_tensor(dst, x_sb[:, dc * TT:(dc + 1) * TT],
                                     x_sb[:, dc * TT:(dc + 1) * TT], OP.mult)
                sq_done.append(self.inc("sdv", ii))

            @self.T
            def _(t):
                lhs = self.combo[:, dc * M_STAT:(dc + 1) * M_STAT]
                for ts in range(2):
                    t.matmul(ps_sum[0:M_STAT, ts * 512:(ts + 1) * 512], lhs,
                             x_sb[:, dc * TT + ts * 512: dc * TT + (ts + 1) * 512],
                             start=(dc == 0), stop=(dc == NDC - 1))
                t.wait_ge(self.sem("sdv"), sq_done[dc])
                ii = None
                for ts in range(2):
                    ii = t.matmul(ps_sq[0:M_STAT, ts * 512:(ts + 1) * 512], lhs,
                                  xsq[:, (dc % 2) * TT + ts * 512:(dc % 2) * TT + (ts + 1) * 512],
                                  start=(dc == 0), stop=(dc == NDC - 1))
                mm_after.append(self.inc("smm", ii))
        stats_mm = self.val("smm")
        Rxsq.rd("smm", stats_mm)

        @self.V
        def _(v):
            v.wait_ge(self.sem("smm"), stats_mm)
            v.tensor_scalar_mul(rows["m"], ps_sum[32:33, :], 1.0 / D)
            v.tensor_scalar_mul(rows["v"], ps_sq[32:33, :], 1.0 / D)
            v.tensor_tensor(rows["a"], rows["m"], rows["m"], OP.mult)
            self.inc("sdv", v.tensor_sub(rows["v"], rows["v"], rows["a"]))
        vd = self.val("sdv")

        @self.A
        def _(a):
            a.wait_ge(self.sem("sdv"), vd)
            a.activation(out=rows["a"], in_=rows["v"], func=AF.Ln, bias=1e-5)
            self.inc("sac", a.activation(out=rows["a"], in_=rows["a"], func=AF.Exp, scale=-0.5))
        ad = self.val("sac")

        @self.V
        def _(v):
            v.wait_ge(self.sem("sac"), ad)
            self.inc("sdv", v.scalar_tensor_tensor(rows["b"], rows["m"], -1.0, rows["a"],
                                                   OP.mult, OP.mult))
        return self.val("sdv"), stats_mm

    def bcast2(self, row_a, row_b, Pa, Ra, Pb, Rb, out_a, out_b, rdy_sem, rdy_val):
        @self.T
        def _(t):
            t.wait_ge(self.sem(rdy_sem), rdy_val)
            Ra.wait_w(self, t)
            Rb.wait_w(self, t)
            ii = None
            for P0, row in ((Pa, row_a), (Pb, row_b)):
                for ts in range(2):
                    ii = t.matmul(P0[0:128, ts * 512:(ts + 1) * 512],
                                  self.onesf[:, :], row[0:1, ts * 512:(ts + 1) * 512],
                                  start=True, stop=True)
            self.inc("smm", ii)
        mmv = self.val("smm")

        @self.A
        def _(a):
            a.wait_ge(self.sem("smm"), mmv)
            a.activation(out=out_a, in_=Pa[:, 0:TT], func=AF.Copy)
            self.inc("sac", a.activation(out=out_b, in_=Pb[:, 0:TT], func=AF.Copy))
        Ra.rd("sac", self.val("sac"))
        Rb.rd("sac", self.val("sac"))
        return self.val("sac")

    def ffn_call(self, call_id):
        nc = self.nc
        self.phase_barrier()
        src = self.xT if call_id == 1 else self.x_res
        with ExitStack() as cctx:
            sb = lambda name, shape, dt: cctx.enter_context(
                nc.sbuf_tensor(f"{name}_{call_id}", shape, dt))
            bf = {}
            bf["x_sb"] = sb("x", [128, NDC * TT], F32)
            bf["xn_sb"] = sb("xn", [128, NDC * TT], BF16)
            bf["act_sb"] = sb("act", [128, NIC * TT], BF16)
            bf["w1b"] = sb("w1b", [128, 2 * NDC * 128], BF16)
            bf["w3b"] = sb("w3b", [128, 2 * NDC * 128], BF16)
            bf["w2b"] = sb("w2b", [128, 2 * NHC * 128], BF16)
            bf["gb_sb"] = sb("gb", [128, NE * TT], BF16)
            bf["ab_sb"] = sb("ab", [128, TT], F32)
            bf["bb_sb"] = sb("bb", [128, TT], F32)
            bf["xsq"] = sb("xsq", [128, 2 * TT], F32)
            rowst = sb("rows", [1, 4 * TT], F32)
            bf["ws1"] = sb("ws1", [128, TT], F32)
            bf["ws2"] = sb("ws2", [128, TT], F32)
            bf["g8"] = sb("g8", [NE, TT], BF16)
            bf["rows"] = {n: rowst[0:1, i * TT:(i + 1) * TT]
                          for i, n in enumerate(("m", "v", "a", "b"))}
            bf["growf"] = rowst[0:1, 0:2 * TT]

            @self.V
            def _(v):
                v.memset(bf["ws1"][:, :], 0.0)
                v.memset(bf["ws2"][:, :], 0.0)
                v.memset(bf["gb_sb"][:, :], 0.0)
                self.inc("sdv", v.memset(bf["g8"][:, :], 0.0))

            with ExitStack() as px:
                P = [px.enter_context(nc.psum_tensor(f"P{i}_{call_id}", [128, TT], F32))
                     for i in range(4)]
                bf["P"] = P
                bf["R"] = dict(P=[Res() for _ in range(4)],
                               w1b=[Res(), Res()], w3b=[Res(), Res()], w2b=[Res(), Res()],
                               x=Res(), xn=Res(), act=Res(), ws=Res(), xsq=Res(),
                               g8=Res(), gdr=Res())
                for tt in range(self.lim.get("ntt", NTT)):
                    self._ffn_tile(call_id, tt, src, bf)
            self.phase_snap = self.snap()

    def _ffn_tile(self, call_id, tt, src, bf):
        t0 = tt * TT
        P, R = bf["P"], bf["R"]
        x_sb, xn_sb = bf["x_sb"], bf["xn_sb"]
        xsq, rows = bf["xsq"], bf["rows"]

        @self.S
        def _(sy):
            self.qdrain(sy, 'sx')
            if tt == 0:
                self.wait_snap(sy, self.phase_snap)
            R["x"].wait_w(self, sy)
            for dc in range(NDC):
                self.inc("sx", sy.dma_start(out=x_sb[:, dc * TT:(dc + 1) * TT],
                                            in_=src[dc * 128:(dc + 1) * 128, t0:t0 + TT]), 16)
        x_ld = self.val("sx")

        @self.V
        def _(v):
            v.wait_ge(self.sem("sx"), x_ld)
            R["xsq"].wait_w(self, v)

        @self.T
        def _(t):
            t.wait_ge(self.sem("sx"), x_ld)
            R["P"][2].wait_w(self, t)
            R["P"][3].wait_w(self, t)

        ab_dv, stats_mm = self.ln_stats(x_sb, xsq, P[2], P[3], rows, R["xsq"])

        ab_ready = self.bcast2(rows["a"], rows["b"], P[0], R["P"][0], P[1], R["P"][1],
                               bf["ab_sb"][:, :], bf["bb_sb"][:, :], "sdv", ab_dv)

        if self.lim.get("router", True):
            self._router(tt, bf, ab_ready)
        else:
            self.gb_ready = self.val("sac")
            self.grow_read_mm = self.val("smm")

        @self.V
        def _(v):
            v.wait_ge(self.sem("sac"), ab_ready)
            R["xn"].wait_w(self, v)
            R["xsq"].wait_w(self, v)
            ii = None
            for dc in range(NDC):
                v.tensor_tensor(xsq[:, 0:TT], x_sb[:, dc * TT:(dc + 1) * TT],
                                bf["ab_sb"][:, :], OP.mult)
                ii = v.tensor_tensor(xn_sb[:, dc * TT:(dc + 1) * TT], xsq[:, 0:TT],
                                     bf["bb_sb"][:, :], OP.add)
            self.inc("sdv", ii)
        self.xn_ready = self.val("sdv")

        self.x_acc_dv = self.xn_ready
        for e in range(self.lim.get("ne", NE)):
            self._expert(e, bf)

        for half in range(2 if self.lim.get("shared", True) else 0):
            self._shared(half, bf)

        if self.lim.get("epi", True):
            self._tile_epilogue(call_id, tt, bf, t0)
        else:
            @self.S
            def _(sy):
                self.qdrain(sy, 'sx')
                sy.wait_ge(self.sem("sdv"), self.x_acc_dv)
                for dc in range(NDC):
                    self.inc("sx", sy.dma_start(
                        out=self.x_res[dc * 128:(dc + 1) * 128, t0:t0 + TT],
                        in_=bf["x_sb"][:, dc * TT:(dc + 1) * TT]), 16)
                if "dbg_xres" in self.dbg_t:
                    for dc in range(NDC):
                        self.inc("sx", sy.dma_start(
                            out=self.dbg_t["dbg_xres"][dc * 128:(dc + 1) * 128, t0:t0 + TT],
                            in_=bf["x_sb"][:, dc * TT:(dc + 1) * TT]), 16)
            bf["R"]["x"].rd("sx", self.val("sx"))

    def _router(self, tt, bf, ab_ready):
        P, R = bf["P"], bf["R"]
        ws1, ws2, g8, xsq = bf["ws1"], bf["ws2"], bf["g8"], bf["xsq"]
        gsb = bass.AP(self.gsum, 0, [[1, NE], [0, TT]])
        rbb = bass.AP(self.rb8, 0, [[1, NE], [0, TT]])

        @self.V
        def _(v):
            v.wait_ge(self.sem("sac"), ab_ready)
            R["ws"].wait_w(self, v)
            v.tensor_tensor(ws1[0:NE, :], P[2][0:NE, :], bf["ab_sb"][0:NE, :], OP.mult)
            v.tensor_tensor(ws2[0:NE, :], bf["bb_sb"][0:NE, :], gsb, OP.mult)
            self.inc("sdv", v.tensor_add(ws1[0:NE, :], ws1[0:NE, :], ws2[0:NE, :]))
        R["P"][2].rd("sdv", self.val("sdv"))
        logit_dv = self.val("sdv")

        @self.A
        def _(a):
            a.wait_ge(self.sem("sdv"), logit_dv)
            self.inc("sac", a.activation(out=ws2[0:NE, :], in_=ws1[0:NE, :], func=AF.Exp, scale=-1.0))
        ed = self.val("sac")

        @self.V
        def _(v):
            v.wait_ge(self.sem("sac"), ed)
            v.tensor_scalar_add(ws2[0:NE, :], ws2[0:NE, :], 1.0)
            v.reciprocal(ws2[0:NE, :], ws2[0:NE, :])
            v.tensor_tensor(ws1[32:32 + NE, :], ws2[0:NE, :], rbb, OP.add)
            self.inc("sdv", v.tensor_copy(ws1[0:NE, :], ws2[0:NE, :]))
        sel_dv = self.val("sdv")
        if self.lim.get("rstage", 9) <= 1:
            self.gb_ready = self.val("sac"); self.grow_read_mm = self.val("smm"); return

        rows = bf["rows"]
        sco = ws1[0:NE, :]
        xscr = xsq
        if self.lim.get("rstage", 9) <= 2:
            self.gb_ready = self.val("sac"); self.grow_read_mm = self.val("smm"); return

        def ptree_max(first_wait_dv, src_base, out_row):
            @self.S
            def _(sy):
                self.qdrain(sy, 'sx')
                sy.wait_ge(self.sem("sdv"), first_wait_dv)
                self.inc("sx", sy.dma_start(out=xscr[0:4, 0:TT], in_=ws1[src_base + 4:src_base + 8, :]), 16)
                self.inc("sx", sy.dma_start(out=ws2[0:4, :], in_=ws1[src_base:src_base + 4, :]), 16)
            d1 = self.val("sx")

            @self.V
            def _(v):
                v.wait_ge(self.sem("sx"), d1)
                self.inc("sdv", v.tensor_max(ws2[0:4, :], ws2[0:4, :], xscr[0:4, 0:TT]))
            v1 = self.val("sdv")

            @self.S
            def _(sy):
                self.qdrain(sy, 'sx')
                sy.wait_ge(self.sem("sdv"), v1)
                self.inc("sx", sy.dma_start(out=xscr[0:2, 0:TT], in_=ws2[2:4, :]), 16)
            d2 = self.val("sx")

            @self.V
            def _(v):
                v.wait_ge(self.sem("sx"), d2)
                self.inc("sdv", v.tensor_max(ws2[0:2, :], ws2[0:2, :], xscr[0:2, 0:TT]))
            v2 = self.val("sdv")

            @self.S
            def _(sy):
                self.qdrain(sy, 'sx')
                sy.wait_ge(self.sem("sdv"), v2)
                self.inc("sx", sy.dma_start(out=xscr[0:1, 0:TT], in_=ws2[1:2, :]), 16)
            d3 = self.val("sx")

            @self.V
            def _(v):
                v.wait_ge(self.sem("sx"), d3)
                self.inc("sdv", v.tensor_max(out_row, ws2[0:1, :], xscr[0:1, 0:TT]))
            return self.val("sdv")

        def bcast8(row_ap, rdy_sem, rdy, out8):
            @self.T
            def _(t):
                t.wait_ge(self.sem(rdy_sem), rdy)
                R["P"][1].wait_w(self, t)
                ii = None
                for ts in range(2):
                    ii = t.matmul(P[1][0:NE, ts * 512:(ts + 1) * 512], self.ones8f[0:1, :],
                                  row_ap[0:1, ts * 512:(ts + 1) * 512], start=True, stop=True)
                self.inc("smm", ii)
            mv = self.val("smm")

            @self.A
            def _(a):
                a.wait_ge(self.sem("smm"), mv)
                self.inc("sac", a.activation(out=out8, in_=P[1][0:NE, :], func=AF.Copy))
            ev = self.val("sac")
            R["P"][1].rd("sac", ev)
            return ev

        def psum8(in8_ap, rdy, out_row):
            @self.T
            def _(t):
                t.wait_ge(self.sem("sdv"), rdy)
                R["P"][1].wait_w(self, t)
                ii = None
                for ts in range(2):
                    ii = t.matmul(P[1][0:1, ts * 512:(ts + 1) * 512], self.ones8c[:, 0:1],
                                  in8_ap[:, ts * 512:(ts + 1) * 512], start=True, stop=True)
                self.inc("smm", ii)
            mv = self.val("smm")

            @self.A
            def _(a):
                a.wait_ge(self.sem("smm"), mv)
                self.inc("sac", a.activation(out=out_row, in_=P[1][0:1, :], func=AF.Copy))
            ev = self.val("sac")
            R["P"][1].rd("sac", ev)
            return ev

        m1_dv = ptree_max(sel_dv, 32, rows["b"])
        m1b_ev = bcast8(rows["b"], "sdv", m1_dv, ws2[32:32 + NE, :])

        @self.V
        def _(v):
            v.wait_ge(self.sem("sac"), m1b_ev)
            v.tensor_sub(ws1[64:64 + NE, :], ws1[32:32 + NE, :], ws2[32:32 + NE, :])
            v.tensor_single_scalar(ws1[64:64 + NE, :], ws1[64:64 + NE, :], 0.0, OP.is_ge)
            v.tensor_copy(ws2[0:NE, :], ws1[64:64 + NE, :])
            v.tensor_copy(xscr[32:32 + NE, 0:TT], ws1[64:64 + NE, :])
            self.inc("sdv", v.tensor_mul(xscr[0:NE, 0:TT], ws2[0:NE, :], sco))
        mask1_dv = self.val("sdv")
        s1_ev = psum8(xscr[0:NE, 0:TT], mask1_dv, rows["m"])

        @self.V
        def _(v):
            v.wait_ge(self.sem("sac"), s1_ev)
            self.inc("sdv", v.scalar_tensor_tensor(ws1[32:32 + NE, :], xscr[32:32 + NE, 0:TT],
                                                   -1e9, ws1[32:32 + NE, :], OP.mult, OP.add))
        sel2_dv = self.val("sdv")
        m2_dv = ptree_max(sel2_dv, 32, rows["b"])
        m2b_ev = bcast8(rows["b"], "sdv", m2_dv, ws2[32:32 + NE, :])

        @self.V
        def _(v):
            v.wait_ge(self.sem("sac"), m2b_ev)
            v.tensor_sub(ws1[32:32 + NE, :], ws1[32:32 + NE, :], ws2[32:32 + NE, :])
            v.tensor_single_scalar(ws1[32:32 + NE, :], ws1[32:32 + NE, :], 0.0, OP.is_ge)
            v.tensor_copy(xscr[32:32 + NE, 0:TT], sco)
            self.inc("sdv", v.tensor_mul(xscr[0:NE, 0:TT], ws1[32:32 + NE, :],
                                         xscr[32:32 + NE, 0:TT]))
        mask2_dv = self.val("sdv")
        s2_ev = psum8(xscr[0:NE, 0:TT], mask2_dv, rows["v"])

        @self.V
        def _(v):
            v.wait_ge(self.sem("sac"), s2_ev)
            v.tensor_add(rows["a"], rows["m"], rows["v"])
            v.tensor_scalar_max(rows["a"], rows["a"], 1e-12)
            v.reciprocal(rows["a"], rows["a"])
            v.tensor_scalar_mul(rows["a"], rows["a"], 0.5)
            v.tensor_mul(rows["b"], rows["m"], rows["a"])
            self.inc("sdv", v.tensor_mul(rows["a"], rows["v"], rows["a"]))
        w_dv = self.val("sdv")
        w1b_ev = bcast8(rows["b"], "sdv", w_dv, ws2[64:64 + NE, :])

        @self.V
        def _(v):
            v.wait_ge(self.sem("sac"), w1b_ev)
            self.inc("sdv", v.tensor_mul(ws2[0:NE, :], ws1[64:64 + NE, :],
                                         ws2[64:64 + NE, :]))
        g1_dv = self.val("sdv")
        w2b_ev = bcast8(rows["a"], "sdv", g1_dv, ws2[32:32 + NE, :])

        @self.V
        def _(v):
            v.wait_ge(self.sem("sac"), w2b_ev)
            R["g8"].wait_w(self, v)
            v.tensor_mul(xscr[0:NE, 0:TT], ws1[32:32 + NE, :], ws2[32:32 + NE, :])
            self.inc("sdv", v.tensor_add(g8[0:NE, :], ws2[0:NE, :], xscr[0:NE, 0:TT]))
        g8_dv = self.val("sdv")
        R["ws"].rd("sdv", g8_dv)
        R["xsq"].rd("sdv", g8_dv)

        @self.S
        def _(sy):
            self.qdrain(sy, 'sx')
            sy.wait_ge(self.sem("sdv"), g8_dv)
            R["gdr"].wait_w(self, sy)
            self.inc("sx", sy.dma_start(out=self.g_dr, in_=g8[:, :]), 16)
        grow_ld = self.val("sx")
        R["g8"].rd("sx", grow_ld)
        if self.lim.get("rstage", 9) <= 5:
            self.gb_ready = self.val("sac"); self.grow_read_mm = self.val("smm"); return

        grow_cp = {}
        growf = bf["growf"]
        cpv = []
        for e in range(NE):
            Pi = e % 2
            rowf = growf[0:1, (e % 2) * TT:(e % 2 + 1) * TT]

            @self.G
            def _(g):
                qn = self.gq_begin(g)
                if e == 0:
                    g.wait_ge(self.sem("sx"), grow_ld)
                if e >= 2:
                    g.wait_ge(self.sem("smm"), grow_cp[e - 2])
                self.inc(qn, g.dma_start(out=rowf, in_=self.g_dr[e:e + 1, :]), 16)
                cpv.append((qn, self.val(qn)))

            @self.T
            def _(t):
                t.wait_ge(self.sem(cpv[-1][0]), cpv[-1][1])
                R["P"][Pi].wait_w(self, t)
                ii = None
                for ts in range(2):
                    ii = t.matmul(P[Pi][0:128, ts * 512:(ts + 1) * 512], self.onesf[:, :],
                                  rowf[0:1, ts * 512:(ts + 1) * 512],
                                  start=True, stop=True)
                self.inc("smm", ii)
            grow_cp[e] = self.val("smm")
            mmv = self.val("smm")

            @self.A
            def _(a):
                a.wait_ge(self.sem("smm"), mmv)
                self.inc("sac", a.activation(out=bf["gb_sb"][:, e * TT:(e + 1) * TT],
                                             in_=P[Pi][:, 0:TT], func=AF.Copy))
            R["P"][Pi].rd("sac", self.val("sac"))
        R["gdr"].rd("sw", self.val("sw"))
        self.gb_ready = self.val("sac")
        self.grow_read_mm = self.val("smm")
        if self.lim.get("dump") and not hasattr(self, "_dumped"):
            self._dumped = True
            dmp_g8 = self.nc.dram_tensor("dmp_g8", [NE, TT], BF16, kind="ExternalOutput").ap()
            dmp_ws1 = self.nc.dram_tensor("dmp_ws1", [128, TT], F32, kind="ExternalOutput").ap()
            dmp_ws2 = self.nc.dram_tensor("dmp_ws2", [128, TT], F32, kind="ExternalOutput").ap()
            dmp_gb = self.nc.dram_tensor("dmp_gb", [128, 2 * TT], BF16, kind="ExternalOutput").ap()
            dmp_gf = self.nc.dram_tensor("dmp_gf", [1, 2 * TT], F32, kind="ExternalOutput").ap()
            dmp_wsm = self.nc.dram_tensor("dmp_wsm", [128, 16], F32, kind="ExternalOutput").ap()
            gbv = self.gb_ready

            @self.S
            def _(sy):
                self.qdrain(sy, 'sx')
                sy.wait_ge(self.sem("sac"), gbv)
                sy.wait_ge(self.sem("sw"), self.val("sw"))
                self.inc("sx", sy.dma_start(out=dmp_g8, in_=g8[:, :]), 16)
                self.inc("sx", sy.dma_start(out=dmp_ws1, in_=ws1[:, :]), 16)
                self.inc("sx", sy.dma_start(out=dmp_ws2, in_=ws2[:, :]), 16)
                self.inc("sx", sy.dma_start(out=dmp_gb, in_=bf["gb_sb"][:, 0:2 * TT]), 16)
                self.inc("sx", sy.dma_start(out=dmp_gf, in_=growf[0:1, :]), 16)
                self.inc("sx", sy.dma_start(out=dmp_wsm, in_=wsm[:, :]), 16)

    def _expert(self, e, bf):
        P, R = bf["P"], bf["R"]
        act_sb, xn_sb, x_sb = bf["act_sb"], bf["xn_sb"], bf["x_sb"]
        w1b, w3b, w2b = bf["w1b"], bf["w3b"], bf["w2b"]

        w_ready = {}

        def load_w13(ic):
            par = ic % 2
            @self.G
            def _(g):
                qn = self.gq_begin(g)
                R["w1b"][par].wait_w(self, g)
                R["w3b"][par].wait_w(self, g)
                self.inc(qn, g.dma_start(out=w1b[:, par * 2048:(par + 1) * 2048],
                                         in_=self.w1_p[e, ic]), 16)
                self.inc(qn, g.dma_start(out=w3b[:, par * 2048:(par + 1) * 2048],
                                         in_=self.w3_p[e, ic]), 16)
                w_ready[ic] = (qn, self.val(qn))

        load_w13(0)
        load_w13(1)

        act_evac = {}
        for ic in range(NIC):
            par = ic % 2
            @self.T
            def _(t):
                t.wait_ge(self.sem(w_ready[ic][0]), w_ready[ic][1])
                if ic == 0:
                    t.wait_ge(self.sem("sdv"), self.xn_ready)
                R["P"][par].wait_w(self, t)
                R["P"][2 + par].wait_w(self, t)
                for dc in range(NDC):
                    lhs1 = w1b[:, par * 2048 + dc * 128: par * 2048 + (dc + 1) * 128]
                    for ts in range(2):
                        t.matmul(P[par][:, ts * 512:(ts + 1) * 512], lhs1,
                                 xn_sb[:, dc * TT + ts * 512: dc * TT + (ts + 1) * 512],
                                 start=(dc == 0), stop=(dc == NDC - 1))
                ii = None
                for dc in range(NDC):
                    lhs3 = w3b[:, par * 2048 + dc * 128: par * 2048 + (dc + 1) * 128]
                    for ts in range(2):
                        ii = t.matmul(P[2 + par][:, ts * 512:(ts + 1) * 512], lhs3,
                                      xn_sb[:, dc * TT + ts * 512: dc * TT + (ts + 1) * 512],
                                      start=(dc == 0), stop=(dc == NDC - 1))
                self.inc("smm", ii)
            h_mm = self.val("smm")
            R["w1b"][par].rd("smm", h_mm)
            R["w3b"][par].rd("smm", h_mm)

            if ic + 2 < NIC:
                load_w13(ic + 2)

            @self.A
            def _(a):
                a.wait_ge(self.sem("smm"), h_mm)
                if ic == 0:
                    a.wait_ge(self.sem("sac"), self.gb_ready)
                R["act"].wait_w(self, a)
                self.inc("sac", a.activation(out=act_sb[:, ic * TT:(ic + 1) * TT],
                                             in_=P[par][:, :], func=AF.Silu))
            sl = self.val("sac")
            R["P"][par].rd("sac", sl)

            @self.V
            def _(v):
                v.wait_ge(self.sem("sac"), sl)
                v.tensor_tensor(act_sb[:, ic * TT:(ic + 1) * TT],
                                act_sb[:, ic * TT:(ic + 1) * TT], P[2 + par][:, :], OP.mult)
                self.inc("sdv", v.tensor_tensor(act_sb[:, ic * TT:(ic + 1) * TT],
                                                act_sb[:, ic * TT:(ic + 1) * TT],
                                                bf["gb_sb"][:, e * TT:(e + 1) * TT], OP.mult))
            act_evac[ic] = self.val("sdv")
            R["P"][2 + par].rd("sdv", act_evac[ic])

        w2_ready = {}

        def load_w2(do):
            par = do % 2
            @self.G
            def _(g):
                qn = self.gq_begin(g)
                R["w2b"][par].wait_w(self, g)
                self.inc(qn, g.dma_start(
                    out=w2b[:, par * NHC * 128: par * NHC * 128 + NIC * 128],
                    in_=self.w2_p[e, do]), 16)
                w2_ready[do] = (qn, self.val(qn))

        load_w2(0)
        load_w2(1)

        for do in range(NDC):
            par = do % 2
            @self.T
            def _(t):
                t.wait_ge(self.sem(w2_ready[do][0]), w2_ready[do][1])
                if do == 0:
                    t.wait_ge(self.sem("sdv"), act_evac[NIC - 1])
                R["P"][par].wait_w(self, t)
                ii = None
                for ic in range(NIC):
                    lhs = w2b[:, par * NHC * 128 + ic * 128: par * NHC * 128 + (ic + 1) * 128]
                    for ts in range(2):
                        ii = t.matmul(P[par][:, ts * 512:(ts + 1) * 512], lhs,
                                      act_sb[:, ic * TT + ts * 512: ic * TT + (ts + 1) * 512],
                                      start=(ic == 0), stop=(ic == NIC - 1))
                self.inc("smm", ii)
            y_mm = self.val("smm")
            R["w2b"][par].rd("smm", y_mm)
            if do == NDC - 1:
                R["act"].rd("smm", y_mm)

            if do + 2 < NDC:
                load_w2(do + 2)

            @self.V
            def _(v):
                v.wait_ge(self.sem("smm"), y_mm)
                self.inc("sdv", v.tensor_tensor(x_sb[:, do * TT:(do + 1) * TT],
                                                x_sb[:, do * TT:(do + 1) * TT],
                                                P[par][:, :], OP.add))
            R["P"][par].rd("sdv", self.val("sdv"))
        self.x_acc_dv = self.val("sdv")

    def _shared(self, half, bf):
        P, R = bf["P"], bf["R"]
        act_sb, xn_sb, x_sb = bf["act_sb"], bf["xn_sb"], bf["x_sb"]
        w1b, w3b, w2b = bf["w1b"], bf["w3b"], bf["w2b"]
        hofs = half * 512

        w_ready = {}

        def load_sw13(hc):
            par = hc % 2
            @self.G
            def _(g):
                qn = self.gq_begin(g)
                R["w1b"][par].wait_w(self, g)
                R["w3b"][par].wait_w(self, g)
                self.inc(qn, g.dma_start(out=w1b[:, par * 2048:(par + 1) * 2048],
                                         in_=self.sw1_p[hc]), 16)
                self.inc(qn, g.dma_start(out=w3b[:, par * 2048:(par + 1) * 2048],
                                         in_=self.sw3_p[hc]), 16)
                w_ready[hc] = (qn, self.val(qn))

        load_sw13(0)
        load_sw13(1)

        act_evac = {}
        for hc in range(NHC):
            par = hc % 2
            @self.T
            def _(t):
                t.wait_ge(self.sem(w_ready[hc][0]), w_ready[hc][1])
                R["P"][par].wait_w(self, t)
                R["P"][2 + par].wait_w(self, t)
                for dc in range(NDC):
                    t.matmul(P[par][:, 0:512],
                             w1b[:, par * 2048 + dc * 128: par * 2048 + (dc + 1) * 128],
                             xn_sb[:, dc * TT + hofs: dc * TT + hofs + 512],
                             start=(dc == 0), stop=(dc == NDC - 1))
                ii = None
                for dc in range(NDC):
                    ii = t.matmul(P[2 + par][:, 0:512],
                                  w3b[:, par * 2048 + dc * 128: par * 2048 + (dc + 1) * 128],
                                  xn_sb[:, dc * TT + hofs: dc * TT + hofs + 512],
                                  start=(dc == 0), stop=(dc == NDC - 1))
                self.inc("smm", ii)
            h_mm = self.val("smm")
            R["w1b"][par].rd("smm", h_mm)
            R["w3b"][par].rd("smm", h_mm)

            if hc + 2 < NHC:
                load_sw13(hc + 2)

            @self.A
            def _(a):
                a.wait_ge(self.sem("smm"), h_mm)
                R["act"].wait_w(self, a)
                self.inc("sac", a.activation(out=act_sb[:, hc * 512:(hc + 1) * 512],
                                             in_=P[par][:, 0:512], func=AF.Silu))
            sl = self.val("sac")
            R["P"][par].rd("sac", sl)

            @self.V
            def _(v):
                v.wait_ge(self.sem("sac"), sl)
                self.inc("sdv", v.scalar_tensor_tensor(
                    act_sb[:, hc * 512:(hc + 1) * 512],
                    act_sb[:, hc * 512:(hc + 1) * 512], 0.5,
                    P[2 + par][:, 0:512], OP.mult, OP.mult))
            act_evac[hc] = self.val("sdv")
            R["P"][2 + par].rd("sdv", act_evac[hc])

        w2_ready = {}

        def load_sw2(do):
            par = do % 2
            @self.G
            def _(g):
                qn = self.gq_begin(g)
                R["w2b"][par].wait_w(self, g)
                self.inc(qn, g.dma_start(out=w2b[:, par * NHC * 128:(par + 1) * NHC * 128],
                                         in_=self.sw2_p[do]), 16)
                w2_ready[do] = (qn, self.val(qn))

        load_sw2(0)
        load_sw2(1)

        for do in range(NDC):
            par = do % 2
            @self.T
            def _(t):
                t.wait_ge(self.sem(w2_ready[do][0]), w2_ready[do][1])
                if do == 0:
                    t.wait_ge(self.sem("sdv"), act_evac[NHC - 1])
                R["P"][par].wait_w(self, t)
                ii = None
                for hc in range(NHC):
                    ii = t.matmul(P[par][:, 0:512],
                                  w2b[:, par * NHC * 128 + hc * 128: par * NHC * 128 + (hc + 1) * 128],
                                  act_sb[:, hc * 512:(hc + 1) * 512],
                                  start=(hc == 0), stop=(hc == NHC - 1))
                self.inc("smm", ii)
            y_mm = self.val("smm")
            R["w2b"][par].rd("smm", y_mm)
            if do == NDC - 1:
                R["act"].rd("smm", y_mm)

            if do + 2 < NDC:
                load_sw2(do + 2)

            @self.V
            def _(v):
                v.wait_ge(self.sem("smm"), y_mm)
                self.inc("sdv", v.tensor_tensor(
                    x_sb[:, do * TT + hofs: do * TT + hofs + 512],
                    x_sb[:, do * TT + hofs: do * TT + hofs + 512],
                    P[par][:, 0:512], OP.add))
            R["P"][par].rd("sdv", self.val("sdv"))
        self.x_acc_dv = self.val("sdv")

    def _tile_epilogue(self, call_id, tt, bf, t0):
        P, R = bf["P"], bf["R"]
        x_sb, xn_sb, xsq, rows = bf["x_sb"], bf["xn_sb"], bf["xsq"], bf["rows"]
        x_done = self.x_acc_dv

        if call_id == 1:
            @self.V
            def _(v):
                v.wait_ge(self.sem("sdv"), x_done)
                v.wait_ge(self.sem("smm"), self.grow_read_mm)
                R["xsq"].wait_w(self, v)

            @self.T
            def _(t):
                t.wait_ge(self.sem("sdv"), x_done)
                R["P"][2].wait_w(self, t)
                R["P"][3].wait_w(self, t)
            ab_dv, _sm = self.ln_stats(x_sb, xsq, P[2], P[3], rows, R["xsq"])
            R["P"][2].rd("sdv", ab_dv)
            R["P"][3].rd("sdv", ab_dv)
            ab2 = self.bcast2(rows["a"], rows["b"], P[0], R["P"][0], P[1], R["P"][1],
                              bf["ab_sb"][:, :], bf["bb_sb"][:, :], "sdv", ab_dv)

            @self.V
            def _(v):
                v.wait_ge(self.sem("sac"), ab2)
                R["xn"].wait_w(self, v)
                R["xsq"].wait_w(self, v)
                ii = None
                for dc in range(NDC):
                    v.tensor_tensor(xsq[:, 0:TT], x_sb[:, dc * TT:(dc + 1) * TT],
                                    bf["ab_sb"][:, :], OP.mult)
                    ii = v.tensor_tensor(xn_sb[:, dc * TT:(dc + 1) * TT], xsq[:, 0:TT],
                                         bf["bb_sb"][:, :], OP.add)
                self.inc("sdv", ii)
            xn2 = self.val("sdv")

            @self.S
            def _(sy):
                self.qdrain(sy, 'sx')
                sy.wait_ge(self.sem("sdv"), xn2)
                for dc in range(NDC):
                    self.inc("sx", sy.dma_start(
                        out=self.xnA[dc * 128:(dc + 1) * 128, t0:t0 + TT],
                        in_=xn_sb[:, dc * TT:(dc + 1) * TT]), 16)
                    self.inc("sx", sy.dma_start(
                        out=self.x_res[dc * 128:(dc + 1) * 128, t0:t0 + TT],
                        in_=x_sb[:, dc * TT:(dc + 1) * TT]), 16)
                for nm, sbuf in (("dbg_xres", x_sb), ("dbg_xnA", xn_sb)):
                    if nm in self.dbg_t:
                        for dc in range(NDC):
                            self.inc("sx", sy.dma_start(
                                out=self.dbg_t[nm][dc * 128:(dc + 1) * 128, t0:t0 + TT],
                                in_=sbuf[:, dc * TT:(dc + 1) * TT]), 16)
            st = self.val("sx")
            R["x"].rd("sx", st)
            R["xn"].rd("sx", st)
        else:
            @self.S
            def _(sy):
                self.qdrain(sy, 'sx')
                sy.wait_ge(self.sem("sdv"), x_done)
                for dc in range(NDC):
                    self.inc("sx", sy.dma_start(
                        out=self.outT[dc * 128:(dc + 1) * 128, t0:t0 + TT],
                        in_=x_sb[:, dc * TT:(dc + 1) * TT]), 16)
            R["x"].rd("sx", self.val("sx"))

    def qkv_phase(self):
        nc = self.nc
        self.phase_barrier()
        barrier = self.phase_snap
        with ExitStack() as cctx:
            sb = lambda name, shape, dt: cctx.enter_context(nc.sbuf_tensor(name, shape, dt))
            xn_sb = sb("xnq", [128, NDC * TLOC], BF16)
            wbuf = sb("wqk", [128, 4 * 2048], BF16)
            wvb = sb("wvb", [128, NDC * 2048], BF16)
            cqb = sb("cqb", [128, D], F32)
            ckb = sb("ckb", [128, D], F32)
            bvb = sb("bvb", [128, D], F32)
            rws = sb("rws", [1, 3 * D], F32)
            stg = sb("qstg", [128, 2 * TLOC], BF16)

            with ExitStack() as px:
                PP = [px.enter_context(nc.psum_tensor(f"QP{i}", [128, TLOC], F32))
                      for i in range(2)]
                Rp = [Res(), Res()]
                Rw = [Res() for _ in range(4)]
                Rstg = [Res(), Res()]

                @self.S
                def _(sy):
                    self.qdrain(sy, 'sx')
                    self.wait_snap(sy, barrier)
                    sy.wait_ge(self.sem("sx"), self.rows_stored)
                    for dc in range(NDC):
                        self.inc("sx", sy.dma_start(out=xn_sb[:, dc * TLOC:(dc + 1) * TLOC],
                                                    in_=self.xnA[dc * 128:(dc + 1) * 128, :]), 16)
                    self.inc("sx", sy.dma_start(out=rws[0:1, 0:2 * D],
                                                in_=self.row_dr[0:2, :].rearrange("a d -> (a d)").rearrange("(o n) -> o n", o=1)), 16)
                    self.inc("sx", sy.dma_start(out=rws[0:1, 2 * D:3 * D], in_=self.bv_r), 16)
                    for bi in range(2):
                        self.inc("sx", sy.dma_start(
                            out=self.bias_t[:, bi * NDC:(bi + 1) * NDC],
                            in_=bass.AP(self.row_dr.tensor, (2 + bi) * D, [[1, 128], [128, NDC]]),
                            allow_slow_non_contiguous=True), 16)
                xn_ld = self.val("sx")

                @self.T
                def _(t):
                    self.wait_snap(t, barrier)
                    t.wait_ge(self.sem("sx"), xn_ld)
                    ii = None
                    for half in range(2):
                        for ts in range(D // 512):
                            ii = t.matmul(PP[half][:, ts * 512:(ts + 1) * 512],
                                          self.onesf[:, :],
                                          rws[0:1, half * D + ts * 512: half * D + (ts + 1) * 512],
                                          start=True, stop=True)
                    self.inc("smm", ii)
                cmm = self.val("smm")

                @self.A
                def _(a):
                    a.wait_ge(self.sem("smm"), cmm)
                    a.activation(out=cqb[:, :], in_=PP[0][:, 0:D], func=AF.Copy)
                    self.inc("sac", a.activation(out=ckb[:, :], in_=PP[1][:, 0:D], func=AF.Copy))
                Rp[0].rd("sac", self.val("sac"))
                Rp[1].rd("sac", self.val("sac"))

                @self.T
                def _(t):
                    Rp[0].wait_w(self, t)
                    ii = None
                    for ts in range(D // 512):
                        ii = t.matmul(PP[0][:, ts * 512:(ts + 1) * 512], self.onesf[:, :],
                                      rws[0:1, 2 * D + ts * 512: 2 * D + (ts + 1) * 512],
                                      start=True, stop=True)
                    self.inc("smm", ii)
                bvm = self.val("smm")

                @self.A
                def _(a):
                    a.wait_ge(self.sem("smm"), bvm)
                    self.inc("sac", a.activation(out=bvb[:, :], in_=PP[0][:, 0:D], func=AF.Copy))
                Rp[0].rd("sac", self.val("sac"))
                cb_done = self.val("sac")

                @self.G
                def _(g):
                    self.qdrain(g, 'sw')
                    for dc in range(NDC):
                        self.inc("sw", g.dma_start(out=wvb[:, dc * 2048:(dc + 1) * 2048],
                                                   in_=self.wv_p[dc]), 16)
                wv_ld = self.val("sw")

                w_ready = {}

                def load_w(pi, jc):
                    slot = pi * 2 + jc % 2
                    @self.G
                    def _(g):
                        qn = self.gq_begin(g)
                        Rw[slot].wait_w(self, g)
                        src = self.wq_p if pi == 0 else self.wk_p
                        self.inc(qn, g.dma_start(out=wbuf[:, slot * 2048:(slot + 1) * 2048],
                                                 in_=src[jc]), 16)
                        w_ready[(pi, jc)] = (qn, self.val(qn), slot)

                for pi in range(2):
                    load_w(pi, 0)
                    load_w(pi, 1)

                for pi in range(2):
                    scale = cqb if pi == 0 else ckb
                    bcol = pi
                    dst = self.q_dr if pi == 0 else self.kc_in.ap()
                    dbgn = "dbg_q" if pi == 0 else "dbg_k"
                    for jc in range(NDC):
                        wqn, wv, slot = w_ready[(pi, jc)]

                        @self.V
                        def _(v):
                            v.wait_ge(self.sem(wqn), wv)
                            v.wait_ge(self.sem("sac"), cb_done)
                            ii = None
                            for dc in range(NDC):
                                ii = v.tensor_tensor(
                                    wbuf[:, slot * 2048 + dc * 128: slot * 2048 + (dc + 1) * 128],
                                    wbuf[:, slot * 2048 + dc * 128: slot * 2048 + (dc + 1) * 128],
                                    scale[:, jc * 128:(jc + 1) * 128], OP.mult)
                            self.inc("sdv", ii)
                        wsc = self.val("sdv")

                        spar = jc % 2
                        @self.T
                        def _(t):
                            t.wait_ge(self.sem("sdv"), wsc)
                            if pi == 0 and jc == 0:
                                t.wait_ge(self.sem("sx"), xn_ld)
                            Rp[spar].wait_w(self, t)
                            ii = None
                            for dc in range(NDC):
                                lhs = wbuf[:, slot * 2048 + dc * 128: slot * 2048 + (dc + 1) * 128]
                                for ts in range(TLOC // 512):
                                    ii = t.matmul(PP[spar][:, ts * 512:(ts + 1) * 512], lhs,
                                                  xn_sb[:, dc * TLOC + ts * 512: dc * TLOC + (ts + 1) * 512],
                                                  start=(dc == 0), stop=(dc == NDC - 1))
                            self.inc("smm", ii)
                        pm = self.val("smm")
                        Rw[slot].rd("smm", pm)

                        if jc + 2 < NDC:
                            load_w(pi, jc + 2)

                        @self.A
                        def _(a):
                            a.wait_ge(self.sem("smm"), pm)
                            Rstg[spar].wait_w(self, a)
                            self.inc("sac", a.activation(
                                out=stg[:, spar * TLOC:(spar + 1) * TLOC],
                                in_=PP[spar][:, :], func=AF.Identity,
                                bias=self.bias_t[:, bcol * NDC + jc: bcol * NDC + jc + 1]))
                        ev = self.val("sac")
                        Rp[spar].rd("sac", ev)

                        @self.S
                        def _(sy):
                            self.qdrain(sy, 'sx')
                            sy.wait_ge(self.sem("sac"), ev)
                            self.inc("sx", sy.dma_start(
                                out=dst[jc * 128:(jc + 1) * 128, :],
                                in_=stg[:, spar * TLOC:(spar + 1) * TLOC]), 16)
                            if dbgn in self.dbg_t:
                                self.inc("sx", sy.dma_start(
                                    out=self.dbg_t[dbgn][jc * 128:(jc + 1) * 128, :],
                                    in_=stg[:, spar * TLOC:(spar + 1) * TLOC]), 16)
                        Rstg[spar].rd("sx", self.val("sx"))

                if self.lim.get("wdump"):
                    wdo = nc.dram_tensor("wdump", [128, NDC * 2048], BF16, kind="ExternalOutput").ap()
                    cdo = nc.dram_tensor("cdump", [128, 2 * D], F32, kind="ExternalOutput").ap()
                    @self.S
                    def _(sy):
                        self.qdrain(sy, 'sx')
                        sy.wait_ge(self.sem("sw"), wv_ld)
                        sy.wait_ge(self.sem("sac"), cb_done)
                        self.inc("sx", sy.dma_start(out=wdo, in_=wvb[:, :]), 16)
                        self.inc("sx", sy.dma_start(out=cdo[:, 0:D], in_=cqb[:, :]), 16)
                        self.inc("sx", sy.dma_start(out=cdo[:, D:2 * D], in_=ckb[:, :]), 16)

                vdst = self.vc_in.ap()
                for tc in range(NDC):
                    spar = tc % 2
                    @self.T
                    def _(t):
                        if tc < 2:
                            t.wait_ge(self.sem("sw"), wv_ld)
                        Rp[spar].wait_w(self, t)
                        ii = None
                        for dc in range(NDC):
                            lhs = xn_sb[:, dc * TLOC + tc * 128: dc * TLOC + (tc + 1) * 128]
                            for js in range(D // 512):
                                ii = t.matmul(PP[spar][:, js * 512:(js + 1) * 512], lhs,
                                              wvb[:, dc * 2048 + js * 512: dc * 2048 + (js + 1) * 512],
                                              start=(dc == 0), stop=(dc == NDC - 1))
                        self.inc("smm", ii)
                    pm = self.val("smm")

                    @self.V
                    def _(v):
                        v.wait_ge(self.sem("smm"), pm)
                        Rstg[spar].wait_w(self, v)
                        self.inc("sdv", v.tensor_add(
                            stg[:, spar * TLOC:(spar + 1) * TLOC],
                            PP[spar][:, :], bvb[:, :]))
                    ev = self.val("sdv")
                    Rp[spar].rd("sdv", ev)

                    @self.S
                    def _(sy):
                        self.qdrain(sy, 'sx')
                        sy.wait_ge(self.sem("sdv"), ev)
                        self.inc("sx", sy.dma_start(
                            out=vdst[tc * 128:(tc + 1) * 128, :],
                            in_=stg[:, spar * TLOC:(spar + 1) * TLOC]), 16)
                        if "dbg_v" in self.dbg_t:
                            self.inc("sx", sy.dma_start(
                                out=self.dbg_t["dbg_v"][tc * 128:(tc + 1) * 128, :],
                                in_=stg[:, spar * TLOC:(spar + 1) * TLOC]), 16)
                    Rstg[spar].rd("sx", self.val("sx"))
            self.phase_snap = self.snap()

    def gather_phase(self):
        kv_stored = self.val("sx")

        @self.G
        def _(g):
            self.qdrain(g, 'sw')
            g.wait_ge(self.sem("sx"), kv_stored)
            self.inc("scc", g.collective_compute(
                "AllGather", OP.bypass,
                replica_groups=[[0, 1, 2, 3], [4, 5, 6, 7]],
                ins=[self.kc_in.ap().opt()], outs=[self.kg.ap().opt()]))
            self.inc("scc", g.collective_compute(
                "AllGather", OP.bypass,
                replica_groups=[[0, 1, 2, 3], [4, 5, 6, 7]],
                ins=[self.vc_in.ap().opt()], outs=[self.vg.ap().opt()]))
        self.phase_snap = self.snap()

    def attn_phase(self):
        nc = self.nc
        self.phase_barrier()
        barrier = self.phase_snap
        kgap, vgap = self.kg.ap(), self.vg.ap()
        with ExitStack() as cctx:
            sb = lambda name, shape, dt: cctx.enter_context(nc.sbuf_tensor(name, shape, dt))
            k_sb = sb("k_sb", [128, 2 * SEQ], BF16)
            vt_sb = sb("vt_sb", [128, 2 * SEQ], BF16)
            o_sb = sb("o_sb", [128, NDC * TLOC], BF16)
            q_sb = sb("q_sb", [128, 4 * 512], BF16)
            pt_sb = sb("pt_sb", [128, 2 * 2048], BF16)
            lrow = sb("lrow", [1, 1024], F32)
            lb_sb = sb("lb_sb", [128, 1024], F32)
            wo_b = sb("wo_b", [128, 2 * 2048], BF16)
            ost = sb("ost", [128, 2 * 512], F32)

            with ExitStack() as px:
                S4 = px.enter_context(nc.psum_tensor("S4", [128, 2048], F32))
                O2 = [px.enter_context(nc.psum_tensor(f"O{i}", [128, 512], F32))
                      for i in range(2)]
                L2 = [px.enter_context(nc.psum_tensor(f"L{i}", [1, 512], F32))
                      for i in range(2)]
                RS = Res()
                RO = [Res(), Res()]
                RL = [Res(), Res()]
                Rk = [Res(), Res()]
                Rvt = [Res(), Res()]
                Rq = [Res() for _ in range(4)]
                Rpt = [Res(), Res()]
                Rlb = Res()
                Ro = Res()

                kv_ready = {}

                def load_kv(h):
                    par = h % 2
                    @self.S
                    def _(sy):
                        self.qdrain(sy, 'sx')
                        if h == 0:
                            self.wait_snap(sy, barrier)
                            sy.wait_ge(self.sem("scc"), self.val("scc"))
                        Rk[par].wait_w(self, sy)
                        Rvt[par].wait_w(self, sy)
                        for r in range(GRP):
                            self.inc("sx", sy.dma_start(
                                out=k_sb[:, par * SEQ + r * TLOC: par * SEQ + (r + 1) * TLOC],
                                in_=kgap[r * D + h * 128: r * D + (h + 1) * 128, :]), 16)
                        self.inc("sx", sy.dma_start(
                            out=vt_sb[:, par * SEQ: (par + 1) * SEQ]
                            .rearrange("p (kc j) -> p kc j", kc=NKC),
                            in_=vgap[:, h * 128:(h + 1) * 128]
                            .rearrange("(kc p) j -> p kc j", p=128)), 16)
                    kv_ready[h] = self.val("sx")

                load_kv(0)

                q_ready = {}

                def load_q(h, qt):
                    slot = qt % 4
                    @self.S
                    def _(sy):
                        self.qdrain(sy, 'sx')
                        Rq[slot].wait_w(self, sy)
                        self.inc("sx", sy.dma_start(
                            out=q_sb[:, slot * 512:(slot + 1) * 512],
                            in_=self.q_dr[h * 128:(h + 1) * 128, qt * 512:(qt + 1) * 512]), 16)
                    q_ready[(h, qt)] = self.val("sx")

                load_q(0, 0)
                load_q(0, 1)

                NH = self.lim.get("nheads", HEADS)
                for h in range(NH):
                    par = h % 2
                    if h + 1 < NH:
                        load_kv(h + 1)
                    for qp in range(2):
                        qt0, qt1 = 2 * qp, 2 * qp + 1
                        s0, s1 = qt0 % 4, qt1 % 4
                        if not (h == 0 and qp == 0):
                            load_q(h, qt0)
                            load_q(h, qt1)
                        NKCL = self.lim.get("nkc", NKC)
                        for kp in range(NKCL // 2):
                            kc0, kc1 = 2 * kp, 2 * kp + 1
                            @self.T
                            def _(t):
                                if kp == 0:
                                    t.wait_ge(self.sem("sx"), q_ready[(h, qt0)])
                                    t.wait_ge(self.sem("sx"), q_ready[(h, qt1)])
                                    t.wait_ge(self.sem("sx"), kv_ready[h])
                                    RO[0].wait_w(self, t)
                                    RO[1].wait_w(self, t)
                                    RL[0].wait_w(self, t)
                                    RL[1].wait_w(self, t)
                                RS.wait_w(self, t)
                                ii = None
                                for i, kc in enumerate((kc0, kc1)):
                                    lhsk = k_sb[:, par * SEQ + kc * 128: par * SEQ + (kc + 1) * 128]
                                    t.matmul(S4[:, (2 * i) * 512:(2 * i + 1) * 512], lhsk,
                                             q_sb[:, s0 * 512:(s0 + 1) * 512],
                                             start=True, stop=True)
                                    ii = t.matmul(S4[:, (2 * i + 1) * 512:(2 * i + 2) * 512], lhsk,
                                                  q_sb[:, s1 * 512:(s1 + 1) * 512],
                                                  start=True, stop=True)
                                self.inc("smm", ii)
                            smv = self.val("smm")

                            po = (kp % 2) * 2048
                            @self.A
                            def _(a):
                                a.wait_ge(self.sem("smm"), smv)
                                Rpt[kp % 2].wait_w(self, a)
                                self.inc("sac", a.activation(
                                    out=pt_sb[:, po: po + 2048],
                                    in_=S4[:, :], func=AF.Exp))
                            ev = self.val("sac")
                            RS.rd("sac", ev)

                            @self.T
                            def _(t):
                                t.wait_ge(self.sem("sac"), ev)
                                ii = None
                                for i, kc in enumerate((kc0, kc1)):
                                    lhsv = vt_sb[:, par * SEQ + kc * 128: par * SEQ + (kc + 1) * 128]
                                    t.matmul(O2[0][:, :], lhsv,
                                             pt_sb[:, po + (2 * i) * 512: po + (2 * i + 1) * 512],
                                             start=(kc == 0), stop=(kc == NKCL - 1))
                                    t.matmul(O2[1][:, :], lhsv,
                                             pt_sb[:, po + (2 * i + 1) * 512: po + (2 * i + 2) * 512],
                                             start=(kc == 0), stop=(kc == NKCL - 1))
                                    t.matmul(L2[0][0:1, :], self.ones_bf[:, :],
                                             pt_sb[:, po + (2 * i) * 512: po + (2 * i + 1) * 512],
                                             start=(kc == 0), stop=(kc == NKCL - 1))
                                    ii = t.matmul(L2[1][0:1, :], self.ones_bf[:, :],
                                                  pt_sb[:, po + (2 * i + 1) * 512: po + (2 * i + 2) * 512],
                                                  start=(kc == 0), stop=(kc == NKCL - 1))
                                self.inc("smm", ii)
                            av = self.val("smm")
                            Rpt[kp % 2].rd("smm", av)
                            Rq[s0].rd("smm", av)
                            Rq[s1].rd("smm", av)
                            Rk[par].rd("smm", av)
                            Rvt[par].rd("smm", av)
                        av_all = self.val("smm")

                        @self.V
                        def _(v):
                            v.wait_ge(self.sem("smm"), av_all)
                            v.tensor_copy(lrow[0:1, 0:512], L2[0][0:1, :])
                            v.tensor_copy(lrow[0:1, 512:1024], L2[1][0:1, :])
                            self.inc("sdv", v.reciprocal(lrow[0:1, :], lrow[0:1, :]))
                        lr = self.val("sdv")
                        RL[0].rd("sdv", lr)
                        RL[1].rd("sdv", lr)

                        @self.T
                        def _(t):
                            t.wait_ge(self.sem("sdv"), lr)
                            RS.wait_w(self, t)
                            ii = None
                            for ts in range(2):
                                ii = t.matmul(S4[:, ts * 512:(ts + 1) * 512], self.onesf[:, :],
                                              lrow[0:1, ts * 512:(ts + 1) * 512],
                                              start=True, stop=True)
                            self.inc("smm", ii)
                        lbm = self.val("smm")

                        @self.A
                        def _(a):
                            a.wait_ge(self.sem("smm"), lbm)
                            Rlb.wait_w(self, a)
                            self.inc("sac", a.activation(out=lb_sb[:, :], in_=S4[:, 0:1024],
                                                         func=AF.Copy))
                        lbe = self.val("sac")
                        RS.rd("sac", lbe)

                        @self.V
                        def _(v):
                            v.wait_ge(self.sem("sac"), lbe)
                            if self.lim.get("nol"):
                                v.tensor_copy(
                                    o_sb[:, h * TLOC + qt0 * 512: h * TLOC + (qt0 + 1) * 512],
                                    O2[0][:, :])
                                self.inc("sdv", v.tensor_copy(
                                    o_sb[:, h * TLOC + qt1 * 512: h * TLOC + (qt1 + 1) * 512],
                                    O2[1][:, :]))
                            else:
                                v.tensor_tensor(
                                    o_sb[:, h * TLOC + qt0 * 512: h * TLOC + (qt0 + 1) * 512],
                                    O2[0][:, :], lb_sb[:, 0:512], OP.mult)
                                self.inc("sdv", v.tensor_tensor(
                                    o_sb[:, h * TLOC + qt1 * 512: h * TLOC + (qt1 + 1) * 512],
                                    O2[1][:, :], lb_sb[:, 512:1024], OP.mult))
                        oev = self.val("sdv")
                        RO[0].rd("sdv", oev)
                        RO[1].rd("sdv", oev)
                        Rlb.rd("sdv", oev)
                o_done = self.val("sdv")

                if self.lim.get("ptdump"):
                    pdo = nc.dram_tensor("ptdump", [128, 2 * 1024], BF16, kind="ExternalOutput").ap()
                    sdo = nc.dram_tensor("sdump", [128, 1024], F32, kind="ExternalOutput").ap()
                    @self.V
                    def _(v):
                        v.wait_ge(self.sem("sdv"), o_done)
                        self.inc("sdv", v.tensor_copy(lb_sb[:, :], S4[:, 0:1024]))
                    sde = self.val("sdv")
                    @self.S
                    def _(sy):
                        self.qdrain(sy, 'sx')
                        sy.wait_ge(self.sem("sdv"), sde)
                        self.inc("sx", sy.dma_start(out=pdo, in_=pt_sb[:, :]), 16)
                        self.inc("sx", sy.dma_start(out=sdo, in_=lb_sb[:, :]), 16)

                if self.lim.get("odump"):
                    odo = nc.dram_tensor("odump", [D, TLOC], BF16, kind="ExternalOutput").ap()
                    @self.S
                    def _(sy):
                        self.qdrain(sy, 'sx')
                        sy.wait_ge(self.sem("sdv"), o_done)
                        for jc in range(NDC):
                            self.inc("sx", sy.dma_start(
                                out=odo[jc * 128:(jc + 1) * 128, :],
                                in_=o_sb[:, jc * TLOC:(jc + 1) * TLOC]), 16)

                Rwo = [Res(), Res()]
                Rost = [Res(), Res()]
                wo_ready = {}

                def load_wo(do):
                    par = do % 2
                    @self.G
                    def _(g):
                        qn = self.gq_begin(g)
                        Rwo[par].wait_w(self, g)
                        self.inc(qn, g.dma_start(out=wo_b[:, par * 2048:(par + 1) * 2048],
                                                 in_=self.wo_p[do]), 16)
                        wo_ready[do] = (qn, self.val(qn))

                load_wo(0)
                load_wo(1)

                for do in range(NDC):
                    par = do % 2
                    for qt in range(4):
                        opar = qt % 2
                        @self.T
                        def _(t):
                            t.wait_ge(self.sem(wo_ready[do][0]), wo_ready[do][1])
                            if do == 0 and qt == 0:
                                t.wait_ge(self.sem("sdv"), o_done)
                            RO[opar].wait_w(self, t)
                            ii = None
                            for jc in range(NDC):
                                lhs = wo_b[:, par * 2048 + jc * 128: par * 2048 + (jc + 1) * 128]
                                ii = t.matmul(O2[opar][:, :], lhs,
                                              o_sb[:, jc * TLOC + qt * 512: jc * TLOC + (qt + 1) * 512],
                                              start=(jc == 0), stop=(jc == NDC - 1))
                            self.inc("smm", ii)
                        wmm = self.val("smm")
                        Ro.rd("smm", wmm)
                        Rwo[par].rd("smm", wmm)

                        sslot = (do * 4 + qt) % 2
                        @self.A
                        def _(a):
                            a.wait_ge(self.sem("smm"), wmm)
                            Rost[sslot].wait_w(self, a)
                            self.inc("sac", a.activation(
                                out=ost[:, sslot * 512:(sslot + 1) * 512],
                                in_=O2[opar][:, :], func=AF.Identity,
                                bias=self.bias_t[:, 3 * NDC + do: 3 * NDC + do + 1]))
                        wev = self.val("sac")
                        RO[opar].rd("sac", wev)

                        @self.G
                        def _(g):
                            self.qdrain(g, 'sw')
                            g.wait_ge(self.sem("sac"), wev)
                            self.inc("sw", g.dma_start(
                                out=self.x_res[do * 128:(do + 1) * 128, qt * 512:(qt + 1) * 512],
                                in_=ost[:, sslot * 512:(sslot + 1) * 512],
                                accum_op=OP.add), 16)
                        Rost[sslot].rd("sw", self.val("sw"))
                    if do + 2 < NDC:
                        load_wo(do + 2)
            self.phase_snap = self.snap()

    def final_wait(self):
        @self.S
        def _(sy):
            self.qdrain(sy, 'sx')
            sy.wait_ge(self.sem("sx"), self.val("sx"))
            sy.wait_ge(self.sem("sw"), self.val("sw"))


def _pack_inputs(inputs):
    f32 = np.float32
    i = inputs
    pk4 = lambda WT: np.ascontiguousarray(
        WT.reshape(NDC, 128, NDC, 128).transpose(2, 1, 0, 3).reshape(NDC, 128, NDC * 128))
    d = {}
    d["wq_p"] = pk4(np.asarray(i["L0_attn_Wq"], f32).T.copy())
    d["wk_p"] = pk4(np.asarray(i["L0_attn_Wk"], f32).T.copy())
    d["wv_p"] = np.ascontiguousarray(np.asarray(i["L0_attn_Wv"], f32).T.reshape(NDC, 128, D))
    d["wo_p"] = pk4(np.asarray(i["L0_attn_Wo"], f32).T.copy())
    bt = np.zeros((128, 4 * NDC), f32)
    for c, nm in enumerate(("L0_attn_bq", "L0_attn_bk", "L0_attn_bv", "L0_attn_bo")):
        bt[:, c * NDC:(c + 1) * NDC] = np.asarray(i[nm], f32).reshape(NDC, 128).T
    d["biases"] = bt
    d["bv_r"] = np.asarray(i["L0_attn_bv"], f32).reshape(1, D).copy()
    d["bq_r"] = np.asarray(i["L0_attn_bq"], f32).reshape(1, D).copy()
    d["bk_r"] = np.asarray(i["L0_attn_bk"], f32).reshape(1, D).copy()
    hl = np.asarray(i["L0_attn_helix_rotation"], f32)
    sp = np.asarray(i["L0_attn_spiral_weight"], f32)
    d["helix_t"] = np.ascontiguousarray(hl.reshape(NDC, 128).T)
    d["helix_r"] = hl.reshape(1, D).copy()
    d["spiral_t"] = np.ascontiguousarray(np.broadcast_to(sp[None, :], (128, NDC))).astype(f32)
    d["spiral_r"] = np.repeat(sp, 128).reshape(1, D).copy()
    d["gwT_p"] = np.ascontiguousarray(np.asarray(i["L0_ffn_gate_w"], f32).T.reshape(NDC, 128, NE))
    d["rb"] = np.asarray(i["L0_ffn_router_bias"], f32).reshape(NE, 1).copy()
    W1 = np.asarray(i["L0_ffn_W1"], f32)
    W3 = np.asarray(i["L0_ffn_W3"], f32)
    W2 = np.asarray(i["L0_ffn_W2"], f32)
    d["w1_p"] = np.ascontiguousarray(
        W1.reshape(NE, NDC, 128, NIC, 128).transpose(0, 3, 2, 1, 4).reshape(NE, NIC, 128, NDC * 128))
    d["w3_p"] = np.ascontiguousarray(
        W3.reshape(NE, NDC, 128, NIC, 128).transpose(0, 3, 2, 1, 4).reshape(NE, NIC, 128, NDC * 128))
    d["w2_p"] = np.ascontiguousarray(
        W2.reshape(NE, NIC, 128, NDC, 128).transpose(0, 3, 2, 1, 4).reshape(NE, NDC, 128, NIC * 128))
    sw1T = np.asarray(i["L0_ffn_sw1"], f32).T.copy()
    sw3T = np.asarray(i["L0_ffn_sw3"], f32).T.copy()
    sw2T = np.asarray(i["L0_ffn_sw2"], f32).T.copy()
    d["sw1_p"] = np.ascontiguousarray(
        sw1T.reshape(NDC, 128, NHC, 128).transpose(2, 1, 0, 3).reshape(NHC, 128, NDC * 128))
    d["sw3_p"] = np.ascontiguousarray(
        sw3T.reshape(NDC, 128, NHC, 128).transpose(2, 1, 0, 3).reshape(NHC, 128, NDC * 128))
    d["sw2_p"] = np.ascontiguousarray(
        sw2T.reshape(NHC, 128, NDC, 128).transpose(2, 1, 0, 3).reshape(NDC, 128, NHC * 128))
    d["identf"] = np.eye(128, dtype=np.float32)
    c = np.zeros((1, 8), np.float32); c[0, :len(CONSTS)] = CONSTS
    d["consts"] = c
    return d


_NC_CACHE = {}


def _get_nc(dbg=(), phases=("ffn1", "qkv", "gather", "attn", "ffn2"), lim=None):
    key = (tuple(sorted(dbg)), tuple(phases), tuple(sorted((lim or {}).items())))
    if key not in _NC_CACHE:
        _NC_CACHE[key] = Builder(bass.Bass("TRN2", target_bir_lowering=False,
                                           detect_race_conditions=False), dbg,
                                 phases, lim).build()
    return _NC_CACHE[key]


def kernel(**inputs):
    x = np.asarray(inputs["L0_attn_x"], np.float32)
    shared = _pack_inputs(inputs)
    nc = _get_nc()
    in_maps = []
    for c in range(8):
        seq, t0 = c // GRP, (c % GRP) * TLOC
        m = dict(shared)
        m["xT"] = np.ascontiguousarray(x[seq, t0:t0 + TLOC, :].T)
        in_maps.append(m)
    res = run_bass_kernel_spmd(nc, in_maps, core_ids=list(range(8)))
    out = np.empty((2, SEQ, D), np.float32)
    for c in range(8):
        seq, t0 = c // GRP, (c % GRP) * TLOC
        out[seq, t0:t0 + TLOC, :] = res.results[c]["outT"].T
    return out


# version 04 (final)
# speedup vs baseline: 1.2307x; 1.2023x over previous
import math
from contextlib import ExitStack

import numpy as np
import ml_dtypes
import concourse.bass as bass
import concourse.mybir as mybir
from concourse.bass_utils import run_bass_kernel_spmd

F32 = mybir.dt.float32
BF16 = mybir.dt.bfloat16
AF = mybir.ActivationFunctionType
OP = mybir.AluOpType
AX = mybir.AxisListType

D = 2048
NDC = 16
TLOC = 2048
TT = 1024
NTT = TLOC // TT
INTER = 1408
NIC = 11
NE = 8
NSH = 2816
NHC = 22
HEADS = 16
DH = 128
SEQ = 8192
GRP = 4
NKC = SEQ // 128
ISQ = 1.0 / math.sqrt(128.0)
M_STAT = 33
CONSTS = (0.5, 0.25, -0.5, -1.0, 1e-5)


class Res:
    __slots__ = ("g",)

    def __init__(self):
        self.g = {}

    def rd(self, sem, val):
        if self.g.get(sem, 0) < val:
            self.g[sem] = val

    def wait_w(self, b, eng):
        for s, v in self.g.items():
            eng.wait_ge(b.sem(s), v)
        self.g = {}


class Builder:
    def __init__(self, nc, dbg=(), phases=("ffn1", "qkv", "gather", "attn", "ffn2"),
                 lim=None):
        self.nc = nc
        self.dbg = set(dbg)
        self.phases = set(phases)
        self.lim = lim or {}

    def _mksem(self, name):
        self.sems[name] = [self.ctx.enter_context(self.nc.semaphore(name)), 0]

    def sem(self, n):
        return self.sems[n][0]

    def val(self, n):
        return self.sems[n][1]

    def inc(self, n, instr, k=1):
        s = self.sems[n]
        instr.then_inc(s[0], k)
        s[1] += k
        return s[1]

    def snap(self):
        return {n: v[1] for n, v in self.sems.items()}

    def wait_snap(self, eng, snapshot):
        for n, v in snapshot.items():
            if v > 0:
                eng.wait_ge(self.sem(n), v)

    def phase_barrier(self):
        snap = self.phase_snap
        for sec in (self.T, self.V, self.A, self.S, self.G):
            def _f(eng, snap=snap):
                self.wait_snap(eng, snap)
            sec(_f)

    def gq_begin(self, g):
        name = f"sw{self.swrot % 4}"
        self.swrot += 1
        if self.val(name) > 0:
            g.wait_ge(self.sem(name), self.val(name))
        return name

    def qdrain(self, eng, qsem):
        if self.val(qsem) > 0:
            eng.wait_ge(self.sem(qsem), self.val(qsem))

    def T(self, f): self.block.tensor(f)
    def V(self, f): self.block.vector(f)
    def A(self, f): self.block.scalar(f)
    def G(self, f): self.block.gpsimd(f)
    def S(self, f): self.block.sync(f)

    def build(self):
        nc = self.nc
        EI = dict(kind="ExternalInput")
        self.xT = nc.dram_tensor("xT", [D, TLOC], F32, **EI).ap()
        self.wq_p = nc.dram_tensor("wq_p", [NDC, 128, NDC * 128], F32, **EI).ap()
        self.wk_p = nc.dram_tensor("wk_p", [NDC, 128, NDC * 128], F32, **EI).ap()
        self.wv_p = nc.dram_tensor("wv_p", [NDC, 128, D], F32, **EI).ap()
        self.wo_p = nc.dram_tensor("wo_p", [NDC, 128, NDC * 128], F32, **EI).ap()
        self.biases = nc.dram_tensor("biases", [128, 4 * NDC], F32, **EI).ap()
        self.bv_r = nc.dram_tensor("bv_r", [1, D], F32, **EI).ap()
        self.bq_r = nc.dram_tensor("bq_r", [1, D], F32, **EI).ap()
        self.bk_r = nc.dram_tensor("bk_r", [1, D], F32, **EI).ap()
        self.helix_t = nc.dram_tensor("helix_t", [128, NDC], F32, **EI).ap()
        self.helix_r = nc.dram_tensor("helix_r", [1, D], F32, **EI).ap()
        self.spiral_t = nc.dram_tensor("spiral_t", [128, NDC], F32, **EI).ap()
        self.spiral_r = nc.dram_tensor("spiral_r", [1, D], F32, **EI).ap()
        self.gwT_p = nc.dram_tensor("gwT_p", [NDC, 128, NE], F32, **EI).ap()
        self.rb_d = nc.dram_tensor("rb", [NE, 1], F32, **EI).ap()
        self.w1_p = nc.dram_tensor("w1_p", [NE, NIC, 128, NDC * 128], F32, **EI).ap()
        self.w3_p = nc.dram_tensor("w3_p", [NE, NIC, 128, NDC * 128], F32, **EI).ap()
        self.w2_p = nc.dram_tensor("w2_p", [NE, NDC, 128, NIC * 128], F32, **EI).ap()
        self.sw1_p = nc.dram_tensor("sw1_p", [NHC, 128, NDC * 128], F32, **EI).ap()
        self.sw3_p = nc.dram_tensor("sw3_p", [NHC, 128, NDC * 128], F32, **EI).ap()
        self.sw2_p = nc.dram_tensor("sw2_p", [NDC, 128, NHC * 128], F32, **EI).ap()
        self.identf_d = nc.dram_tensor("identf", [128, 128], F32, **EI).ap()
        self.consts_d = nc.dram_tensor("consts", [1, 8], F32, **EI).ap()

        self.outT = nc.dram_tensor("outT", [D, TLOC], F32, kind="ExternalOutput").ap()

        self.x_res = nc.dram_tensor("x_res", [D, TLOC], F32).ap()
        self.xnA = nc.dram_tensor("xnA", [D, TLOC], BF16).ap()
        self.q_dr = nc.dram_tensor("q_dr", [D, TLOC], BF16).ap()
        self.g_dr = nc.dram_tensor("g_dr", [NE, TT], BF16).ap()
        self.row_dr = nc.dram_tensor("row_dr", [4, D], F32).ap()
        self.kc_in = nc.dram_tensor("kc_in", [D, TLOC], BF16)
        self.vc_in = nc.dram_tensor("vc_in", [TLOC, D], BF16)
        self.kg = nc.dram_tensor("kg", [GRP * D, TLOC], BF16)
        self.vg = nc.dram_tensor("vg", [GRP * TLOC, D], BF16)

        self.dbg_t = {}
        for name in self.dbg:
            shp = [TLOC, D] if name == "dbg_v" else [D, TLOC]
            dt = BF16 if name in ("dbg_xnA", "dbg_q", "dbg_k", "dbg_v") else F32
            self.dbg_t[name] = nc.dram_tensor(name, shp, dt, kind="ExternalOutput").ap()

        with ExitStack() as ctx:
            self.ctx = ctx
            self.block = ctx.enter_context(nc.Block())
            self.sems = {}
            for n in ("sw", "sx", "smm", "sac", "sdv", "scc",
                      "sw0", "sw1", "sw2", "sw3"):
                self._mksem(n)
            self.swrot = 0

            sb = lambda name, shape, dt: ctx.enter_context(nc.sbuf_tensor(name, shape, dt))
            self.combo = sb("combo", [128, NDC * M_STAT], F32)
            self.onesf = sb("onesf", [1, 128], F32)
            self.onesbr = sb("onesbr", [1, 128], BF16)
            self.ones_bf = sb("ones_bf", [128, 1], BF16)
            self.bias_t = sb("bias_t", [128, 4 * NDC], F32)
            self.rb8 = sb("rb8", [NE, 1], F32)
            self.gsum = sb("gsum", [NE, 1], F32)
            self.idf = sb("idf", [128, 128], F32)
            self.ones8f = sb("ones8f", [1, NE], F32)
            self.ones8c = sb("ones8c", [NE, 1], F32)
            self.consts = sb("consts_sb", [1, 8], F32)
            for ci, cv in enumerate(CONSTS):
                nc.const_aps.register(F32, cv, self.consts[0:1, ci:ci + 1])

            self.prologue()
            if "ffn1" in self.phases:
                self.ffn_call(1)
            if "qkv" in self.phases:
                self.qkv_phase()
            if "gather" in self.phases:
                self.gather_phase()
            if "attn" in self.phases:
                self.attn_phase()
            if "kdump" in self.phases:
                kdo = nc.dram_tensor("kdump", [GRP * D, TLOC], BF16, kind="ExternalOutput").ap()
                vdo = nc.dram_tensor("vdump", [GRP * TLOC, D], BF16, kind="ExternalOutput").ap()
                @self.S
                def _(sy):
                    self.qdrain(sy, 'sx')
                    sy.wait_ge(self.sem("scc"), self.val("scc"))
                    self.inc("sx", sy.dma_start(out=kdo, in_=self.kg.ap()), 16)
                    self.inc("sx", sy.dma_start(out=vdo, in_=self.vg.ap()), 16)
            if "xdump" in self.phases:
                @self.S
                def _(sy):
                    self.qdrain(sy, 'sx')
                    self.wait_snap(sy, self.phase_snap)
                    sy.wait_ge(self.sem("sw"), self.val("sw"))
                    self.inc("sx", sy.dma_start(out=self.outT, in_=self.x_res), 16)
            if "ffn2" in self.phases:
                self.ffn_call(2)
            self.final_wait()
        return nc

    def prologue(self):
        nc = self.nc
        with ExitStack() as pctx:
            trow = pctx.enter_context(nc.sbuf_tensor("trow", [1, 6 * D], F32))
            gps = pctx.enter_context(nc.psum_tensor("gsum_ps", [NE, 8], F32))

            @self.S
            def _(sy):
                self.qdrain(sy, 'sx')
                for dst, src in ((self.rb8[:, :], self.rb_d),
                                 (self.bias_t[:, :], self.biases),
                                 (self.idf[:, :], self.identf_d),
                                 (self.consts[:, :], self.consts_d),
                                 (trow[0:1, 0:D], self.helix_r),
                                 (trow[0:1, 4 * D:5 * D], self.bq_r),
                                 (trow[0:1, 5 * D:6 * D], self.bk_r)):
                    self.inc("sx", sy.dma_start(out=dst, in_=src), 16)
            ld0 = self.val("sx")

            @self.G
            def _(g):
                self.qdrain(g, 'sw')
                for dc in range(NDC):
                    self.inc("sw", g.dma_start(out=self.combo[:, dc * M_STAT: dc * M_STAT + NE],
                                               in_=self.gwT_p[dc]), 16)
                self.inc("sw", g.dma_start(out=trow[0:1, D:2 * D], in_=self.spiral_r), 16)
            ldw0 = self.val("sw")

            @self.V
            def _(v):
                v.memset(self.onesf[:, :], 1.0)
                v.memset(self.onesbr[:, :], 1.0)
                v.memset(self.ones_bf[:, :], 1.0)
                v.memset(self.ones8f[:, :], 1.0)
                v.memset(self.ones8c[:, :], 1.0)
                for dc in range(NDC):
                    v.memset(self.combo[:, dc * M_STAT + NE: dc * M_STAT + 32], 0.0)
                ii = None
                for dc in range(NDC):
                    ii = v.memset(self.combo[:, dc * M_STAT + 32: dc * M_STAT + 33], 1.0)
                self.inc("sdv", ii)
            mz = self.val("sdv")

            @self.A
            def _(a):
                a.wait_ge(self.sem("sx"), ld0)
                a.wait_ge(self.sem("sw"), ldw0)
                a.activation(out=trow[0:1, 2 * D:3 * D], in_=trow[0:1, 0:D], func=AF.Sin, scale=0.5)
                ii = a.activation(out=trow[0:1, 3 * D:4 * D], in_=trow[0:1, 0:D], func=AF.Sin, scale=0.25)
                self.inc("sac", ii)
            sin_done = self.val("sac")

            @self.V
            def _(v):
                v.wait_ge(self.sem("sac"), sin_done)
                v.wait_ge(self.sem("sdv"), mz)
                r_cq = trow[0:1, 0:D]
                r_sw = trow[0:1, D:2 * D]
                r_s2 = trow[0:1, 2 * D:3 * D]
                r_s4 = trow[0:1, 3 * D:4 * D]
                v.tensor_tensor(r_cq, r_s2, r_s2, OP.mult)
                v.tensor_scalar(r_cq, r_cq, -2.0, 1.0, OP.mult, OP.add)
                v.tensor_tensor(r_cq, r_cq, r_sw, OP.mult)
                v.tensor_scalar_mul(r_cq, r_cq, ISQ)
                v.tensor_tensor(r_sw, r_s4, r_s4, OP.mult)
                v.tensor_scalar(r_sw, r_sw, -2.0, 1.0, OP.mult, OP.add)
                v.tensor_tensor(r_s2, r_s2, r_sw, OP.mult)
                v.tensor_scalar_mul(trow[0:1, D:2 * D], r_s2, 2.0)
                v.tensor_tensor(trow[0:1, 2 * D:3 * D], trow[0:1, 4 * D:5 * D], r_cq, OP.mult)
                self.inc("sdv", v.tensor_tensor(trow[0:1, 3 * D:4 * D], trow[0:1, 5 * D:6 * D],
                                                trow[0:1, D:2 * D], OP.mult))
            trig_dv = self.val("sdv")

            @self.S
            def _(sy):
                self.qdrain(sy, 'sx')
                sy.wait_ge(self.sem("sdv"), trig_dv)
                self.inc("sx", sy.dma_start(out=self.row_dr.rearrange("a d -> (a d)").rearrange("(o n) -> o n", o=1),
                                            in_=trow[0:1, 0:4 * D]), 16)
            self.rows_stored = self.val("sx")

            @self.T
            def _(t):
                t.wait_ge(self.sem("sw"), ldw0)
                t.wait_ge(self.sem("sdv"), mz)
                ii = None
                for dc in range(NDC):
                    ii = t.matmul(gps[0:NE, 0:1], self.combo[:, dc * M_STAT: dc * M_STAT + NE],
                                  self.combo[:, dc * M_STAT + 32: dc * M_STAT + 33],
                                  start=(dc == 0), stop=(dc == NDC - 1))
                self.inc("smm", ii)
            gmm = self.val("smm")

            @self.V
            def _(v):
                v.wait_ge(self.sem("smm"), gmm)
                self.inc("sdv", v.tensor_copy(self.gsum[:, :], gps[0:NE, 0:1]))
            self.phase_snap = self.snap()

    def ln_stats(self, x_sb, xsq, ps_sum, ps_sq, rows, Rxsq):
        sq_done = []
        mm_after = []
        for dc in range(NDC):
            @self.V
            def _(v):
                if dc >= 2:
                    v.wait_ge(self.sem("smm"), mm_after[dc - 2])
                dst = xsq[:, (dc % 2) * TT:(dc % 2 + 1) * TT]
                ii = v.tensor_tensor(dst, x_sb[:, dc * TT:(dc + 1) * TT],
                                     x_sb[:, dc * TT:(dc + 1) * TT], OP.mult)
                sq_done.append(self.inc("sdv", ii))

            @self.T
            def _(t):
                lhs = self.combo[:, dc * M_STAT:(dc + 1) * M_STAT]
                for ts in range(2):
                    t.matmul(ps_sum[0:M_STAT, ts * 512:(ts + 1) * 512], lhs,
                             x_sb[:, dc * TT + ts * 512: dc * TT + (ts + 1) * 512],
                             start=(dc == 0), stop=(dc == NDC - 1))
                t.wait_ge(self.sem("sdv"), sq_done[dc])
                ii = None
                for ts in range(2):
                    ii = t.matmul(ps_sq[0:M_STAT, ts * 512:(ts + 1) * 512], lhs,
                                  xsq[:, (dc % 2) * TT + ts * 512:(dc % 2) * TT + (ts + 1) * 512],
                                  start=(dc == 0), stop=(dc == NDC - 1))
                mm_after.append(self.inc("smm", ii))
        stats_mm = self.val("smm")
        Rxsq.rd("smm", stats_mm)

        @self.V
        def _(v):
            v.wait_ge(self.sem("smm"), stats_mm)
            v.tensor_scalar_mul(rows["m"], ps_sum[32:33, :], 1.0 / D)
            v.tensor_scalar_mul(rows["v"], ps_sq[32:33, :], 1.0 / D)
            v.tensor_tensor(rows["a"], rows["m"], rows["m"], OP.mult)
            self.inc("sdv", v.tensor_sub(rows["v"], rows["v"], rows["a"]))
        vd = self.val("sdv")

        @self.A
        def _(a):
            a.wait_ge(self.sem("sdv"), vd)
            a.activation(out=rows["a"], in_=rows["v"], func=AF.Ln, bias=1e-5)
            self.inc("sac", a.activation(out=rows["a"], in_=rows["a"], func=AF.Exp, scale=-0.5))
        ad = self.val("sac")

        @self.V
        def _(v):
            v.wait_ge(self.sem("sac"), ad)
            self.inc("sdv", v.scalar_tensor_tensor(rows["b"], rows["m"], -1.0, rows["a"],
                                                   OP.mult, OP.mult))
        return self.val("sdv"), stats_mm

    def bcast2(self, row_a, row_b, Pa, Ra, Pb, Rb, out_a, out_b, rdy_sem, rdy_val):
        @self.T
        def _(t):
            t.wait_ge(self.sem(rdy_sem), rdy_val)
            Ra.wait_w(self, t)
            Rb.wait_w(self, t)
            ii = None
            for P0, row in ((Pa, row_a), (Pb, row_b)):
                for ts in range(2):
                    ii = t.matmul(P0[0:128, ts * 512:(ts + 1) * 512],
                                  self.onesf[:, :], row[0:1, ts * 512:(ts + 1) * 512],
                                  start=True, stop=True)
            self.inc("smm", ii)
        mmv = self.val("smm")

        @self.A
        def _(a):
            a.wait_ge(self.sem("smm"), mmv)
            a.activation(out=out_a, in_=Pa[:, 0:TT], func=AF.Copy)
            self.inc("sac", a.activation(out=out_b, in_=Pb[:, 0:TT], func=AF.Copy))
        Ra.rd("sac", self.val("sac"))
        Rb.rd("sac", self.val("sac"))
        return self.val("sac")

    def ffn_call(self, call_id):
        nc = self.nc
        self.phase_barrier()
        src = self.xT if call_id == 1 else self.x_res
        with ExitStack() as cctx:
            sb = lambda name, shape, dt: cctx.enter_context(
                nc.sbuf_tensor(f"{name}_{call_id}", shape, dt))
            bf = {}
            bf["x_sb"] = sb("x", [128, NDC * TT], F32)
            bf["xn_sb"] = sb("xn", [128, NDC * TT], BF16)
            bf["act_sb"] = sb("act", [128, NIC * TT], BF16)
            bf["w1b"] = sb("w1b", [128, 2 * NDC * 128], BF16)
            bf["w3b"] = sb("w3b", [128, 2 * NDC * 128], BF16)
            bf["w2b"] = sb("w2b", [128, 2 * NHC * 128], BF16)
            bf["gb_sb"] = sb("gb", [128, NE * TT], BF16)
            bf["ab_sb"] = sb("ab", [128, TT], F32)
            bf["bb_sb"] = sb("bb", [128, TT], F32)
            bf["xsq"] = sb("xsq", [128, 2 * TT], F32)
            rowst = sb("rows", [1, 4 * TT], F32)
            bf["ws1"] = sb("ws1", [128, TT], F32)
            bf["ws2"] = sb("ws2", [128, TT], F32)
            bf["g8"] = sb("g8", [NE, TT], BF16)
            bf["rows"] = {n: rowst[0:1, i * TT:(i + 1) * TT]
                          for i, n in enumerate(("m", "v", "a", "b"))}
            bf["growf"] = rowst[0:1, 0:2 * TT]

            @self.V
            def _(v):
                v.memset(bf["ws1"][:, :], 0.0)
                v.memset(bf["ws2"][:, :], 0.0)
                v.memset(bf["gb_sb"][:, :], 0.0)
                self.inc("sdv", v.memset(bf["g8"][:, :], 0.0))

            with ExitStack() as px:
                P = [px.enter_context(nc.psum_tensor(f"P{i}_{call_id}", [128, TT], F32))
                     for i in range(4)]
                bf["P"] = P
                bf["R"] = dict(P=[Res() for _ in range(4)],
                               w1b=[Res(), Res()], w3b=[Res(), Res()], w2b=[Res(), Res()],
                               x=Res(), xn=Res(), act=Res(), ws=Res(), xsq=Res(),
                               g8=Res(), gdr=Res())
                for tt in range(self.lim.get("ntt", NTT)):
                    self._ffn_tile(call_id, tt, src, bf)
            self.phase_snap = self.snap()

    def _ffn_tile(self, call_id, tt, src, bf):
        t0 = tt * TT
        P, R = bf["P"], bf["R"]
        x_sb, xn_sb = bf["x_sb"], bf["xn_sb"]
        xsq, rows = bf["xsq"], bf["rows"]

        @self.S
        def _(sy):
            self.qdrain(sy, 'sx')
            if tt == 0:
                self.wait_snap(sy, self.phase_snap)
            R["x"].wait_w(self, sy)
            for dc in range(NDC):
                self.inc("sx", sy.dma_start(out=x_sb[:, dc * TT:(dc + 1) * TT],
                                            in_=src[dc * 128:(dc + 1) * 128, t0:t0 + TT]), 16)
        x_ld = self.val("sx")

        @self.V
        def _(v):
            v.wait_ge(self.sem("sx"), x_ld)
            R["xsq"].wait_w(self, v)

        @self.T
        def _(t):
            t.wait_ge(self.sem("sx"), x_ld)
            R["P"][2].wait_w(self, t)
            R["P"][3].wait_w(self, t)

        ab_dv, stats_mm = self.ln_stats(x_sb, xsq, P[2], P[3], rows, R["xsq"])

        ab_ready = self.bcast2(rows["a"], rows["b"], P[0], R["P"][0], P[1], R["P"][1],
                               bf["ab_sb"][:, :], bf["bb_sb"][:, :], "sdv", ab_dv)

        if self.lim.get("router", True):
            self._router(tt, bf, ab_ready)
        else:
            self.gb_ready = self.val("sac")
            self.grow_read_mm = self.val("smm")

        @self.V
        def _(v):
            v.wait_ge(self.sem("sac"), ab_ready)
            R["xn"].wait_w(self, v)
            R["xsq"].wait_w(self, v)
            ii = None
            for dc in range(NDC):
                v.tensor_tensor(xsq[:, 0:TT], x_sb[:, dc * TT:(dc + 1) * TT],
                                bf["ab_sb"][:, :], OP.mult)
                ii = v.tensor_tensor(xn_sb[:, dc * TT:(dc + 1) * TT], xsq[:, 0:TT],
                                     bf["bb_sb"][:, :], OP.add)
            self.inc("sdv", ii)
        self.xn_ready = self.val("sdv")

        self.x_acc_dv = self.xn_ready
        for e in range(self.lim.get("ne", NE)):
            self._expert(e, bf)

        for half in range(2 if self.lim.get("shared", True) else 0):
            self._shared(half, bf)

        if self.lim.get("epi", True):
            self._tile_epilogue(call_id, tt, bf, t0)
        else:
            @self.S
            def _(sy):
                self.qdrain(sy, 'sx')
                sy.wait_ge(self.sem("sdv"), self.x_acc_dv)
                for dc in range(NDC):
                    self.inc("sx", sy.dma_start(
                        out=self.x_res[dc * 128:(dc + 1) * 128, t0:t0 + TT],
                        in_=bf["x_sb"][:, dc * TT:(dc + 1) * TT]), 16)
                if "dbg_xres" in self.dbg_t:
                    for dc in range(NDC):
                        self.inc("sx", sy.dma_start(
                            out=self.dbg_t["dbg_xres"][dc * 128:(dc + 1) * 128, t0:t0 + TT],
                            in_=bf["x_sb"][:, dc * TT:(dc + 1) * TT]), 16)
            bf["R"]["x"].rd("sx", self.val("sx"))

    def _router(self, tt, bf, ab_ready):
        P, R = bf["P"], bf["R"]
        ws1, ws2, g8, xsq = bf["ws1"], bf["ws2"], bf["g8"], bf["xsq"]
        gsb = bass.AP(self.gsum, 0, [[1, NE], [0, TT]])
        rbb = bass.AP(self.rb8, 0, [[1, NE], [0, TT]])

        @self.V
        def _(v):
            v.wait_ge(self.sem("sac"), ab_ready)
            R["ws"].wait_w(self, v)
            v.tensor_tensor(ws1[0:NE, :], P[2][0:NE, :], bf["ab_sb"][0:NE, :], OP.mult)
            v.tensor_tensor(ws2[0:NE, :], bf["bb_sb"][0:NE, :], gsb, OP.mult)
            self.inc("sdv", v.tensor_add(ws1[0:NE, :], ws1[0:NE, :], ws2[0:NE, :]))
        R["P"][2].rd("sdv", self.val("sdv"))
        logit_dv = self.val("sdv")

        @self.A
        def _(a):
            a.wait_ge(self.sem("sdv"), logit_dv)
            self.inc("sac", a.activation(out=ws2[0:NE, :], in_=ws1[0:NE, :], func=AF.Exp, scale=-1.0))
        ed = self.val("sac")

        @self.V
        def _(v):
            v.wait_ge(self.sem("sac"), ed)
            v.tensor_scalar_add(ws2[0:NE, :], ws2[0:NE, :], 1.0)
            v.reciprocal(ws2[0:NE, :], ws2[0:NE, :])
            v.tensor_tensor(ws1[32:32 + NE, :], ws2[0:NE, :], rbb, OP.add)
            self.inc("sdv", v.tensor_copy(ws1[0:NE, :], ws2[0:NE, :]))
        sel_dv = self.val("sdv")
        if self.lim.get("rstage", 9) <= 1:
            self.gb_ready = self.val("sac"); self.grow_read_mm = self.val("smm"); return

        rows = bf["rows"]
        sco = ws1[0:NE, :]
        xscr = xsq
        if self.lim.get("rstage", 9) <= 2:
            self.gb_ready = self.val("sac"); self.grow_read_mm = self.val("smm"); return

        def ptree_max(first_wait_dv, src_base, out_row):
            @self.S
            def _(sy):
                self.qdrain(sy, 'sx')
                sy.wait_ge(self.sem("sdv"), first_wait_dv)
                self.inc("sx", sy.dma_start(out=xscr[0:4, 0:TT], in_=ws1[src_base + 4:src_base + 8, :]), 16)
                self.inc("sx", sy.dma_start(out=ws2[0:4, :], in_=ws1[src_base:src_base + 4, :]), 16)
            d1 = self.val("sx")

            @self.V
            def _(v):
                v.wait_ge(self.sem("sx"), d1)
                self.inc("sdv", v.tensor_max(ws2[0:4, :], ws2[0:4, :], xscr[0:4, 0:TT]))
            v1 = self.val("sdv")

            @self.S
            def _(sy):
                self.qdrain(sy, 'sx')
                sy.wait_ge(self.sem("sdv"), v1)
                self.inc("sx", sy.dma_start(out=xscr[0:2, 0:TT], in_=ws2[2:4, :]), 16)
            d2 = self.val("sx")

            @self.V
            def _(v):
                v.wait_ge(self.sem("sx"), d2)
                self.inc("sdv", v.tensor_max(ws2[0:2, :], ws2[0:2, :], xscr[0:2, 0:TT]))
            v2 = self.val("sdv")

            @self.S
            def _(sy):
                self.qdrain(sy, 'sx')
                sy.wait_ge(self.sem("sdv"), v2)
                self.inc("sx", sy.dma_start(out=xscr[0:1, 0:TT], in_=ws2[1:2, :]), 16)
            d3 = self.val("sx")

            @self.V
            def _(v):
                v.wait_ge(self.sem("sx"), d3)
                self.inc("sdv", v.tensor_max(out_row, ws2[0:1, :], xscr[0:1, 0:TT]))
            return self.val("sdv")

        def bcast8(row_ap, rdy_sem, rdy, out8):
            @self.T
            def _(t):
                t.wait_ge(self.sem(rdy_sem), rdy)
                R["P"][1].wait_w(self, t)
                ii = None
                for ts in range(2):
                    ii = t.matmul(P[1][0:NE, ts * 512:(ts + 1) * 512], self.ones8f[0:1, :],
                                  row_ap[0:1, ts * 512:(ts + 1) * 512], start=True, stop=True)
                self.inc("smm", ii)
            mv = self.val("smm")

            @self.A
            def _(a):
                a.wait_ge(self.sem("smm"), mv)
                self.inc("sac", a.activation(out=out8, in_=P[1][0:NE, :], func=AF.Copy))
            ev = self.val("sac")
            R["P"][1].rd("sac", ev)
            return ev

        def psum8(in8_ap, rdy, out_row):
            @self.T
            def _(t):
                t.wait_ge(self.sem("sdv"), rdy)
                R["P"][1].wait_w(self, t)
                ii = None
                for ts in range(2):
                    ii = t.matmul(P[1][0:1, ts * 512:(ts + 1) * 512], self.ones8c[:, 0:1],
                                  in8_ap[:, ts * 512:(ts + 1) * 512], start=True, stop=True)
                self.inc("smm", ii)
            mv = self.val("smm")

            @self.A
            def _(a):
                a.wait_ge(self.sem("smm"), mv)
                self.inc("sac", a.activation(out=out_row, in_=P[1][0:1, :], func=AF.Copy))
            ev = self.val("sac")
            R["P"][1].rd("sac", ev)
            return ev

        m1_dv = ptree_max(sel_dv, 32, rows["b"])
        m1b_ev = bcast8(rows["b"], "sdv", m1_dv, ws2[32:32 + NE, :])

        @self.V
        def _(v):
            v.wait_ge(self.sem("sac"), m1b_ev)
            v.tensor_sub(ws1[64:64 + NE, :], ws1[32:32 + NE, :], ws2[32:32 + NE, :])
            v.tensor_single_scalar(ws1[64:64 + NE, :], ws1[64:64 + NE, :], 0.0, OP.is_ge)
            v.tensor_copy(ws2[0:NE, :], ws1[64:64 + NE, :])
            v.tensor_copy(xscr[32:32 + NE, 0:TT], ws1[64:64 + NE, :])
            self.inc("sdv", v.tensor_mul(xscr[0:NE, 0:TT], ws2[0:NE, :], sco))
        mask1_dv = self.val("sdv")
        s1_ev = psum8(xscr[0:NE, 0:TT], mask1_dv, rows["m"])

        @self.V
        def _(v):
            v.wait_ge(self.sem("sac"), s1_ev)
            self.inc("sdv", v.scalar_tensor_tensor(ws1[32:32 + NE, :], xscr[32:32 + NE, 0:TT],
                                                   -1e9, ws1[32:32 + NE, :], OP.mult, OP.add))
        sel2_dv = self.val("sdv")
        m2_dv = ptree_max(sel2_dv, 32, rows["b"])
        m2b_ev = bcast8(rows["b"], "sdv", m2_dv, ws2[32:32 + NE, :])

        @self.V
        def _(v):
            v.wait_ge(self.sem("sac"), m2b_ev)
            v.tensor_sub(ws1[32:32 + NE, :], ws1[32:32 + NE, :], ws2[32:32 + NE, :])
            v.tensor_single_scalar(ws1[32:32 + NE, :], ws1[32:32 + NE, :], 0.0, OP.is_ge)
            v.tensor_copy(xscr[32:32 + NE, 0:TT], sco)
            self.inc("sdv", v.tensor_mul(xscr[0:NE, 0:TT], ws1[32:32 + NE, :],
                                         xscr[32:32 + NE, 0:TT]))
        mask2_dv = self.val("sdv")
        s2_ev = psum8(xscr[0:NE, 0:TT], mask2_dv, rows["v"])

        @self.V
        def _(v):
            v.wait_ge(self.sem("sac"), s2_ev)
            v.tensor_add(rows["a"], rows["m"], rows["v"])
            v.tensor_scalar_max(rows["a"], rows["a"], 1e-12)
            v.reciprocal(rows["a"], rows["a"])
            v.tensor_scalar_mul(rows["a"], rows["a"], 0.5)
            v.tensor_mul(rows["b"], rows["m"], rows["a"])
            self.inc("sdv", v.tensor_mul(rows["a"], rows["v"], rows["a"]))
        w_dv = self.val("sdv")
        w1b_ev = bcast8(rows["b"], "sdv", w_dv, ws2[64:64 + NE, :])

        @self.V
        def _(v):
            v.wait_ge(self.sem("sac"), w1b_ev)
            self.inc("sdv", v.tensor_mul(ws2[0:NE, :], ws1[64:64 + NE, :],
                                         ws2[64:64 + NE, :]))
        g1_dv = self.val("sdv")
        w2b_ev = bcast8(rows["a"], "sdv", g1_dv, ws2[32:32 + NE, :])

        @self.V
        def _(v):
            v.wait_ge(self.sem("sac"), w2b_ev)
            R["g8"].wait_w(self, v)
            v.tensor_mul(xscr[0:NE, 0:TT], ws1[32:32 + NE, :], ws2[32:32 + NE, :])
            self.inc("sdv", v.tensor_add(g8[0:NE, :], ws2[0:NE, :], xscr[0:NE, 0:TT]))
        g8_dv = self.val("sdv")
        R["ws"].rd("sdv", g8_dv)
        R["xsq"].rd("sdv", g8_dv)

        @self.S
        def _(sy):
            self.qdrain(sy, 'sx')
            sy.wait_ge(self.sem("sdv"), g8_dv)
            R["gdr"].wait_w(self, sy)
            self.inc("sx", sy.dma_start(out=self.g_dr, in_=g8[:, :]), 16)
        grow_ld = self.val("sx")
        R["g8"].rd("sx", grow_ld)
        if self.lim.get("rstage", 9) <= 5:
            self.gb_ready = self.val("sac"); self.grow_read_mm = self.val("smm"); return

        grow_cp = {}
        growf = bf["growf"]
        cpv = []
        for e in range(NE):
            Pi = e % 2
            rowf = growf[0:1, (e % 2) * TT:(e % 2 + 1) * TT]

            @self.G
            def _(g):
                qn = self.gq_begin(g)
                if e == 0:
                    g.wait_ge(self.sem("sx"), grow_ld)
                if e >= 2:
                    g.wait_ge(self.sem("smm"), grow_cp[e - 2])
                self.inc(qn, g.dma_start(out=rowf, in_=self.g_dr[e:e + 1, :]), 16)
                cpv.append((qn, self.val(qn)))

            @self.T
            def _(t):
                t.wait_ge(self.sem(cpv[-1][0]), cpv[-1][1])
                R["P"][Pi].wait_w(self, t)
                ii = None
                for ts in range(2):
                    ii = t.matmul(P[Pi][0:128, ts * 512:(ts + 1) * 512], self.onesf[:, :],
                                  rowf[0:1, ts * 512:(ts + 1) * 512],
                                  start=True, stop=True)
                self.inc("smm", ii)
            grow_cp[e] = self.val("smm")
            mmv = self.val("smm")

            @self.A
            def _(a):
                a.wait_ge(self.sem("smm"), mmv)
                self.inc("sac", a.activation(out=bf["gb_sb"][:, e * TT:(e + 1) * TT],
                                             in_=P[Pi][:, 0:TT], func=AF.Copy))
            R["P"][Pi].rd("sac", self.val("sac"))
        R["gdr"].rd("sw", self.val("sw"))
        self.gb_ready = self.val("sac")
        self.grow_read_mm = self.val("smm")
        if self.lim.get("dump") and not hasattr(self, "_dumped"):
            self._dumped = True
            dmp_g8 = self.nc.dram_tensor("dmp_g8", [NE, TT], BF16, kind="ExternalOutput").ap()
            dmp_ws1 = self.nc.dram_tensor("dmp_ws1", [128, TT], F32, kind="ExternalOutput").ap()
            dmp_ws2 = self.nc.dram_tensor("dmp_ws2", [128, TT], F32, kind="ExternalOutput").ap()
            dmp_gb = self.nc.dram_tensor("dmp_gb", [128, 2 * TT], BF16, kind="ExternalOutput").ap()
            dmp_gf = self.nc.dram_tensor("dmp_gf", [1, 2 * TT], F32, kind="ExternalOutput").ap()
            dmp_wsm = self.nc.dram_tensor("dmp_wsm", [128, 16], F32, kind="ExternalOutput").ap()
            gbv = self.gb_ready

            @self.S
            def _(sy):
                self.qdrain(sy, 'sx')
                sy.wait_ge(self.sem("sac"), gbv)
                sy.wait_ge(self.sem("sw"), self.val("sw"))
                self.inc("sx", sy.dma_start(out=dmp_g8, in_=g8[:, :]), 16)
                self.inc("sx", sy.dma_start(out=dmp_ws1, in_=ws1[:, :]), 16)
                self.inc("sx", sy.dma_start(out=dmp_ws2, in_=ws2[:, :]), 16)
                self.inc("sx", sy.dma_start(out=dmp_gb, in_=bf["gb_sb"][:, 0:2 * TT]), 16)
                self.inc("sx", sy.dma_start(out=dmp_gf, in_=growf[0:1, :]), 16)
                self.inc("sx", sy.dma_start(out=dmp_wsm, in_=wsm[:, :]), 16)

    def _expert(self, e, bf):
        P, R = bf["P"], bf["R"]
        act_sb, xn_sb, x_sb = bf["act_sb"], bf["xn_sb"], bf["x_sb"]
        w1b, w3b, w2b = bf["w1b"], bf["w3b"], bf["w2b"]

        w_ready = {}

        def load_w13(ic):
            par = ic % 2
            @self.G
            def _(g):
                qn = self.gq_begin(g)
                R["w1b"][par].wait_w(self, g)
                R["w3b"][par].wait_w(self, g)
                self.inc(qn, g.dma_start(out=w1b[:, par * 2048:(par + 1) * 2048],
                                         in_=self.w1_p[e, ic]), 16)
                self.inc(qn, g.dma_start(out=w3b[:, par * 2048:(par + 1) * 2048],
                                         in_=self.w3_p[e, ic]), 16)
                w_ready[ic] = (qn, self.val(qn))

        load_w13(0)
        load_w13(1)

        act_evac = {}
        for ic in range(NIC):
            par = ic % 2
            @self.T
            def _(t):
                t.wait_ge(self.sem(w_ready[ic][0]), w_ready[ic][1])
                if ic == 0:
                    t.wait_ge(self.sem("sdv"), self.xn_ready)
                R["P"][par].wait_w(self, t)
                R["P"][2 + par].wait_w(self, t)
                for dc in range(NDC):
                    lhs1 = w1b[:, par * 2048 + dc * 128: par * 2048 + (dc + 1) * 128]
                    for ts in range(2):
                        t.matmul(P[par][:, ts * 512:(ts + 1) * 512], lhs1,
                                 xn_sb[:, dc * TT + ts * 512: dc * TT + (ts + 1) * 512],
                                 start=(dc == 0), stop=(dc == NDC - 1))
                ii = None
                for dc in range(NDC):
                    lhs3 = w3b[:, par * 2048 + dc * 128: par * 2048 + (dc + 1) * 128]
                    for ts in range(2):
                        ii = t.matmul(P[2 + par][:, ts * 512:(ts + 1) * 512], lhs3,
                                      xn_sb[:, dc * TT + ts * 512: dc * TT + (ts + 1) * 512],
                                      start=(dc == 0), stop=(dc == NDC - 1))
                self.inc("smm", ii)
            h_mm = self.val("smm")
            R["w1b"][par].rd("smm", h_mm)
            R["w3b"][par].rd("smm", h_mm)

            if ic + 2 < NIC:
                load_w13(ic + 2)

            @self.A
            def _(a):
                a.wait_ge(self.sem("smm"), h_mm)
                if ic == 0:
                    a.wait_ge(self.sem("sac"), self.gb_ready)
                R["act"].wait_w(self, a)
                self.inc("sac", a.activation(out=act_sb[:, ic * TT:(ic + 1) * TT],
                                             in_=P[par][:, :], func=AF.Silu))
            sl = self.val("sac")
            R["P"][par].rd("sac", sl)

            @self.V
            def _(v):
                v.wait_ge(self.sem("sac"), sl)
                v.tensor_tensor(act_sb[:, ic * TT:(ic + 1) * TT],
                                act_sb[:, ic * TT:(ic + 1) * TT], P[2 + par][:, :], OP.mult)
                self.inc("sdv", v.tensor_tensor(act_sb[:, ic * TT:(ic + 1) * TT],
                                                act_sb[:, ic * TT:(ic + 1) * TT],
                                                bf["gb_sb"][:, e * TT:(e + 1) * TT], OP.mult))
            act_evac[ic] = self.val("sdv")
            R["P"][2 + par].rd("sdv", act_evac[ic])

        w2_ready = {}

        def load_w2(do):
            par = do % 2
            @self.G
            def _(g):
                qn = self.gq_begin(g)
                R["w2b"][par].wait_w(self, g)
                self.inc(qn, g.dma_start(
                    out=w2b[:, par * NHC * 128: par * NHC * 128 + NIC * 128],
                    in_=self.w2_p[e, do]), 16)
                w2_ready[do] = (qn, self.val(qn))

        load_w2(0)
        load_w2(1)

        for do in range(NDC):
            par = do % 2
            @self.T
            def _(t):
                t.wait_ge(self.sem(w2_ready[do][0]), w2_ready[do][1])
                if do == 0:
                    t.wait_ge(self.sem("sdv"), act_evac[NIC - 1])
                R["P"][par].wait_w(self, t)
                ii = None
                for ic in range(NIC):
                    lhs = w2b[:, par * NHC * 128 + ic * 128: par * NHC * 128 + (ic + 1) * 128]
                    for ts in range(2):
                        ii = t.matmul(P[par][:, ts * 512:(ts + 1) * 512], lhs,
                                      act_sb[:, ic * TT + ts * 512: ic * TT + (ts + 1) * 512],
                                      start=(ic == 0), stop=(ic == NIC - 1))
                self.inc("smm", ii)
            y_mm = self.val("smm")
            R["w2b"][par].rd("smm", y_mm)
            if do == NDC - 1:
                R["act"].rd("smm", y_mm)

            if do + 2 < NDC:
                load_w2(do + 2)

            @self.V
            def _(v):
                v.wait_ge(self.sem("smm"), y_mm)
                self.inc("sdv", v.tensor_tensor(x_sb[:, do * TT:(do + 1) * TT],
                                                x_sb[:, do * TT:(do + 1) * TT],
                                                P[par][:, :], OP.add))
            R["P"][par].rd("sdv", self.val("sdv"))
        self.x_acc_dv = self.val("sdv")

    def _shared(self, half, bf):
        P, R = bf["P"], bf["R"]
        act_sb, xn_sb, x_sb = bf["act_sb"], bf["xn_sb"], bf["x_sb"]
        w1b, w3b, w2b = bf["w1b"], bf["w3b"], bf["w2b"]
        hofs = half * 512

        w_ready = {}

        def load_sw13(hc):
            par = hc % 2
            @self.G
            def _(g):
                qn = self.gq_begin(g)
                R["w1b"][par].wait_w(self, g)
                R["w3b"][par].wait_w(self, g)
                self.inc(qn, g.dma_start(out=w1b[:, par * 2048:(par + 1) * 2048],
                                         in_=self.sw1_p[hc]), 16)
                self.inc(qn, g.dma_start(out=w3b[:, par * 2048:(par + 1) * 2048],
                                         in_=self.sw3_p[hc]), 16)
                w_ready[hc] = (qn, self.val(qn))

        load_sw13(0)
        load_sw13(1)

        act_evac = {}
        for hc in range(NHC):
            par = hc % 2
            @self.T
            def _(t):
                t.wait_ge(self.sem(w_ready[hc][0]), w_ready[hc][1])
                R["P"][par].wait_w(self, t)
                R["P"][2 + par].wait_w(self, t)
                for dc in range(NDC):
                    t.matmul(P[par][:, 0:512],
                             w1b[:, par * 2048 + dc * 128: par * 2048 + (dc + 1) * 128],
                             xn_sb[:, dc * TT + hofs: dc * TT + hofs + 512],
                             start=(dc == 0), stop=(dc == NDC - 1))
                ii = None
                for dc in range(NDC):
                    ii = t.matmul(P[2 + par][:, 0:512],
                                  w3b[:, par * 2048 + dc * 128: par * 2048 + (dc + 1) * 128],
                                  xn_sb[:, dc * TT + hofs: dc * TT + hofs + 512],
                                  start=(dc == 0), stop=(dc == NDC - 1))
                self.inc("smm", ii)
            h_mm = self.val("smm")
            R["w1b"][par].rd("smm", h_mm)
            R["w3b"][par].rd("smm", h_mm)

            if hc + 2 < NHC:
                load_sw13(hc + 2)

            @self.A
            def _(a):
                a.wait_ge(self.sem("smm"), h_mm)
                R["act"].wait_w(self, a)
                self.inc("sac", a.activation(out=act_sb[:, hc * 512:(hc + 1) * 512],
                                             in_=P[par][:, 0:512], func=AF.Silu))
            sl = self.val("sac")
            R["P"][par].rd("sac", sl)

            @self.V
            def _(v):
                v.wait_ge(self.sem("sac"), sl)
                self.inc("sdv", v.scalar_tensor_tensor(
                    act_sb[:, hc * 512:(hc + 1) * 512],
                    act_sb[:, hc * 512:(hc + 1) * 512], 0.5,
                    P[2 + par][:, 0:512], OP.mult, OP.mult))
            act_evac[hc] = self.val("sdv")
            R["P"][2 + par].rd("sdv", act_evac[hc])

        w2_ready = {}

        def load_sw2(do):
            par = do % 2
            @self.G
            def _(g):
                qn = self.gq_begin(g)
                R["w2b"][par].wait_w(self, g)
                self.inc(qn, g.dma_start(out=w2b[:, par * NHC * 128:(par + 1) * NHC * 128],
                                         in_=self.sw2_p[do]), 16)
                w2_ready[do] = (qn, self.val(qn))

        load_sw2(0)
        load_sw2(1)

        for do in range(NDC):
            par = do % 2
            @self.T
            def _(t):
                t.wait_ge(self.sem(w2_ready[do][0]), w2_ready[do][1])
                if do == 0:
                    t.wait_ge(self.sem("sdv"), act_evac[NHC - 1])
                R["P"][par].wait_w(self, t)
                ii = None
                for hc in range(NHC):
                    ii = t.matmul(P[par][:, 0:512],
                                  w2b[:, par * NHC * 128 + hc * 128: par * NHC * 128 + (hc + 1) * 128],
                                  act_sb[:, hc * 512:(hc + 1) * 512],
                                  start=(hc == 0), stop=(hc == NHC - 1))
                self.inc("smm", ii)
            y_mm = self.val("smm")
            R["w2b"][par].rd("smm", y_mm)
            if do == NDC - 1:
                R["act"].rd("smm", y_mm)

            if do + 2 < NDC:
                load_sw2(do + 2)

            @self.V
            def _(v):
                v.wait_ge(self.sem("smm"), y_mm)
                self.inc("sdv", v.tensor_tensor(
                    x_sb[:, do * TT + hofs: do * TT + hofs + 512],
                    x_sb[:, do * TT + hofs: do * TT + hofs + 512],
                    P[par][:, 0:512], OP.add))
            R["P"][par].rd("sdv", self.val("sdv"))
        self.x_acc_dv = self.val("sdv")

    def _tile_epilogue(self, call_id, tt, bf, t0):
        P, R = bf["P"], bf["R"]
        x_sb, xn_sb, xsq, rows = bf["x_sb"], bf["xn_sb"], bf["xsq"], bf["rows"]
        x_done = self.x_acc_dv

        if call_id == 1:
            @self.V
            def _(v):
                v.wait_ge(self.sem("sdv"), x_done)
                v.wait_ge(self.sem("smm"), self.grow_read_mm)
                R["xsq"].wait_w(self, v)

            @self.T
            def _(t):
                t.wait_ge(self.sem("sdv"), x_done)
                R["P"][2].wait_w(self, t)
                R["P"][3].wait_w(self, t)
            ab_dv, _sm = self.ln_stats(x_sb, xsq, P[2], P[3], rows, R["xsq"])
            R["P"][2].rd("sdv", ab_dv)
            R["P"][3].rd("sdv", ab_dv)
            ab2 = self.bcast2(rows["a"], rows["b"], P[0], R["P"][0], P[1], R["P"][1],
                              bf["ab_sb"][:, :], bf["bb_sb"][:, :], "sdv", ab_dv)

            @self.V
            def _(v):
                v.wait_ge(self.sem("sac"), ab2)
                R["xn"].wait_w(self, v)
                R["xsq"].wait_w(self, v)
                ii = None
                for dc in range(NDC):
                    v.tensor_tensor(xsq[:, 0:TT], x_sb[:, dc * TT:(dc + 1) * TT],
                                    bf["ab_sb"][:, :], OP.mult)
                    ii = v.tensor_tensor(xn_sb[:, dc * TT:(dc + 1) * TT], xsq[:, 0:TT],
                                         bf["bb_sb"][:, :], OP.add)
                self.inc("sdv", ii)
            xn2 = self.val("sdv")

            @self.S
            def _(sy):
                self.qdrain(sy, 'sx')
                sy.wait_ge(self.sem("sdv"), xn2)
                for dc in range(NDC):
                    self.inc("sx", sy.dma_start(
                        out=self.xnA[dc * 128:(dc + 1) * 128, t0:t0 + TT],
                        in_=xn_sb[:, dc * TT:(dc + 1) * TT]), 16)
                    self.inc("sx", sy.dma_start(
                        out=self.x_res[dc * 128:(dc + 1) * 128, t0:t0 + TT],
                        in_=x_sb[:, dc * TT:(dc + 1) * TT]), 16)
                for nm, sbuf in (("dbg_xres", x_sb), ("dbg_xnA", xn_sb)):
                    if nm in self.dbg_t:
                        for dc in range(NDC):
                            self.inc("sx", sy.dma_start(
                                out=self.dbg_t[nm][dc * 128:(dc + 1) * 128, t0:t0 + TT],
                                in_=sbuf[:, dc * TT:(dc + 1) * TT]), 16)
            st = self.val("sx")
            R["x"].rd("sx", st)
            R["xn"].rd("sx", st)
        else:
            @self.S
            def _(sy):
                self.qdrain(sy, 'sx')
                sy.wait_ge(self.sem("sdv"), x_done)
                for dc in range(NDC):
                    self.inc("sx", sy.dma_start(
                        out=self.outT[dc * 128:(dc + 1) * 128, t0:t0 + TT],
                        in_=x_sb[:, dc * TT:(dc + 1) * TT]), 16)
            R["x"].rd("sx", self.val("sx"))

    def qkv_phase(self):
        nc = self.nc
        self.phase_barrier()
        barrier = self.phase_snap
        with ExitStack() as cctx:
            sb = lambda name, shape, dt: cctx.enter_context(nc.sbuf_tensor(name, shape, dt))
            xn_sb = sb("xnq", [128, NDC * TLOC], BF16)
            wbuf = sb("wqk", [128, 4 * 2048], BF16)
            wvb = sb("wvb", [128, NDC * 2048], BF16)
            cqb = sb("cqb", [128, D], F32)
            ckb = sb("ckb", [128, D], F32)
            bvb = sb("bvb", [128, D], F32)
            rws = sb("rws", [1, 3 * D], F32)
            stg = sb("qstg", [128, 2 * TLOC], BF16)

            with ExitStack() as px:
                PP = [px.enter_context(nc.psum_tensor(f"QP{i}", [128, TLOC], F32))
                      for i in range(2)]
                Rp = [Res(), Res()]
                Rw = [Res() for _ in range(4)]
                Rstg = [Res(), Res()]

                @self.S
                def _(sy):
                    self.qdrain(sy, 'sx')
                    self.wait_snap(sy, barrier)
                    sy.wait_ge(self.sem("sx"), self.rows_stored)
                    for dc in range(NDC):
                        self.inc("sx", sy.dma_start(out=xn_sb[:, dc * TLOC:(dc + 1) * TLOC],
                                                    in_=self.xnA[dc * 128:(dc + 1) * 128, :]), 16)
                    self.inc("sx", sy.dma_start(out=rws[0:1, 0:2 * D],
                                                in_=self.row_dr[0:2, :].rearrange("a d -> (a d)").rearrange("(o n) -> o n", o=1)), 16)
                    self.inc("sx", sy.dma_start(out=rws[0:1, 2 * D:3 * D], in_=self.bv_r), 16)
                    for bi in range(2):
                        self.inc("sx", sy.dma_start(
                            out=self.bias_t[:, bi * NDC:(bi + 1) * NDC],
                            in_=bass.AP(self.row_dr.tensor, (2 + bi) * D, [[1, 128], [128, NDC]]),
                            allow_slow_non_contiguous=True), 16)
                xn_ld = self.val("sx")

                @self.T
                def _(t):
                    self.wait_snap(t, barrier)
                    t.wait_ge(self.sem("sx"), xn_ld)
                    ii = None
                    for half in range(2):
                        for ts in range(D // 512):
                            ii = t.matmul(PP[half][:, ts * 512:(ts + 1) * 512],
                                          self.onesf[:, :],
                                          rws[0:1, half * D + ts * 512: half * D + (ts + 1) * 512],
                                          start=True, stop=True)
                    self.inc("smm", ii)
                cmm = self.val("smm")

                @self.A
                def _(a):
                    a.wait_ge(self.sem("smm"), cmm)
                    a.activation(out=cqb[:, :], in_=PP[0][:, 0:D], func=AF.Copy)
                    self.inc("sac", a.activation(out=ckb[:, :], in_=PP[1][:, 0:D], func=AF.Copy))
                Rp[0].rd("sac", self.val("sac"))
                Rp[1].rd("sac", self.val("sac"))

                @self.T
                def _(t):
                    Rp[0].wait_w(self, t)
                    ii = None
                    for ts in range(D // 512):
                        ii = t.matmul(PP[0][:, ts * 512:(ts + 1) * 512], self.onesf[:, :],
                                      rws[0:1, 2 * D + ts * 512: 2 * D + (ts + 1) * 512],
                                      start=True, stop=True)
                    self.inc("smm", ii)
                bvm = self.val("smm")

                @self.A
                def _(a):
                    a.wait_ge(self.sem("smm"), bvm)
                    self.inc("sac", a.activation(out=bvb[:, :], in_=PP[0][:, 0:D], func=AF.Copy))
                Rp[0].rd("sac", self.val("sac"))
                cb_done = self.val("sac")

                @self.G
                def _(g):
                    self.qdrain(g, 'sw')
                    for dc in range(NDC):
                        self.inc("sw", g.dma_start(out=wvb[:, dc * 2048:(dc + 1) * 2048],
                                                   in_=self.wv_p[dc]), 16)
                wv_ld = self.val("sw")

                w_ready = {}

                def load_w(pi, jc):
                    slot = pi * 2 + jc % 2
                    @self.G
                    def _(g):
                        qn = self.gq_begin(g)
                        Rw[slot].wait_w(self, g)
                        src = self.wq_p if pi == 0 else self.wk_p
                        self.inc(qn, g.dma_start(out=wbuf[:, slot * 2048:(slot + 1) * 2048],
                                                 in_=src[jc]), 16)
                        w_ready[(pi, jc)] = (qn, self.val(qn), slot)

                for pi in range(2):
                    load_w(pi, 0)
                    load_w(pi, 1)

                for pi in range(2):
                    scale = cqb if pi == 0 else ckb
                    bcol = pi
                    dst = self.q_dr if pi == 0 else self.kc_in.ap()
                    dbgn = "dbg_q" if pi == 0 else "dbg_k"
                    for jc in range(NDC):
                        wqn, wv, slot = w_ready[(pi, jc)]

                        @self.V
                        def _(v):
                            v.wait_ge(self.sem(wqn), wv)
                            v.wait_ge(self.sem("sac"), cb_done)
                            ii = None
                            for dc in range(NDC):
                                ii = v.tensor_tensor(
                                    wbuf[:, slot * 2048 + dc * 128: slot * 2048 + (dc + 1) * 128],
                                    wbuf[:, slot * 2048 + dc * 128: slot * 2048 + (dc + 1) * 128],
                                    scale[:, jc * 128:(jc + 1) * 128], OP.mult)
                            self.inc("sdv", ii)
                        wsc = self.val("sdv")

                        spar = jc % 2
                        @self.T
                        def _(t):
                            t.wait_ge(self.sem("sdv"), wsc)
                            if pi == 0 and jc == 0:
                                t.wait_ge(self.sem("sx"), xn_ld)
                            Rp[spar].wait_w(self, t)
                            ii = None
                            for dc in range(NDC):
                                lhs = wbuf[:, slot * 2048 + dc * 128: slot * 2048 + (dc + 1) * 128]
                                for ts in range(TLOC // 512):
                                    ii = t.matmul(PP[spar][:, ts * 512:(ts + 1) * 512], lhs,
                                                  xn_sb[:, dc * TLOC + ts * 512: dc * TLOC + (ts + 1) * 512],
                                                  start=(dc == 0), stop=(dc == NDC - 1))
                            self.inc("smm", ii)
                        pm = self.val("smm")
                        Rw[slot].rd("smm", pm)

                        if jc + 2 < NDC:
                            load_w(pi, jc + 2)

                        @self.A
                        def _(a):
                            a.wait_ge(self.sem("smm"), pm)
                            Rstg[spar].wait_w(self, a)
                            self.inc("sac", a.activation(
                                out=stg[:, spar * TLOC:(spar + 1) * TLOC],
                                in_=PP[spar][:, :], func=AF.Identity,
                                bias=self.bias_t[:, bcol * NDC + jc: bcol * NDC + jc + 1]))
                        ev = self.val("sac")
                        Rp[spar].rd("sac", ev)

                        @self.S
                        def _(sy):
                            self.qdrain(sy, 'sx')
                            sy.wait_ge(self.sem("sac"), ev)
                            self.inc("sx", sy.dma_start(
                                out=dst[jc * 128:(jc + 1) * 128, :],
                                in_=stg[:, spar * TLOC:(spar + 1) * TLOC]), 16)
                            if dbgn in self.dbg_t:
                                self.inc("sx", sy.dma_start(
                                    out=self.dbg_t[dbgn][jc * 128:(jc + 1) * 128, :],
                                    in_=stg[:, spar * TLOC:(spar + 1) * TLOC]), 16)
                        Rstg[spar].rd("sx", self.val("sx"))

                if self.lim.get("wdump"):
                    wdo = nc.dram_tensor("wdump", [128, NDC * 2048], BF16, kind="ExternalOutput").ap()
                    cdo = nc.dram_tensor("cdump", [128, 2 * D], F32, kind="ExternalOutput").ap()
                    @self.S
                    def _(sy):
                        self.qdrain(sy, 'sx')
                        sy.wait_ge(self.sem("sw"), wv_ld)
                        sy.wait_ge(self.sem("sac"), cb_done)
                        self.inc("sx", sy.dma_start(out=wdo, in_=wvb[:, :]), 16)
                        self.inc("sx", sy.dma_start(out=cdo[:, 0:D], in_=cqb[:, :]), 16)
                        self.inc("sx", sy.dma_start(out=cdo[:, D:2 * D], in_=ckb[:, :]), 16)

                vdst = self.vc_in.ap()
                for tc in range(NDC):
                    spar = tc % 2
                    @self.T
                    def _(t):
                        if tc < 2:
                            t.wait_ge(self.sem("sw"), wv_ld)
                        Rp[spar].wait_w(self, t)
                        ii = None
                        for dc in range(NDC):
                            lhs = xn_sb[:, dc * TLOC + tc * 128: dc * TLOC + (tc + 1) * 128]
                            for js in range(D // 512):
                                ii = t.matmul(PP[spar][:, js * 512:(js + 1) * 512], lhs,
                                              wvb[:, dc * 2048 + js * 512: dc * 2048 + (js + 1) * 512],
                                              start=(dc == 0), stop=(dc == NDC - 1))
                        self.inc("smm", ii)
                    pm = self.val("smm")

                    @self.V
                    def _(v):
                        v.wait_ge(self.sem("smm"), pm)
                        Rstg[spar].wait_w(self, v)
                        self.inc("sdv", v.tensor_add(
                            stg[:, spar * TLOC:(spar + 1) * TLOC],
                            PP[spar][:, :], bvb[:, :]))
                    ev = self.val("sdv")
                    Rp[spar].rd("sdv", ev)

                    @self.S
                    def _(sy):
                        self.qdrain(sy, 'sx')
                        sy.wait_ge(self.sem("sdv"), ev)
                        self.inc("sx", sy.dma_start(
                            out=vdst[tc * 128:(tc + 1) * 128, :],
                            in_=stg[:, spar * TLOC:(spar + 1) * TLOC]), 16)
                        if "dbg_v" in self.dbg_t:
                            self.inc("sx", sy.dma_start(
                                out=self.dbg_t["dbg_v"][tc * 128:(tc + 1) * 128, :],
                                in_=stg[:, spar * TLOC:(spar + 1) * TLOC]), 16)
                    Rstg[spar].rd("sx", self.val("sx"))
            self.phase_snap = self.snap()

    def gather_phase(self):
        kv_stored = self.val("sx")

        @self.G
        def _(g):
            self.qdrain(g, 'sw')
            g.wait_ge(self.sem("sx"), kv_stored)
            self.inc("scc", g.collective_compute(
                "AllGather", OP.bypass,
                replica_groups=[[0, 1, 2, 3], [4, 5, 6, 7]],
                ins=[self.kc_in.ap().opt()], outs=[self.kg.ap().opt()]))
            self.inc("scc", g.collective_compute(
                "AllGather", OP.bypass,
                replica_groups=[[0, 1, 2, 3], [4, 5, 6, 7]],
                ins=[self.vc_in.ap().opt()], outs=[self.vg.ap().opt()]))
        self.phase_snap = self.snap()

    def attn_phase(self):
        nc = self.nc
        self.phase_barrier()
        barrier = self.phase_snap
        kgap, vgap = self.kg.ap(), self.vg.ap()
        with ExitStack() as cctx:
            sb = lambda name, shape, dt: cctx.enter_context(nc.sbuf_tensor(name, shape, dt))
            k_sb = sb("k_sb", [128, 2 * SEQ], BF16)
            vt_sb = sb("vt_sb", [128, 2 * SEQ], BF16)
            o_sb = sb("o_sb", [128, NDC * TLOC], BF16)
            q_sb = sb("q_sb", [128, 4 * 512], BF16)
            pt_sb = sb("pt_sb", [128, 2 * 1024], BF16)
            lrow = sb("lrow", [1, 1024], F32)
            lb_sb = sb("lb_sb", [128, 1024], F32)
            wo_b = sb("wo_b", [128, 2 * 2048], BF16)
            ost = sb("ost", [128, 2 * 512], F32)

            with ExitStack() as px:
                S2 = [px.enter_context(nc.psum_tensor(f"S{i}", [128, 1024], F32))
                      for i in range(2)]
                O2 = [px.enter_context(nc.psum_tensor(f"O{i}", [128, 512], F32))
                      for i in range(2)]
                L2 = [px.enter_context(nc.psum_tensor(f"L{i}", [1, 512], F32))
                      for i in range(2)]
                RS = [Res(), Res()]
                RO = [Res(), Res()]
                RL = [Res(), Res()]
                Rk = [Res(), Res()]
                Rvt = [Res(), Res()]
                Rq = [Res() for _ in range(4)]
                Rpt = [Res(), Res()]
                Rlb = Res()
                Ro = Res()

                kv_ready = {}

                def load_kv(h):
                    par = h % 2
                    @self.S
                    def _(sy):
                        self.qdrain(sy, 'sx')
                        if h == 0:
                            self.wait_snap(sy, barrier)
                            sy.wait_ge(self.sem("scc"), self.val("scc"))
                        Rk[par].wait_w(self, sy)
                        for r in range(GRP):
                            self.inc("sx", sy.dma_start(
                                out=k_sb[:, par * SEQ + r * TLOC: par * SEQ + (r + 1) * TLOC],
                                in_=kgap[r * D + h * 128: r * D + (h + 1) * 128, :]), 16)
                    kx = self.val("sx")

                    @self.G
                    def _(g):
                        qn = self.gq_begin(g)
                        if h == 0:
                            self.wait_snap(g, barrier)
                            g.wait_ge(self.sem("scc"), self.val("scc"))
                        Rvt[par].wait_w(self, g)
                        self.inc(qn, g.dma_start(
                            out=vt_sb[:, par * SEQ: (par + 1) * SEQ]
                            .rearrange("p (kc j) -> p kc j", kc=NKC),
                            in_=vgap[:, h * 128:(h + 1) * 128]
                            .rearrange("(kc p) j -> p kc j", p=128)), 16)
                        kv_ready[h] = (kx, qn, self.val(qn))

                load_kv(0)

                q_ready = {}

                def load_q(h, qt):
                    slot = qt % 4
                    @self.S
                    def _(sy):
                        self.qdrain(sy, 'sx')
                        Rq[slot].wait_w(self, sy)
                        self.inc("sx", sy.dma_start(
                            out=q_sb[:, slot * 512:(slot + 1) * 512],
                            in_=self.q_dr[h * 128:(h + 1) * 128, qt * 512:(qt + 1) * 512]), 16)
                    q_ready[(h, qt)] = self.val("sx")

                load_q(0, 0)
                load_q(0, 1)

                NH = self.lim.get("nheads", HEADS)
                for h in range(NH):
                    par = h % 2
                    if h + 1 < NH:
                        load_kv(h + 1)
                    for qp in range(2):
                        qt0, qt1 = 2 * qp, 2 * qp + 1
                        s0, s1 = qt0 % 4, qt1 % 4
                        if not (h == 0 and qp == 0):
                            load_q(h, qt0)
                            load_q(h, qt1)
                        NKCL = self.lim.get("nkc", NKC)
                        exp_ev = {}

                        def emit_S(kc):
                            sp = kc % 2
                            @self.T
                            def _(t):
                                if kc == 0:
                                    t.wait_ge(self.sem("sx"), q_ready[(h, qt0)])
                                    t.wait_ge(self.sem("sx"), q_ready[(h, qt1)])
                                    t.wait_ge(self.sem("sx"), kv_ready[h][0])
                                    t.wait_ge(self.sem(kv_ready[h][1]), kv_ready[h][2])
                                    RO[0].wait_w(self, t)
                                    RO[1].wait_w(self, t)
                                    RL[0].wait_w(self, t)
                                    RL[1].wait_w(self, t)
                                RS[sp].wait_w(self, t)
                                lhsk = k_sb[:, par * SEQ + kc * 128: par * SEQ + (kc + 1) * 128]
                                t.matmul(S2[sp][:, 0:512], lhsk, q_sb[:, s0 * 512:(s0 + 1) * 512],
                                         start=True, stop=True)
                                ii = t.matmul(S2[sp][:, 512:1024], lhsk,
                                              q_sb[:, s1 * 512:(s1 + 1) * 512],
                                              start=True, stop=True)
                                self.inc("smm", ii)
                            smv = self.val("smm")

                            @self.A
                            def _(a):
                                a.wait_ge(self.sem("smm"), smv)
                                Rpt[kc % 2].wait_w(self, a)
                                self.inc("sac", a.activation(
                                    out=pt_sb[:, (kc % 2) * 1024:(kc % 2 + 1) * 1024],
                                    in_=S2[sp][:, :], func=AF.Exp))
                            exp_ev[kc] = self.val("sac")
                            RS[sp].rd("sac", exp_ev[kc])

                        def emit_AV(kc):
                            po = (kc % 2) * 1024
                            @self.T
                            def _(t):
                                t.wait_ge(self.sem("sac"), exp_ev[kc])
                                lhsv = vt_sb[:, par * SEQ + kc * 128: par * SEQ + (kc + 1) * 128]
                                t.matmul(O2[0][:, :], lhsv, pt_sb[:, po: po + 512],
                                         start=(kc == 0), stop=(kc == NKCL - 1))
                                t.matmul(O2[1][:, :], lhsv, pt_sb[:, po + 512: po + 1024],
                                         start=(kc == 0), stop=(kc == NKCL - 1))
                                t.matmul(L2[0][0:1, :], self.ones_bf[:, :],
                                         pt_sb[:, po: po + 512],
                                         start=(kc == 0), stop=(kc == NKCL - 1))
                                ii = t.matmul(L2[1][0:1, :], self.ones_bf[:, :],
                                              pt_sb[:, po + 512: po + 1024],
                                              start=(kc == 0), stop=(kc == NKCL - 1))
                                self.inc("smm", ii)
                            av = self.val("smm")
                            Rpt[kc % 2].rd("smm", av)
                            Rq[s0].rd("smm", av)
                            Rq[s1].rd("smm", av)
                            Rk[par].rd("smm", av)
                            Rvt[par].rd("smm", av)

                        emit_S(0)
                        for kc in range(1, NKCL):
                            emit_S(kc)
                            emit_AV(kc - 1)
                        emit_AV(NKCL - 1)
                        av_all = self.val("smm")

                        @self.V
                        def _(v):
                            v.wait_ge(self.sem("smm"), av_all)
                            v.tensor_copy(lrow[0:1, 0:512], L2[0][0:1, :])
                            v.tensor_copy(lrow[0:1, 512:1024], L2[1][0:1, :])
                            self.inc("sdv", v.reciprocal(lrow[0:1, :], lrow[0:1, :]))
                        lr = self.val("sdv")
                        RL[0].rd("sdv", lr)
                        RL[1].rd("sdv", lr)

                        @self.T
                        def _(t):
                            t.wait_ge(self.sem("sdv"), lr)
                            RS[0].wait_w(self, t)
                            ii = None
                            for ts in range(2):
                                ii = t.matmul(S2[0][:, ts * 512:(ts + 1) * 512], self.onesf[:, :],
                                              lrow[0:1, ts * 512:(ts + 1) * 512],
                                              start=True, stop=True)
                            self.inc("smm", ii)
                        lbm = self.val("smm")

                        @self.A
                        def _(a):
                            a.wait_ge(self.sem("smm"), lbm)
                            Rlb.wait_w(self, a)
                            self.inc("sac", a.activation(out=lb_sb[:, :], in_=S2[0][:, :],
                                                         func=AF.Copy))
                        lbe = self.val("sac")
                        RS[0].rd("sac", lbe)

                        @self.V
                        def _(v):
                            v.wait_ge(self.sem("sac"), lbe)
                            if self.lim.get("nol"):
                                v.tensor_copy(
                                    o_sb[:, h * TLOC + qt0 * 512: h * TLOC + (qt0 + 1) * 512],
                                    O2[0][:, :])
                                self.inc("sdv", v.tensor_copy(
                                    o_sb[:, h * TLOC + qt1 * 512: h * TLOC + (qt1 + 1) * 512],
                                    O2[1][:, :]))
                            else:
                                v.tensor_tensor(
                                    o_sb[:, h * TLOC + qt0 * 512: h * TLOC + (qt0 + 1) * 512],
                                    O2[0][:, :], lb_sb[:, 0:512], OP.mult)
                                self.inc("sdv", v.tensor_tensor(
                                    o_sb[:, h * TLOC + qt1 * 512: h * TLOC + (qt1 + 1) * 512],
                                    O2[1][:, :], lb_sb[:, 512:1024], OP.mult))
                        oev = self.val("sdv")
                        RO[0].rd("sdv", oev)
                        RO[1].rd("sdv", oev)
                        Rlb.rd("sdv", oev)
                o_done = self.val("sdv")

                if self.lim.get("ptdump"):
                    pdo = nc.dram_tensor("ptdump", [128, 2 * 1024], BF16, kind="ExternalOutput").ap()
                    sdo = nc.dram_tensor("sdump", [128, 1024], F32, kind="ExternalOutput").ap()
                    @self.V
                    def _(v):
                        v.wait_ge(self.sem("sdv"), o_done)
                        self.inc("sdv", v.tensor_copy(lb_sb[:, :], S2[0][:, :]))
                    sde = self.val("sdv")
                    @self.S
                    def _(sy):
                        self.qdrain(sy, 'sx')
                        sy.wait_ge(self.sem("sdv"), sde)
                        self.inc("sx", sy.dma_start(out=pdo, in_=pt_sb[:, :]), 16)
                        self.inc("sx", sy.dma_start(out=sdo, in_=lb_sb[:, :]), 16)

                if self.lim.get("odump"):
                    odo = nc.dram_tensor("odump", [D, TLOC], BF16, kind="ExternalOutput").ap()
                    @self.S
                    def _(sy):
                        self.qdrain(sy, 'sx')
                        sy.wait_ge(self.sem("sdv"), o_done)
                        for jc in range(NDC):
                            self.inc("sx", sy.dma_start(
                                out=odo[jc * 128:(jc + 1) * 128, :],
                                in_=o_sb[:, jc * TLOC:(jc + 1) * TLOC]), 16)

                Rwo = [Res(), Res()]
                Rost = [Res(), Res()]
                wo_ready = {}

                def load_wo(do):
                    par = do % 2
                    @self.G
                    def _(g):
                        qn = self.gq_begin(g)
                        Rwo[par].wait_w(self, g)
                        self.inc(qn, g.dma_start(out=wo_b[:, par * 2048:(par + 1) * 2048],
                                                 in_=self.wo_p[do]), 16)
                        wo_ready[do] = (qn, self.val(qn))

                load_wo(0)
                load_wo(1)

                for do in range(NDC):
                    par = do % 2
                    for qt in range(4):
                        opar = qt % 2
                        @self.T
                        def _(t):
                            t.wait_ge(self.sem(wo_ready[do][0]), wo_ready[do][1])
                            if do == 0 and qt == 0:
                                t.wait_ge(self.sem("sdv"), o_done)
                            RO[opar].wait_w(self, t)
                            ii = None
                            for jc in range(NDC):
                                lhs = wo_b[:, par * 2048 + jc * 128: par * 2048 + (jc + 1) * 128]
                                ii = t.matmul(O2[opar][:, :], lhs,
                                              o_sb[:, jc * TLOC + qt * 512: jc * TLOC + (qt + 1) * 512],
                                              start=(jc == 0), stop=(jc == NDC - 1))
                            self.inc("smm", ii)
                        wmm = self.val("smm")
                        Ro.rd("smm", wmm)
                        Rwo[par].rd("smm", wmm)

                        sslot = (do * 4 + qt) % 2
                        @self.A
                        def _(a):
                            a.wait_ge(self.sem("smm"), wmm)
                            Rost[sslot].wait_w(self, a)
                            self.inc("sac", a.activation(
                                out=ost[:, sslot * 512:(sslot + 1) * 512],
                                in_=O2[opar][:, :], func=AF.Identity,
                                bias=self.bias_t[:, 3 * NDC + do: 3 * NDC + do + 1]))
                        wev = self.val("sac")
                        RO[opar].rd("sac", wev)

                        @self.G
                        def _(g):
                            self.qdrain(g, 'sw')
                            g.wait_ge(self.sem("sac"), wev)
                            self.inc("sw", g.dma_start(
                                out=self.x_res[do * 128:(do + 1) * 128, qt * 512:(qt + 1) * 512],
                                in_=ost[:, sslot * 512:(sslot + 1) * 512],
                                accum_op=OP.add), 16)
                        Rost[sslot].rd("sw", self.val("sw"))
                    if do + 2 < NDC:
                        load_wo(do + 2)
            self.phase_snap = self.snap()

    def final_wait(self):
        @self.S
        def _(sy):
            self.qdrain(sy, 'sx')
            sy.wait_ge(self.sem("sx"), self.val("sx"))
            sy.wait_ge(self.sem("sw"), self.val("sw"))


def _pack_inputs(inputs):
    f32 = np.float32
    i = inputs
    pk4 = lambda WT: np.ascontiguousarray(
        WT.reshape(NDC, 128, NDC, 128).transpose(2, 1, 0, 3).reshape(NDC, 128, NDC * 128))
    d = {}
    d["wq_p"] = pk4(np.asarray(i["L0_attn_Wq"], f32).T.copy())
    d["wk_p"] = pk4(np.asarray(i["L0_attn_Wk"], f32).T.copy())
    d["wv_p"] = np.ascontiguousarray(np.asarray(i["L0_attn_Wv"], f32).T.reshape(NDC, 128, D))
    d["wo_p"] = pk4(np.asarray(i["L0_attn_Wo"], f32).T.copy())
    bt = np.zeros((128, 4 * NDC), f32)
    for c, nm in enumerate(("L0_attn_bq", "L0_attn_bk", "L0_attn_bv", "L0_attn_bo")):
        bt[:, c * NDC:(c + 1) * NDC] = np.asarray(i[nm], f32).reshape(NDC, 128).T
    d["biases"] = bt
    d["bv_r"] = np.asarray(i["L0_attn_bv"], f32).reshape(1, D).copy()
    d["bq_r"] = np.asarray(i["L0_attn_bq"], f32).reshape(1, D).copy()
    d["bk_r"] = np.asarray(i["L0_attn_bk"], f32).reshape(1, D).copy()
    hl = np.asarray(i["L0_attn_helix_rotation"], f32)
    sp = np.asarray(i["L0_attn_spiral_weight"], f32)
    d["helix_t"] = np.ascontiguousarray(hl.reshape(NDC, 128).T)
    d["helix_r"] = hl.reshape(1, D).copy()
    d["spiral_t"] = np.ascontiguousarray(np.broadcast_to(sp[None, :], (128, NDC))).astype(f32)
    d["spiral_r"] = np.repeat(sp, 128).reshape(1, D).copy()
    d["gwT_p"] = np.ascontiguousarray(np.asarray(i["L0_ffn_gate_w"], f32).T.reshape(NDC, 128, NE))
    d["rb"] = np.asarray(i["L0_ffn_router_bias"], f32).reshape(NE, 1).copy()
    W1 = np.asarray(i["L0_ffn_W1"], f32)
    W3 = np.asarray(i["L0_ffn_W3"], f32)
    W2 = np.asarray(i["L0_ffn_W2"], f32)
    d["w1_p"] = np.ascontiguousarray(
        W1.reshape(NE, NDC, 128, NIC, 128).transpose(0, 3, 2, 1, 4).reshape(NE, NIC, 128, NDC * 128))
    d["w3_p"] = np.ascontiguousarray(
        W3.reshape(NE, NDC, 128, NIC, 128).transpose(0, 3, 2, 1, 4).reshape(NE, NIC, 128, NDC * 128))
    d["w2_p"] = np.ascontiguousarray(
        W2.reshape(NE, NIC, 128, NDC, 128).transpose(0, 3, 2, 1, 4).reshape(NE, NDC, 128, NIC * 128))
    sw1T = np.asarray(i["L0_ffn_sw1"], f32).T.copy()
    sw3T = np.asarray(i["L0_ffn_sw3"], f32).T.copy()
    sw2T = np.asarray(i["L0_ffn_sw2"], f32).T.copy()
    d["sw1_p"] = np.ascontiguousarray(
        sw1T.reshape(NDC, 128, NHC, 128).transpose(2, 1, 0, 3).reshape(NHC, 128, NDC * 128))
    d["sw3_p"] = np.ascontiguousarray(
        sw3T.reshape(NDC, 128, NHC, 128).transpose(2, 1, 0, 3).reshape(NHC, 128, NDC * 128))
    d["sw2_p"] = np.ascontiguousarray(
        sw2T.reshape(NHC, 128, NDC, 128).transpose(2, 1, 0, 3).reshape(NDC, 128, NHC * 128))
    d["identf"] = np.eye(128, dtype=np.float32)
    c = np.zeros((1, 8), np.float32); c[0, :len(CONSTS)] = CONSTS
    d["consts"] = c
    return d


_NC_CACHE = {}


def _get_nc(dbg=(), phases=("ffn1", "qkv", "gather", "attn", "ffn2"), lim=None):
    key = (tuple(sorted(dbg)), tuple(phases), tuple(sorted((lim or {}).items())))
    if key not in _NC_CACHE:
        _NC_CACHE[key] = Builder(bass.Bass("TRN2", target_bir_lowering=False,
                                           detect_race_conditions=False), dbg,
                                 phases, lim).build()
    return _NC_CACHE[key]


def kernel(**inputs):
    x = np.asarray(inputs["L0_attn_x"], np.float32)
    shared = _pack_inputs(inputs)
    nc = _get_nc()
    in_maps = []
    for c in range(8):
        seq, t0 = c // GRP, (c % GRP) * TLOC
        m = dict(shared)
        m["xT"] = np.ascontiguousarray(x[seq, t0:t0 + TLOC, :].T)
        in_maps.append(m)
    res = run_bass_kernel_spmd(nc, in_maps, core_ids=list(range(8)))
    out = np.empty((2, SEQ, D), np.float32)
    for c in range(8):
        seq, t0 = c // GRP, (c % GRP) * TLOC
        out[seq, t0:t0 + TLOC, :] = res.results[c]["outT"].T
    return out


# version 09
# speedup vs baseline: 1.3156x; 1.0690x over previous
import math
from contextlib import ExitStack

import numpy as np
import ml_dtypes
import concourse.bass as bass
import concourse.mybir as mybir
from concourse.bass_utils import run_bass_kernel_spmd

F32 = mybir.dt.float32
BF16 = mybir.dt.bfloat16
AF = mybir.ActivationFunctionType
OP = mybir.AluOpType
AX = mybir.AxisListType

D = 2048
NDC = 16
TLOC = 2048
TT = 1024
NTT = TLOC // TT
INTER = 1408
NIC = 11
NE = 8
NSH = 2816
NHC = 22
HEADS = 16
DH = 128
SEQ = 8192
GRP = 4
NKC = SEQ // 128
ISQ = 1.0 / math.sqrt(128.0)
M_STAT = 33
CONSTS = (0.5, 0.25, -0.5, -1.0, 1e-5)


class Res:
    __slots__ = ("g",)

    def __init__(self):
        self.g = {}

    def rd(self, sem, val):
        if self.g.get(sem, 0) < val:
            self.g[sem] = val

    def wait_w(self, b, eng):
        for s, v in self.g.items():
            eng.wait_ge(b.sem(s), v)
        self.g = {}


class Builder:
    def __init__(self, nc, dbg=(), phases=("ffn1", "qkv", "gather", "attn", "ffn2"),
                 lim=None):
        self.nc = nc
        self.dbg = set(dbg)
        self.phases = set(phases)
        self.lim = lim or {}

    def _mksem(self, name):
        self.sems[name] = [self.ctx.enter_context(self.nc.semaphore(name)), 0]

    def sem(self, n):
        return self.sems[n][0]

    def val(self, n):
        return self.sems[n][1]

    def inc(self, n, instr, k=1):
        s = self.sems[n]
        instr.then_inc(s[0], k)
        s[1] += k
        return s[1]

    def snap(self):
        return {n: v[1] for n, v in self.sems.items()}

    def wait_snap(self, eng, snapshot):
        for n, v in snapshot.items():
            if v > 0:
                eng.wait_ge(self.sem(n), v)

    def phase_barrier(self):
        snap = self.phase_snap
        for sec in (self.T, self.V, self.A, self.S, self.G):
            def _f(eng, snap=snap):
                self.wait_snap(eng, snap)
            sec(_f)

    def gq_begin(self, g):
        name = f"sw{self.swrot % 4}"
        self.swrot += 1
        if self.val(name) > 0:
            g.wait_ge(self.sem(name), self.val(name))
        return name

    def qdrain(self, eng, qsem):
        if self.val(qsem) > 0:
            eng.wait_ge(self.sem(qsem), self.val(qsem))

    def T(self, f): self.block.tensor(f)
    def V(self, f): self.block.vector(f)
    def A(self, f): self.block.scalar(f)
    def G(self, f): self.block.gpsimd(f)
    def S(self, f): self.block.sync(f)

    def build(self):
        nc = self.nc
        EI = dict(kind="ExternalInput")
        self.xT = nc.dram_tensor("xT", [D, TLOC], F32, **EI).ap()
        self.wq_p = nc.dram_tensor("wq_p", [NDC, 128, NDC * 128], F32, **EI).ap()
        self.wk_p = nc.dram_tensor("wk_p", [NDC, 128, NDC * 128], F32, **EI).ap()
        self.wv_p = nc.dram_tensor("wv_p", [NDC, 128, D], F32, **EI).ap()
        self.wo_p = nc.dram_tensor("wo_p", [NDC, 128, NDC * 128], F32, **EI).ap()
        self.biases = nc.dram_tensor("biases", [128, 4 * NDC], F32, **EI).ap()
        self.bv_r = nc.dram_tensor("bv_r", [1, D], F32, **EI).ap()
        self.bq_r = nc.dram_tensor("bq_r", [1, D], F32, **EI).ap()
        self.bk_r = nc.dram_tensor("bk_r", [1, D], F32, **EI).ap()
        self.helix_t = nc.dram_tensor("helix_t", [128, NDC], F32, **EI).ap()
        self.helix_r = nc.dram_tensor("helix_r", [1, D], F32, **EI).ap()
        self.spiral_t = nc.dram_tensor("spiral_t", [128, NDC], F32, **EI).ap()
        self.spiral_r = nc.dram_tensor("spiral_r", [1, D], F32, **EI).ap()
        self.gwT_p = nc.dram_tensor("gwT_p", [NDC, 128, NE], F32, **EI).ap()
        self.rb_d = nc.dram_tensor("rb", [NE, 1], F32, **EI).ap()
        self.w1_p = nc.dram_tensor("w1_p", [NE, NIC, 128, NDC * 128], F32, **EI).ap()
        self.w3_p = nc.dram_tensor("w3_p", [NE, NIC, 128, NDC * 128], F32, **EI).ap()
        self.w2_p = nc.dram_tensor("w2_p", [NE, NDC, 128, NIC * 128], F32, **EI).ap()
        self.sw1_p = nc.dram_tensor("sw1_p", [NHC, 128, NDC * 128], F32, **EI).ap()
        self.sw3_p = nc.dram_tensor("sw3_p", [NHC, 128, NDC * 128], F32, **EI).ap()
        self.sw2_p = nc.dram_tensor("sw2_p", [NDC, 128, NHC * 128], F32, **EI).ap()
        self.identf_d = nc.dram_tensor("identf", [128, 128], F32, **EI).ap()
        self.consts_d = nc.dram_tensor("consts", [1, 8], F32, **EI).ap()

        self.outT = nc.dram_tensor("outT", [D, TLOC], F32, kind="ExternalOutput").ap()

        self.x_res = nc.dram_tensor("x_res", [D, TLOC], F32).ap()
        self.xnA = nc.dram_tensor("xnA", [D, TLOC], BF16).ap()
        self.q_dr = nc.dram_tensor("q_dr", [D, TLOC], BF16).ap()
        self.g_dr = nc.dram_tensor("g_dr", [NE, TT], BF16).ap()
        self.row_dr = nc.dram_tensor("row_dr", [4, D], F32).ap()
        self.kc_in = nc.dram_tensor("kc_in", [D, TLOC], BF16)
        self.vc_in = nc.dram_tensor("vc_in", [TLOC, D], BF16)
        self.kg = nc.dram_tensor("kg", [GRP * D, TLOC], BF16)
        self.vg = nc.dram_tensor("vg", [GRP * TLOC, D], BF16)

        self.dbg_t = {}
        for name in self.dbg:
            shp = [TLOC, D] if name == "dbg_v" else [D, TLOC]
            dt = BF16 if name in ("dbg_xnA", "dbg_q", "dbg_k", "dbg_v") else F32
            self.dbg_t[name] = nc.dram_tensor(name, shp, dt, kind="ExternalOutput").ap()

        with ExitStack() as ctx:
            self.ctx = ctx
            self.block = ctx.enter_context(nc.Block())
            self.sems = {}
            for n in ("sw", "sx", "smm", "sac", "sdv", "scc",
                      "sw0", "sw1", "sw2", "sw3"):
                self._mksem(n)
            self.swrot = 0

            sb = lambda name, shape, dt: ctx.enter_context(nc.sbuf_tensor(name, shape, dt))
            self.combo = sb("combo", [128, NDC * M_STAT], F32)
            self.onesf = sb("onesf", [1, 128], F32)
            self.onesbr = sb("onesbr", [1, 128], BF16)
            self.ones_bf = sb("ones_bf", [128, 1], BF16)
            self.bias_t = sb("bias_t", [128, 4 * NDC], F32)
            self.rb8 = sb("rb8", [NE, 1], F32)
            self.gsum = sb("gsum", [NE, 1], F32)
            self.idf = sb("idf", [128, 128], F32)
            self.ones8f = sb("ones8f", [1, NE], F32)
            self.ones8c = sb("ones8c", [NE, 1], F32)
            self.consts = sb("consts_sb", [1, 8], F32)
            for ci, cv in enumerate(CONSTS):
                nc.const_aps.register(F32, cv, self.consts[0:1, ci:ci + 1])

            self.prologue()
            if "ffn1" in self.phases:
                self.ffn_call(1)
            if "qkv" in self.phases:
                self.qkv_phase()
            if "gather" in self.phases:
                self.gather_phase()
            if "attn" in self.phases:
                self.attn_phase()
            if "kdump" in self.phases:
                kdo = nc.dram_tensor("kdump", [GRP * D, TLOC], BF16, kind="ExternalOutput").ap()
                vdo = nc.dram_tensor("vdump", [GRP * TLOC, D], BF16, kind="ExternalOutput").ap()
                @self.S
                def _(sy):
                    self.qdrain(sy, 'sx')
                    sy.wait_ge(self.sem("scc"), self.val("scc"))
                    self.inc("sx", sy.dma_start(out=kdo, in_=self.kg.ap()), 16)
                    self.inc("sx", sy.dma_start(out=vdo, in_=self.vg.ap()), 16)
            if "xdump" in self.phases:
                @self.S
                def _(sy):
                    self.qdrain(sy, 'sx')
                    self.wait_snap(sy, self.phase_snap)
                    sy.wait_ge(self.sem("sw"), self.val("sw"))
                    self.inc("sx", sy.dma_start(out=self.outT, in_=self.x_res), 16)
            if "ffn2" in self.phases:
                self.ffn_call(2)
            self.final_wait()
        return nc

    def prologue(self):
        nc = self.nc
        with ExitStack() as pctx:
            trow = pctx.enter_context(nc.sbuf_tensor("trow", [1, 6 * D], F32))
            gps = pctx.enter_context(nc.psum_tensor("gsum_ps", [NE, 8], F32))

            @self.S
            def _(sy):
                self.qdrain(sy, 'sx')
                for dst, src in ((self.rb8[:, :], self.rb_d),
                                 (self.bias_t[:, :], self.biases),
                                 (self.idf[:, :], self.identf_d),
                                 (self.consts[:, :], self.consts_d),
                                 (trow[0:1, 0:D], self.helix_r),
                                 (trow[0:1, 4 * D:5 * D], self.bq_r),
                                 (trow[0:1, 5 * D:6 * D], self.bk_r)):
                    self.inc("sx", sy.dma_start(out=dst, in_=src), 16)
            ld0 = self.val("sx")

            @self.G
            def _(g):
                self.qdrain(g, 'sw')
                for dc in range(NDC):
                    self.inc("sw", g.dma_start(out=self.combo[:, dc * M_STAT: dc * M_STAT + NE],
                                               in_=self.gwT_p[dc]), 16)
                self.inc("sw", g.dma_start(out=trow[0:1, D:2 * D], in_=self.spiral_r), 16)
            ldw0 = self.val("sw")

            @self.V
            def _(v):
                v.memset(self.onesf[:, :], 1.0)
                v.memset(self.onesbr[:, :], 1.0)
                v.memset(self.ones_bf[:, :], 1.0)
                v.memset(self.ones8f[:, :], 1.0)
                v.memset(self.ones8c[:, :], 1.0)
                for dc in range(NDC):
                    v.memset(self.combo[:, dc * M_STAT + NE: dc * M_STAT + 32], 0.0)
                ii = None
                for dc in range(NDC):
                    ii = v.memset(self.combo[:, dc * M_STAT + 32: dc * M_STAT + 33], 1.0)
                self.inc("sdv", ii)
            mz = self.val("sdv")

            @self.A
            def _(a):
                a.wait_ge(self.sem("sx"), ld0)
                a.wait_ge(self.sem("sw"), ldw0)
                a.activation(out=trow[0:1, 2 * D:3 * D], in_=trow[0:1, 0:D], func=AF.Sin, scale=0.5)
                ii = a.activation(out=trow[0:1, 3 * D:4 * D], in_=trow[0:1, 0:D], func=AF.Sin, scale=0.25)
                self.inc("sac", ii)
            sin_done = self.val("sac")

            @self.V
            def _(v):
                v.wait_ge(self.sem("sac"), sin_done)
                v.wait_ge(self.sem("sdv"), mz)
                r_cq = trow[0:1, 0:D]
                r_sw = trow[0:1, D:2 * D]
                r_s2 = trow[0:1, 2 * D:3 * D]
                r_s4 = trow[0:1, 3 * D:4 * D]
                v.tensor_tensor(r_cq, r_s2, r_s2, OP.mult)
                v.tensor_scalar(r_cq, r_cq, -2.0, 1.0, OP.mult, OP.add)
                v.tensor_tensor(r_cq, r_cq, r_sw, OP.mult)
                v.tensor_scalar_mul(r_cq, r_cq, ISQ)
                v.tensor_tensor(r_sw, r_s4, r_s4, OP.mult)
                v.tensor_scalar(r_sw, r_sw, -2.0, 1.0, OP.mult, OP.add)
                v.tensor_tensor(r_s2, r_s2, r_sw, OP.mult)
                v.tensor_scalar_mul(trow[0:1, D:2 * D], r_s2, 2.0)
                v.tensor_tensor(trow[0:1, 2 * D:3 * D], trow[0:1, 4 * D:5 * D], r_cq, OP.mult)
                self.inc("sdv", v.tensor_tensor(trow[0:1, 3 * D:4 * D], trow[0:1, 5 * D:6 * D],
                                                trow[0:1, D:2 * D], OP.mult))
            trig_dv = self.val("sdv")

            @self.S
            def _(sy):
                self.qdrain(sy, 'sx')
                sy.wait_ge(self.sem("sdv"), trig_dv)
                self.inc("sx", sy.dma_start(out=self.row_dr.rearrange("a d -> (a d)").rearrange("(o n) -> o n", o=1),
                                            in_=trow[0:1, 0:4 * D]), 16)
            self.rows_stored = self.val("sx")

            @self.T
            def _(t):
                t.wait_ge(self.sem("sw"), ldw0)
                t.wait_ge(self.sem("sdv"), mz)
                ii = None
                for dc in range(NDC):
                    ii = t.matmul(gps[0:NE, 0:1], self.combo[:, dc * M_STAT: dc * M_STAT + NE],
                                  self.combo[:, dc * M_STAT + 32: dc * M_STAT + 33],
                                  start=(dc == 0), stop=(dc == NDC - 1))
                self.inc("smm", ii)
            gmm = self.val("smm")

            @self.V
            def _(v):
                v.wait_ge(self.sem("smm"), gmm)
                self.inc("sdv", v.tensor_copy(self.gsum[:, :], gps[0:NE, 0:1]))
            self.phase_snap = self.snap()

    def ln_stats(self, x_sb, xsq, ps_sum, ps_sq, rows, Rxsq):
        sq_done = []
        mm_after = []
        for dc in range(NDC):
            @self.V
            def _(v):
                if dc >= 2:
                    v.wait_ge(self.sem("smm"), mm_after[dc - 2])
                dst = xsq[:, (dc % 2) * TT:(dc % 2 + 1) * TT]
                ii = v.tensor_tensor(dst, x_sb[:, dc * TT:(dc + 1) * TT],
                                     x_sb[:, dc * TT:(dc + 1) * TT], OP.mult)
                sq_done.append(self.inc("sdv", ii))

            @self.T
            def _(t):
                lhs = self.combo[:, dc * M_STAT:(dc + 1) * M_STAT]
                for ts in range(2):
                    t.matmul(ps_sum[0:M_STAT, ts * 512:(ts + 1) * 512], lhs,
                             x_sb[:, dc * TT + ts * 512: dc * TT + (ts + 1) * 512],
                             start=(dc == 0), stop=(dc == NDC - 1))
                t.wait_ge(self.sem("sdv"), sq_done[dc])
                ii = None
                for ts in range(2):
                    ii = t.matmul(ps_sq[0:M_STAT, ts * 512:(ts + 1) * 512], lhs,
                                  xsq[:, (dc % 2) * TT + ts * 512:(dc % 2) * TT + (ts + 1) * 512],
                                  start=(dc == 0), stop=(dc == NDC - 1))
                mm_after.append(self.inc("smm", ii))
        stats_mm = self.val("smm")
        Rxsq.rd("smm", stats_mm)

        @self.V
        def _(v):
            v.wait_ge(self.sem("smm"), stats_mm)
            v.tensor_scalar_mul(rows["m"], ps_sum[32:33, :], 1.0 / D)
            v.tensor_scalar_mul(rows["v"], ps_sq[32:33, :], 1.0 / D)
            v.tensor_tensor(rows["a"], rows["m"], rows["m"], OP.mult)
            self.inc("sdv", v.tensor_sub(rows["v"], rows["v"], rows["a"]))
        vd = self.val("sdv")

        @self.A
        def _(a):
            a.wait_ge(self.sem("sdv"), vd)
            a.activation(out=rows["a"], in_=rows["v"], func=AF.Ln, bias=1e-5)
            self.inc("sac", a.activation(out=rows["a"], in_=rows["a"], func=AF.Exp, scale=-0.5))
        ad = self.val("sac")

        @self.V
        def _(v):
            v.wait_ge(self.sem("sac"), ad)
            self.inc("sdv", v.scalar_tensor_tensor(rows["b"], rows["m"], -1.0, rows["a"],
                                                   OP.mult, OP.mult))
        return self.val("sdv"), stats_mm

    def bcast2(self, row_a, row_b, Pa, Ra, Pb, Rb, out_a, out_b, rdy_sem, rdy_val):
        @self.T
        def _(t):
            t.wait_ge(self.sem(rdy_sem), rdy_val)
            Ra.wait_w(self, t)
            Rb.wait_w(self, t)
            ii = None
            for P0, row in ((Pa, row_a), (Pb, row_b)):
                for ts in range(2):
                    ii = t.matmul(P0[0:128, ts * 512:(ts + 1) * 512],
                                  self.onesf[:, :], row[0:1, ts * 512:(ts + 1) * 512],
                                  start=True, stop=True)
            self.inc("smm", ii)
        mmv = self.val("smm")

        @self.A
        def _(a):
            a.wait_ge(self.sem("smm"), mmv)
            a.activation(out=out_a, in_=Pa[:, 0:TT], func=AF.Copy)
            self.inc("sac", a.activation(out=out_b, in_=Pb[:, 0:TT], func=AF.Copy))
        Ra.rd("sac", self.val("sac"))
        Rb.rd("sac", self.val("sac"))
        return self.val("sac")

    def ffn_call(self, call_id):
        nc = self.nc
        self.phase_barrier()
        src = self.xT if call_id == 1 else self.x_res
        with ExitStack() as cctx:
            sb = lambda name, shape, dt: cctx.enter_context(
                nc.sbuf_tensor(f"{name}_{call_id}", shape, dt))
            bf = {}
            bf["x_sb"] = sb("x", [128, NDC * TT], F32)
            bf["xn_sb"] = sb("xn", [128, NDC * TT], BF16)
            bf["act_sb"] = sb("act", [128, NIC * TT], BF16)
            bf["w1b"] = sb("w1b", [128, 2 * NDC * 128], BF16)
            bf["w3b"] = sb("w3b", [128, 2 * NDC * 128], BF16)
            bf["w2b"] = sb("w2b", [128, 2 * NHC * 128], BF16)
            bf["gb_sb"] = sb("gb", [128, NE * TT], BF16)
            bf["ab_sb"] = sb("ab", [128, TT], F32)
            bf["bb_sb"] = sb("bb", [128, TT], F32)
            bf["xsq"] = sb("xsq", [128, 2 * TT], F32)
            rowst = sb("rows", [1, 4 * TT], F32)
            bf["ws1"] = sb("ws1", [128, TT], F32)
            bf["ws2"] = sb("ws2", [128, TT], F32)
            bf["g8"] = sb("g8", [NE, TT], BF16)
            bf["rows"] = {n: rowst[0:1, i * TT:(i + 1) * TT]
                          for i, n in enumerate(("m", "v", "a", "b"))}
            bf["growf"] = rowst[0:1, 0:2 * TT]

            @self.V
            def _(v):
                v.memset(bf["ws1"][:, :], 0.0)
                v.memset(bf["ws2"][:, :], 0.0)
                v.memset(bf["gb_sb"][:, :], 0.0)
                self.inc("sdv", v.memset(bf["g8"][:, :], 0.0))

            with ExitStack() as px:
                P = [px.enter_context(nc.psum_tensor(f"P{i}_{call_id}", [128, TT], F32))
                     for i in range(4)]
                bf["P"] = P
                bf["R"] = dict(P=[Res() for _ in range(4)],
                               w1b=[Res(), Res()], w3b=[Res(), Res()], w2b=[Res(), Res()],
                               x=Res(), xn=Res(), act=Res(), ws=Res(), xsq=Res(),
                               g8=Res(), gdr=Res())
                for tt in range(self.lim.get("ntt", NTT)):
                    self._ffn_tile(call_id, tt, src, bf)
            self.phase_snap = self.snap()

    def _ffn_tile(self, call_id, tt, src, bf):
        t0 = tt * TT
        P, R = bf["P"], bf["R"]
        x_sb, xn_sb = bf["x_sb"], bf["xn_sb"]
        xsq, rows = bf["xsq"], bf["rows"]

        @self.S
        def _(sy):
            self.qdrain(sy, 'sx')
            if tt == 0:
                self.wait_snap(sy, self.phase_snap)
            R["x"].wait_w(self, sy)
            for dc in range(NDC):
                self.inc("sx", sy.dma_start(out=x_sb[:, dc * TT:(dc + 1) * TT],
                                            in_=src[dc * 128:(dc + 1) * 128, t0:t0 + TT]), 16)
        x_ld = self.val("sx")

        @self.V
        def _(v):
            v.wait_ge(self.sem("sx"), x_ld)
            R["xsq"].wait_w(self, v)

        @self.T
        def _(t):
            t.wait_ge(self.sem("sx"), x_ld)
            R["P"][2].wait_w(self, t)
            R["P"][3].wait_w(self, t)

        ab_dv, stats_mm = self.ln_stats(x_sb, xsq, P[2], P[3], rows, R["xsq"])

        ab_ready = self.bcast2(rows["a"], rows["b"], P[0], R["P"][0], P[1], R["P"][1],
                               bf["ab_sb"][:, :], bf["bb_sb"][:, :], "sdv", ab_dv)

        if self.lim.get("router", True):
            self._router(tt, bf, ab_ready)
        else:
            self.gb_ready = self.val("sac")
            self.grow_read_mm = self.val("smm")

        @self.V
        def _(v):
            v.wait_ge(self.sem("sac"), ab_ready)
            R["xn"].wait_w(self, v)
            R["xsq"].wait_w(self, v)
            ii = None
            for dc in range(NDC):
                v.tensor_tensor(xsq[:, 0:TT], x_sb[:, dc * TT:(dc + 1) * TT],
                                bf["ab_sb"][:, :], OP.mult)
                ii = v.tensor_tensor(xn_sb[:, dc * TT:(dc + 1) * TT], xsq[:, 0:TT],
                                     bf["bb_sb"][:, :], OP.add)
            self.inc("sdv", ii)
        self.xn_ready = self.val("sdv")

        self.x_acc_dv = self.xn_ready
        for e in range(self.lim.get("ne", NE)):
            self._expert(e, bf)

        for half in range(2 if self.lim.get("shared", True) else 0):
            self._shared(half, bf)

        if self.lim.get("epi", True):
            self._tile_epilogue(call_id, tt, bf, t0)
        else:
            @self.S
            def _(sy):
                self.qdrain(sy, 'sx')
                sy.wait_ge(self.sem("sdv"), self.x_acc_dv)
                for dc in range(NDC):
                    self.inc("sx", sy.dma_start(
                        out=self.x_res[dc * 128:(dc + 1) * 128, t0:t0 + TT],
                        in_=bf["x_sb"][:, dc * TT:(dc + 1) * TT]), 16)
                if "dbg_xres" in self.dbg_t:
                    for dc in range(NDC):
                        self.inc("sx", sy.dma_start(
                            out=self.dbg_t["dbg_xres"][dc * 128:(dc + 1) * 128, t0:t0 + TT],
                            in_=bf["x_sb"][:, dc * TT:(dc + 1) * TT]), 16)
            bf["R"]["x"].rd("sx", self.val("sx"))

    def _router(self, tt, bf, ab_ready):
        P, R = bf["P"], bf["R"]
        ws1, ws2, g8, xsq = bf["ws1"], bf["ws2"], bf["g8"], bf["xsq"]
        gsb = bass.AP(self.gsum, 0, [[1, NE], [0, TT]])
        rbb = bass.AP(self.rb8, 0, [[1, NE], [0, TT]])

        @self.V
        def _(v):
            v.wait_ge(self.sem("sac"), ab_ready)
            R["ws"].wait_w(self, v)
            v.tensor_tensor(ws1[0:NE, :], P[2][0:NE, :], bf["ab_sb"][0:NE, :], OP.mult)
            v.tensor_tensor(ws2[0:NE, :], bf["bb_sb"][0:NE, :], gsb, OP.mult)
            self.inc("sdv", v.tensor_add(ws1[0:NE, :], ws1[0:NE, :], ws2[0:NE, :]))
        R["P"][2].rd("sdv", self.val("sdv"))
        logit_dv = self.val("sdv")

        @self.A
        def _(a):
            a.wait_ge(self.sem("sdv"), logit_dv)
            self.inc("sac", a.activation(out=ws2[0:NE, :], in_=ws1[0:NE, :], func=AF.Exp, scale=-1.0))
        ed = self.val("sac")

        @self.V
        def _(v):
            v.wait_ge(self.sem("sac"), ed)
            v.tensor_scalar_add(ws2[0:NE, :], ws2[0:NE, :], 1.0)
            v.reciprocal(ws2[0:NE, :], ws2[0:NE, :])
            v.tensor_tensor(ws1[32:32 + NE, :], ws2[0:NE, :], rbb, OP.add)
            self.inc("sdv", v.tensor_copy(ws1[0:NE, :], ws2[0:NE, :]))
        sel_dv = self.val("sdv")
        if self.lim.get("rstage", 9) <= 1:
            self.gb_ready = self.val("sac"); self.grow_read_mm = self.val("smm"); return

        rows = bf["rows"]
        sco = ws1[0:NE, :]
        xscr = xsq
        if self.lim.get("rstage", 9) <= 2:
            self.gb_ready = self.val("sac"); self.grow_read_mm = self.val("smm"); return

        def ptree_max(first_wait_dv, src_base, out_row):
            @self.S
            def _(sy):
                self.qdrain(sy, 'sx')
                sy.wait_ge(self.sem("sdv"), first_wait_dv)
                self.inc("sx", sy.dma_start(out=xscr[0:4, 0:TT], in_=ws1[src_base + 4:src_base + 8, :]), 16)
                self.inc("sx", sy.dma_start(out=ws2[0:4, :], in_=ws1[src_base:src_base + 4, :]), 16)
            d1 = self.val("sx")

            @self.V
            def _(v):
                v.wait_ge(self.sem("sx"), d1)
                self.inc("sdv", v.tensor_max(ws2[0:4, :], ws2[0:4, :], xscr[0:4, 0:TT]))
            v1 = self.val("sdv")

            @self.S
            def _(sy):
                self.qdrain(sy, 'sx')
                sy.wait_ge(self.sem("sdv"), v1)
                self.inc("sx", sy.dma_start(out=xscr[0:2, 0:TT], in_=ws2[2:4, :]), 16)
            d2 = self.val("sx")

            @self.V
            def _(v):
                v.wait_ge(self.sem("sx"), d2)
                self.inc("sdv", v.tensor_max(ws2[0:2, :], ws2[0:2, :], xscr[0:2, 0:TT]))
            v2 = self.val("sdv")

            @self.S
            def _(sy):
                self.qdrain(sy, 'sx')
                sy.wait_ge(self.sem("sdv"), v2)
                self.inc("sx", sy.dma_start(out=xscr[0:1, 0:TT], in_=ws2[1:2, :]), 16)
            d3 = self.val("sx")

            @self.V
            def _(v):
                v.wait_ge(self.sem("sx"), d3)
                self.inc("sdv", v.tensor_max(out_row, ws2[0:1, :], xscr[0:1, 0:TT]))
            return self.val("sdv")

        def bcast8(row_ap, rdy_sem, rdy, out8):
            @self.T
            def _(t):
                t.wait_ge(self.sem(rdy_sem), rdy)
                R["P"][1].wait_w(self, t)
                ii = None
                for ts in range(2):
                    ii = t.matmul(P[1][0:NE, ts * 512:(ts + 1) * 512], self.ones8f[0:1, :],
                                  row_ap[0:1, ts * 512:(ts + 1) * 512], start=True, stop=True)
                self.inc("smm", ii)
            mv = self.val("smm")

            @self.A
            def _(a):
                a.wait_ge(self.sem("smm"), mv)
                self.inc("sac", a.activation(out=out8, in_=P[1][0:NE, :], func=AF.Copy))
            ev = self.val("sac")
            R["P"][1].rd("sac", ev)
            return ev

        def psum8(in8_ap, rdy, out_row):
            @self.T
            def _(t):
                t.wait_ge(self.sem("sdv"), rdy)
                R["P"][1].wait_w(self, t)
                ii = None
                for ts in range(2):
                    ii = t.matmul(P[1][0:1, ts * 512:(ts + 1) * 512], self.ones8c[:, 0:1],
                                  in8_ap[:, ts * 512:(ts + 1) * 512], start=True, stop=True)
                self.inc("smm", ii)
            mv = self.val("smm")

            @self.A
            def _(a):
                a.wait_ge(self.sem("smm"), mv)
                self.inc("sac", a.activation(out=out_row, in_=P[1][0:1, :], func=AF.Copy))
            ev = self.val("sac")
            R["P"][1].rd("sac", ev)
            return ev

        m1_dv = ptree_max(sel_dv, 32, rows["b"])
        m1b_ev = bcast8(rows["b"], "sdv", m1_dv, ws2[32:32 + NE, :])

        @self.V
        def _(v):
            v.wait_ge(self.sem("sac"), m1b_ev)
            v.tensor_sub(ws1[64:64 + NE, :], ws1[32:32 + NE, :], ws2[32:32 + NE, :])
            v.tensor_single_scalar(ws1[64:64 + NE, :], ws1[64:64 + NE, :], 0.0, OP.is_ge)
            v.tensor_copy(ws2[0:NE, :], ws1[64:64 + NE, :])
            v.tensor_copy(xscr[32:32 + NE, 0:TT], ws1[64:64 + NE, :])
            self.inc("sdv", v.tensor_mul(xscr[0:NE, 0:TT], ws2[0:NE, :], sco))
        mask1_dv = self.val("sdv")
        s1_ev = psum8(xscr[0:NE, 0:TT], mask1_dv, rows["m"])

        @self.V
        def _(v):
            v.wait_ge(self.sem("sac"), s1_ev)
            self.inc("sdv", v.scalar_tensor_tensor(ws1[32:32 + NE, :], xscr[32:32 + NE, 0:TT],
                                                   -1e9, ws1[32:32 + NE, :], OP.mult, OP.add))
        sel2_dv = self.val("sdv")
        m2_dv = ptree_max(sel2_dv, 32, rows["b"])
        m2b_ev = bcast8(rows["b"], "sdv", m2_dv, ws2[32:32 + NE, :])

        @self.V
        def _(v):
            v.wait_ge(self.sem("sac"), m2b_ev)
            v.tensor_sub(ws1[32:32 + NE, :], ws1[32:32 + NE, :], ws2[32:32 + NE, :])
            v.tensor_single_scalar(ws1[32:32 + NE, :], ws1[32:32 + NE, :], 0.0, OP.is_ge)
            v.tensor_copy(xscr[32:32 + NE, 0:TT], sco)
            self.inc("sdv", v.tensor_mul(xscr[0:NE, 0:TT], ws1[32:32 + NE, :],
                                         xscr[32:32 + NE, 0:TT]))
        mask2_dv = self.val("sdv")
        s2_ev = psum8(xscr[0:NE, 0:TT], mask2_dv, rows["v"])

        @self.V
        def _(v):
            v.wait_ge(self.sem("sac"), s2_ev)
            v.tensor_add(rows["a"], rows["m"], rows["v"])
            v.tensor_scalar_max(rows["a"], rows["a"], 1e-12)
            v.reciprocal(rows["a"], rows["a"])
            v.tensor_scalar_mul(rows["a"], rows["a"], 0.5)
            v.tensor_mul(rows["b"], rows["m"], rows["a"])
            self.inc("sdv", v.tensor_mul(rows["a"], rows["v"], rows["a"]))
        w_dv = self.val("sdv")
        w1b_ev = bcast8(rows["b"], "sdv", w_dv, ws2[64:64 + NE, :])

        @self.V
        def _(v):
            v.wait_ge(self.sem("sac"), w1b_ev)
            self.inc("sdv", v.tensor_mul(ws2[0:NE, :], ws1[64:64 + NE, :],
                                         ws2[64:64 + NE, :]))
        g1_dv = self.val("sdv")
        w2b_ev = bcast8(rows["a"], "sdv", g1_dv, ws2[32:32 + NE, :])

        @self.V
        def _(v):
            v.wait_ge(self.sem("sac"), w2b_ev)
            R["g8"].wait_w(self, v)
            v.tensor_mul(xscr[0:NE, 0:TT], ws1[32:32 + NE, :], ws2[32:32 + NE, :])
            self.inc("sdv", v.tensor_add(g8[0:NE, :], ws2[0:NE, :], xscr[0:NE, 0:TT]))
        g8_dv = self.val("sdv")
        R["ws"].rd("sdv", g8_dv)
        R["xsq"].rd("sdv", g8_dv)

        @self.S
        def _(sy):
            self.qdrain(sy, 'sx')
            sy.wait_ge(self.sem("sdv"), g8_dv)
            R["gdr"].wait_w(self, sy)
            self.inc("sx", sy.dma_start(out=self.g_dr, in_=g8[:, :]), 16)
        grow_ld = self.val("sx")
        R["g8"].rd("sx", grow_ld)
        if self.lim.get("rstage", 9) <= 5:
            self.gb_ready = self.val("sac"); self.grow_read_mm = self.val("smm"); return

        grow_cp = {}
        growf = bf["growf"]
        cpv = []
        for e in range(NE):
            Pi = e % 2
            rowf = growf[0:1, (e % 2) * TT:(e % 2 + 1) * TT]

            @self.G
            def _(g):
                qn = self.gq_begin(g)
                if e == 0:
                    g.wait_ge(self.sem("sx"), grow_ld)
                if e >= 2:
                    g.wait_ge(self.sem("smm"), grow_cp[e - 2])
                self.inc(qn, g.dma_start(out=rowf, in_=self.g_dr[e:e + 1, :]), 16)
                cpv.append((qn, self.val(qn)))

            @self.T
            def _(t):
                t.wait_ge(self.sem(cpv[-1][0]), cpv[-1][1])
                R["P"][Pi].wait_w(self, t)
                ii = None
                for ts in range(2):
                    ii = t.matmul(P[Pi][0:128, ts * 512:(ts + 1) * 512], self.onesf[:, :],
                                  rowf[0:1, ts * 512:(ts + 1) * 512],
                                  start=True, stop=True)
                self.inc("smm", ii)
            grow_cp[e] = self.val("smm")
            mmv = self.val("smm")

            @self.A
            def _(a):
                a.wait_ge(self.sem("smm"), mmv)
                self.inc("sac", a.activation(out=bf["gb_sb"][:, e * TT:(e + 1) * TT],
                                             in_=P[Pi][:, 0:TT], func=AF.Copy))
            R["P"][Pi].rd("sac", self.val("sac"))
        R["gdr"].rd("sw", self.val("sw"))
        self.gb_ready = self.val("sac")
        self.grow_read_mm = self.val("smm")
        if self.lim.get("dump") and not hasattr(self, "_dumped"):
            self._dumped = True
            dmp_g8 = self.nc.dram_tensor("dmp_g8", [NE, TT], BF16, kind="ExternalOutput").ap()
            dmp_ws1 = self.nc.dram_tensor("dmp_ws1", [128, TT], F32, kind="ExternalOutput").ap()
            dmp_ws2 = self.nc.dram_tensor("dmp_ws2", [128, TT], F32, kind="ExternalOutput").ap()
            dmp_gb = self.nc.dram_tensor("dmp_gb", [128, 2 * TT], BF16, kind="ExternalOutput").ap()
            dmp_gf = self.nc.dram_tensor("dmp_gf", [1, 2 * TT], F32, kind="ExternalOutput").ap()
            dmp_wsm = self.nc.dram_tensor("dmp_wsm", [128, 16], F32, kind="ExternalOutput").ap()
            gbv = self.gb_ready

            @self.S
            def _(sy):
                self.qdrain(sy, 'sx')
                sy.wait_ge(self.sem("sac"), gbv)
                sy.wait_ge(self.sem("sw"), self.val("sw"))
                self.inc("sx", sy.dma_start(out=dmp_g8, in_=g8[:, :]), 16)
                self.inc("sx", sy.dma_start(out=dmp_ws1, in_=ws1[:, :]), 16)
                self.inc("sx", sy.dma_start(out=dmp_ws2, in_=ws2[:, :]), 16)
                self.inc("sx", sy.dma_start(out=dmp_gb, in_=bf["gb_sb"][:, 0:2 * TT]), 16)
                self.inc("sx", sy.dma_start(out=dmp_gf, in_=growf[0:1, :]), 16)
                self.inc("sx", sy.dma_start(out=dmp_wsm, in_=wsm[:, :]), 16)

    def _expert(self, e, bf):
        P, R = bf["P"], bf["R"]
        act_sb, xn_sb, x_sb = bf["act_sb"], bf["xn_sb"], bf["x_sb"]
        w1b, w3b, w2b = bf["w1b"], bf["w3b"], bf["w2b"]

        w_ready = {}

        def load_w13(ic):
            par = ic % 2
            @self.G
            def _(g):
                qn = self.gq_begin(g)
                R["w1b"][par].wait_w(self, g)
                R["w3b"][par].wait_w(self, g)
                self.inc(qn, g.dma_start(out=w1b[:, par * 2048:(par + 1) * 2048],
                                         in_=self.w1_p[e, ic]), 16)
                self.inc(qn, g.dma_start(out=w3b[:, par * 2048:(par + 1) * 2048],
                                         in_=self.w3_p[e, ic]), 16)
                w_ready[ic] = (qn, self.val(qn))

        load_w13(0)
        load_w13(1)

        act_evac = {}
        for ic in range(NIC):
            par = ic % 2
            @self.T
            def _(t):
                t.wait_ge(self.sem(w_ready[ic][0]), w_ready[ic][1])
                if ic == 0:
                    t.wait_ge(self.sem("sdv"), self.xn_ready)
                R["P"][par].wait_w(self, t)
                R["P"][2 + par].wait_w(self, t)
                for dc in range(NDC):
                    lhs1 = w1b[:, par * 2048 + dc * 128: par * 2048 + (dc + 1) * 128]
                    for ts in range(2):
                        t.matmul(P[par][:, ts * 512:(ts + 1) * 512], lhs1,
                                 xn_sb[:, dc * TT + ts * 512: dc * TT + (ts + 1) * 512],
                                 start=(dc == 0), stop=(dc == NDC - 1))
                ii = None
                for dc in range(NDC):
                    lhs3 = w3b[:, par * 2048 + dc * 128: par * 2048 + (dc + 1) * 128]
                    for ts in range(2):
                        ii = t.matmul(P[2 + par][:, ts * 512:(ts + 1) * 512], lhs3,
                                      xn_sb[:, dc * TT + ts * 512: dc * TT + (ts + 1) * 512],
                                      start=(dc == 0), stop=(dc == NDC - 1))
                self.inc("smm", ii)
            h_mm = self.val("smm")
            R["w1b"][par].rd("smm", h_mm)
            R["w3b"][par].rd("smm", h_mm)

            if ic + 2 < NIC:
                load_w13(ic + 2)

            @self.A
            def _(a):
                a.wait_ge(self.sem("smm"), h_mm)
                if ic == 0:
                    a.wait_ge(self.sem("sac"), self.gb_ready)
                R["act"].wait_w(self, a)
                self.inc("sac", a.activation(out=act_sb[:, ic * TT:(ic + 1) * TT],
                                             in_=P[par][:, :], func=AF.Silu))
            sl = self.val("sac")
            R["P"][par].rd("sac", sl)

            @self.V
            def _(v):
                v.wait_ge(self.sem("sac"), sl)
                v.tensor_tensor(act_sb[:, ic * TT:(ic + 1) * TT],
                                act_sb[:, ic * TT:(ic + 1) * TT], P[2 + par][:, :], OP.mult)
                self.inc("sdv", v.tensor_tensor(act_sb[:, ic * TT:(ic + 1) * TT],
                                                act_sb[:, ic * TT:(ic + 1) * TT],
                                                bf["gb_sb"][:, e * TT:(e + 1) * TT], OP.mult))
            act_evac[ic] = self.val("sdv")
            R["P"][2 + par].rd("sdv", act_evac[ic])

        w2_ready = {}

        def load_w2(do):
            par = do % 2
            @self.G
            def _(g):
                qn = self.gq_begin(g)
                R["w2b"][par].wait_w(self, g)
                self.inc(qn, g.dma_start(
                    out=w2b[:, par * NHC * 128: par * NHC * 128 + NIC * 128],
                    in_=self.w2_p[e, do]), 16)
                w2_ready[do] = (qn, self.val(qn))

        load_w2(0)
        load_w2(1)

        for do in range(NDC):
            par = do % 2
            @self.T
            def _(t):
                t.wait_ge(self.sem(w2_ready[do][0]), w2_ready[do][1])
                if do == 0:
                    t.wait_ge(self.sem("sdv"), act_evac[NIC - 1])
                R["P"][par].wait_w(self, t)
                ii = None
                for ic in range(NIC):
                    lhs = w2b[:, par * NHC * 128 + ic * 128: par * NHC * 128 + (ic + 1) * 128]
                    for ts in range(2):
                        ii = t.matmul(P[par][:, ts * 512:(ts + 1) * 512], lhs,
                                      act_sb[:, ic * TT + ts * 512: ic * TT + (ts + 1) * 512],
                                      start=(ic == 0), stop=(ic == NIC - 1))
                self.inc("smm", ii)
            y_mm = self.val("smm")
            R["w2b"][par].rd("smm", y_mm)
            if do == NDC - 1:
                R["act"].rd("smm", y_mm)

            if do + 2 < NDC:
                load_w2(do + 2)

            @self.V
            def _(v):
                v.wait_ge(self.sem("smm"), y_mm)
                self.inc("sdv", v.tensor_tensor(x_sb[:, do * TT:(do + 1) * TT],
                                                x_sb[:, do * TT:(do + 1) * TT],
                                                P[par][:, :], OP.add))
            R["P"][par].rd("sdv", self.val("sdv"))
        self.x_acc_dv = self.val("sdv")

    def _shared(self, half, bf):
        P, R = bf["P"], bf["R"]
        act_sb, xn_sb, x_sb = bf["act_sb"], bf["xn_sb"], bf["x_sb"]
        w1b, w3b, w2b = bf["w1b"], bf["w3b"], bf["w2b"]
        hofs = half * 512

        w_ready = {}

        def load_sw13(hc):
            par = hc % 2
            @self.G
            def _(g):
                qn = self.gq_begin(g)
                R["w1b"][par].wait_w(self, g)
                R["w3b"][par].wait_w(self, g)
                self.inc(qn, g.dma_start(out=w1b[:, par * 2048:(par + 1) * 2048],
                                         in_=self.sw1_p[hc]), 16)
                self.inc(qn, g.dma_start(out=w3b[:, par * 2048:(par + 1) * 2048],
                                         in_=self.sw3_p[hc]), 16)
                w_ready[hc] = (qn, self.val(qn))

        load_sw13(0)
        load_sw13(1)

        act_evac = {}
        for hc in range(NHC):
            par = hc % 2
            @self.T
            def _(t):
                t.wait_ge(self.sem(w_ready[hc][0]), w_ready[hc][1])
                R["P"][par].wait_w(self, t)
                R["P"][2 + par].wait_w(self, t)
                for dc in range(NDC):
                    t.matmul(P[par][:, 0:512],
                             w1b[:, par * 2048 + dc * 128: par * 2048 + (dc + 1) * 128],
                             xn_sb[:, dc * TT + hofs: dc * TT + hofs + 512],
                             start=(dc == 0), stop=(dc == NDC - 1))
                ii = None
                for dc in range(NDC):
                    ii = t.matmul(P[2 + par][:, 0:512],
                                  w3b[:, par * 2048 + dc * 128: par * 2048 + (dc + 1) * 128],
                                  xn_sb[:, dc * TT + hofs: dc * TT + hofs + 512],
                                  start=(dc == 0), stop=(dc == NDC - 1))
                self.inc("smm", ii)
            h_mm = self.val("smm")
            R["w1b"][par].rd("smm", h_mm)
            R["w3b"][par].rd("smm", h_mm)

            if hc + 2 < NHC:
                load_sw13(hc + 2)

            @self.A
            def _(a):
                a.wait_ge(self.sem("smm"), h_mm)
                R["act"].wait_w(self, a)
                self.inc("sac", a.activation(out=act_sb[:, hc * 512:(hc + 1) * 512],
                                             in_=P[par][:, 0:512], func=AF.Silu))
            sl = self.val("sac")
            R["P"][par].rd("sac", sl)

            @self.V
            def _(v):
                v.wait_ge(self.sem("sac"), sl)
                self.inc("sdv", v.scalar_tensor_tensor(
                    act_sb[:, hc * 512:(hc + 1) * 512],
                    act_sb[:, hc * 512:(hc + 1) * 512], 0.5,
                    P[2 + par][:, 0:512], OP.mult, OP.mult))
            act_evac[hc] = self.val("sdv")
            R["P"][2 + par].rd("sdv", act_evac[hc])

        w2_ready = {}

        def load_sw2(do):
            par = do % 2
            @self.G
            def _(g):
                qn = self.gq_begin(g)
                R["w2b"][par].wait_w(self, g)
                self.inc(qn, g.dma_start(out=w2b[:, par * NHC * 128:(par + 1) * NHC * 128],
                                         in_=self.sw2_p[do]), 16)
                w2_ready[do] = (qn, self.val(qn))

        load_sw2(0)
        load_sw2(1)

        for do in range(NDC):
            par = do % 2
            @self.T
            def _(t):
                t.wait_ge(self.sem(w2_ready[do][0]), w2_ready[do][1])
                if do == 0:
                    t.wait_ge(self.sem("sdv"), act_evac[NHC - 1])
                R["P"][par].wait_w(self, t)
                ii = None
                for hc in range(NHC):
                    ii = t.matmul(P[par][:, 0:512],
                                  w2b[:, par * NHC * 128 + hc * 128: par * NHC * 128 + (hc + 1) * 128],
                                  act_sb[:, hc * 512:(hc + 1) * 512],
                                  start=(hc == 0), stop=(hc == NHC - 1))
                self.inc("smm", ii)
            y_mm = self.val("smm")
            R["w2b"][par].rd("smm", y_mm)
            if do == NDC - 1:
                R["act"].rd("smm", y_mm)

            if do + 2 < NDC:
                load_sw2(do + 2)

            @self.V
            def _(v):
                v.wait_ge(self.sem("smm"), y_mm)
                self.inc("sdv", v.tensor_tensor(
                    x_sb[:, do * TT + hofs: do * TT + hofs + 512],
                    x_sb[:, do * TT + hofs: do * TT + hofs + 512],
                    P[par][:, 0:512], OP.add))
            R["P"][par].rd("sdv", self.val("sdv"))
        self.x_acc_dv = self.val("sdv")

    def _tile_epilogue(self, call_id, tt, bf, t0):
        P, R = bf["P"], bf["R"]
        x_sb, xn_sb, xsq, rows = bf["x_sb"], bf["xn_sb"], bf["xsq"], bf["rows"]
        x_done = self.x_acc_dv

        if call_id == 1:
            @self.V
            def _(v):
                v.wait_ge(self.sem("sdv"), x_done)
                v.wait_ge(self.sem("smm"), self.grow_read_mm)
                R["xsq"].wait_w(self, v)

            @self.T
            def _(t):
                t.wait_ge(self.sem("sdv"), x_done)
                R["P"][2].wait_w(self, t)
                R["P"][3].wait_w(self, t)
            ab_dv, _sm = self.ln_stats(x_sb, xsq, P[2], P[3], rows, R["xsq"])
            R["P"][2].rd("sdv", ab_dv)
            R["P"][3].rd("sdv", ab_dv)
            ab2 = self.bcast2(rows["a"], rows["b"], P[0], R["P"][0], P[1], R["P"][1],
                              bf["ab_sb"][:, :], bf["bb_sb"][:, :], "sdv", ab_dv)

            @self.V
            def _(v):
                v.wait_ge(self.sem("sac"), ab2)
                R["xn"].wait_w(self, v)
                R["xsq"].wait_w(self, v)
                ii = None
                for dc in range(NDC):
                    v.tensor_tensor(xsq[:, 0:TT], x_sb[:, dc * TT:(dc + 1) * TT],
                                    bf["ab_sb"][:, :], OP.mult)
                    ii = v.tensor_tensor(xn_sb[:, dc * TT:(dc + 1) * TT], xsq[:, 0:TT],
                                         bf["bb_sb"][:, :], OP.add)
                self.inc("sdv", ii)
            xn2 = self.val("sdv")

            @self.S
            def _(sy):
                self.qdrain(sy, 'sx')
                sy.wait_ge(self.sem("sdv"), xn2)
                for dc in range(NDC):
                    self.inc("sx", sy.dma_start(
                        out=self.xnA[dc * 128:(dc + 1) * 128, t0:t0 + TT],
                        in_=xn_sb[:, dc * TT:(dc + 1) * TT]), 16)
                    self.inc("sx", sy.dma_start(
                        out=self.x_res[dc * 128:(dc + 1) * 128, t0:t0 + TT],
                        in_=x_sb[:, dc * TT:(dc + 1) * TT]), 16)
                for nm, sbuf in (("dbg_xres", x_sb), ("dbg_xnA", xn_sb)):
                    if nm in self.dbg_t:
                        for dc in range(NDC):
                            self.inc("sx", sy.dma_start(
                                out=self.dbg_t[nm][dc * 128:(dc + 1) * 128, t0:t0 + TT],
                                in_=sbuf[:, dc * TT:(dc + 1) * TT]), 16)
            st = self.val("sx")
            R["x"].rd("sx", st)
            R["xn"].rd("sx", st)
        else:
            @self.S
            def _(sy):
                self.qdrain(sy, 'sx')
                sy.wait_ge(self.sem("sdv"), x_done)
                for dc in range(NDC):
                    self.inc("sx", sy.dma_start(
                        out=self.outT[dc * 128:(dc + 1) * 128, t0:t0 + TT],
                        in_=x_sb[:, dc * TT:(dc + 1) * TT]), 16)
            R["x"].rd("sx", self.val("sx"))

    def qkv_phase(self):
        nc = self.nc
        self.phase_barrier()
        barrier = self.phase_snap
        with ExitStack() as cctx:
            sb = lambda name, shape, dt: cctx.enter_context(nc.sbuf_tensor(name, shape, dt))
            xn_sb = sb("xnq", [128, NDC * TLOC], BF16)
            wbuf = sb("wqk", [128, 4 * 2048], BF16)
            wvb = sb("wvb", [128, NDC * 2048], BF16)
            cqb = sb("cqb", [128, D], F32)
            ckb = sb("ckb", [128, D], F32)
            bvb = sb("bvb", [128, D], F32)
            rws = sb("rws", [1, 3 * D], F32)
            stg = sb("qstg", [128, 2 * TLOC], BF16)

            with ExitStack() as px:
                PP = [px.enter_context(nc.psum_tensor(f"QP{i}", [128, TLOC], F32))
                      for i in range(2)]
                Rp = [Res(), Res()]
                Rw = [Res() for _ in range(4)]
                Rstg = [Res(), Res()]

                @self.S
                def _(sy):
                    self.qdrain(sy, 'sx')
                    self.wait_snap(sy, barrier)
                    sy.wait_ge(self.sem("sx"), self.rows_stored)
                    for dc in range(NDC):
                        self.inc("sx", sy.dma_start(out=xn_sb[:, dc * TLOC:(dc + 1) * TLOC],
                                                    in_=self.xnA[dc * 128:(dc + 1) * 128, :]), 16)
                    self.inc("sx", sy.dma_start(out=rws[0:1, 0:2 * D],
                                                in_=self.row_dr[0:2, :].rearrange("a d -> (a d)").rearrange("(o n) -> o n", o=1)), 16)
                    self.inc("sx", sy.dma_start(out=rws[0:1, 2 * D:3 * D], in_=self.bv_r), 16)
                    for bi in range(2):
                        self.inc("sx", sy.dma_start(
                            out=self.bias_t[:, bi * NDC:(bi + 1) * NDC],
                            in_=bass.AP(self.row_dr.tensor, (2 + bi) * D, [[1, 128], [128, NDC]]),
                            allow_slow_non_contiguous=True), 16)
                xn_ld = self.val("sx")

                @self.T
                def _(t):
                    self.wait_snap(t, barrier)
                    t.wait_ge(self.sem("sx"), xn_ld)
                    ii = None
                    for half in range(2):
                        for ts in range(D // 512):
                            ii = t.matmul(PP[half][:, ts * 512:(ts + 1) * 512],
                                          self.onesf[:, :],
                                          rws[0:1, half * D + ts * 512: half * D + (ts + 1) * 512],
                                          start=True, stop=True)
                    self.inc("smm", ii)
                cmm = self.val("smm")

                @self.A
                def _(a):
                    a.wait_ge(self.sem("smm"), cmm)
                    a.activation(out=cqb[:, :], in_=PP[0][:, 0:D], func=AF.Copy)
                    self.inc("sac", a.activation(out=ckb[:, :], in_=PP[1][:, 0:D], func=AF.Copy))
                Rp[0].rd("sac", self.val("sac"))
                Rp[1].rd("sac", self.val("sac"))

                @self.T
                def _(t):
                    Rp[0].wait_w(self, t)
                    ii = None
                    for ts in range(D // 512):
                        ii = t.matmul(PP[0][:, ts * 512:(ts + 1) * 512], self.onesf[:, :],
                                      rws[0:1, 2 * D + ts * 512: 2 * D + (ts + 1) * 512],
                                      start=True, stop=True)
                    self.inc("smm", ii)
                bvm = self.val("smm")

                @self.A
                def _(a):
                    a.wait_ge(self.sem("smm"), bvm)
                    self.inc("sac", a.activation(out=bvb[:, :], in_=PP[0][:, 0:D], func=AF.Copy))
                Rp[0].rd("sac", self.val("sac"))
                cb_done = self.val("sac")

                @self.G
                def _(g):
                    self.qdrain(g, 'sw')
                    for dc in range(NDC):
                        self.inc("sw", g.dma_start(out=wvb[:, dc * 2048:(dc + 1) * 2048],
                                                   in_=self.wv_p[dc]), 16)
                wv_ld = self.val("sw")

                w_ready = {}

                def load_w(pi, jc):
                    slot = pi * 2 + jc % 2
                    @self.G
                    def _(g):
                        qn = self.gq_begin(g)
                        Rw[slot].wait_w(self, g)
                        src = self.wq_p if pi == 0 else self.wk_p
                        self.inc(qn, g.dma_start(out=wbuf[:, slot * 2048:(slot + 1) * 2048],
                                                 in_=src[jc]), 16)
                        w_ready[(pi, jc)] = (qn, self.val(qn), slot)

                for pi in range(2):
                    load_w(pi, 0)
                    load_w(pi, 1)

                for pi in range(2):
                    scale = cqb if pi == 0 else ckb
                    bcol = pi
                    dst = self.q_dr if pi == 0 else self.kc_in.ap()
                    dbgn = "dbg_q" if pi == 0 else "dbg_k"
                    for jc in range(NDC):
                        wqn, wv, slot = w_ready[(pi, jc)]

                        @self.V
                        def _(v):
                            v.wait_ge(self.sem(wqn), wv)
                            v.wait_ge(self.sem("sac"), cb_done)
                            ii = None
                            for dc in range(NDC):
                                ii = v.tensor_tensor(
                                    wbuf[:, slot * 2048 + dc * 128: slot * 2048 + (dc + 1) * 128],
                                    wbuf[:, slot * 2048 + dc * 128: slot * 2048 + (dc + 1) * 128],
                                    scale[:, jc * 128:(jc + 1) * 128], OP.mult)
                            self.inc("sdv", ii)
                        wsc = self.val("sdv")

                        spar = jc % 2
                        @self.T
                        def _(t):
                            t.wait_ge(self.sem("sdv"), wsc)
                            if pi == 0 and jc == 0:
                                t.wait_ge(self.sem("sx"), xn_ld)
                            Rp[spar].wait_w(self, t)
                            ii = None
                            for dc in range(NDC):
                                lhs = wbuf[:, slot * 2048 + dc * 128: slot * 2048 + (dc + 1) * 128]
                                for ts in range(TLOC // 512):
                                    ii = t.matmul(PP[spar][:, ts * 512:(ts + 1) * 512], lhs,
                                                  xn_sb[:, dc * TLOC + ts * 512: dc * TLOC + (ts + 1) * 512],
                                                  start=(dc == 0), stop=(dc == NDC - 1))
                            self.inc("smm", ii)
                        pm = self.val("smm")
                        Rw[slot].rd("smm", pm)

                        if jc + 2 < NDC:
                            load_w(pi, jc + 2)

                        @self.A
                        def _(a):
                            a.wait_ge(self.sem("smm"), pm)
                            Rstg[spar].wait_w(self, a)
                            self.inc("sac", a.activation(
                                out=stg[:, spar * TLOC:(spar + 1) * TLOC],
                                in_=PP[spar][:, :], func=AF.Identity,
                                bias=self.bias_t[:, bcol * NDC + jc: bcol * NDC + jc + 1]))
                        ev = self.val("sac")
                        Rp[spar].rd("sac", ev)

                        @self.S
                        def _(sy):
                            self.qdrain(sy, 'sx')
                            sy.wait_ge(self.sem("sac"), ev)
                            self.inc("sx", sy.dma_start(
                                out=dst[jc * 128:(jc + 1) * 128, :],
                                in_=stg[:, spar * TLOC:(spar + 1) * TLOC]), 16)
                            if dbgn in self.dbg_t:
                                self.inc("sx", sy.dma_start(
                                    out=self.dbg_t[dbgn][jc * 128:(jc + 1) * 128, :],
                                    in_=stg[:, spar * TLOC:(spar + 1) * TLOC]), 16)
                        Rstg[spar].rd("sx", self.val("sx"))

                if self.lim.get("wdump"):
                    wdo = nc.dram_tensor("wdump", [128, NDC * 2048], BF16, kind="ExternalOutput").ap()
                    cdo = nc.dram_tensor("cdump", [128, 2 * D], F32, kind="ExternalOutput").ap()
                    @self.S
                    def _(sy):
                        self.qdrain(sy, 'sx')
                        sy.wait_ge(self.sem("sw"), wv_ld)
                        sy.wait_ge(self.sem("sac"), cb_done)
                        self.inc("sx", sy.dma_start(out=wdo, in_=wvb[:, :]), 16)
                        self.inc("sx", sy.dma_start(out=cdo[:, 0:D], in_=cqb[:, :]), 16)
                        self.inc("sx", sy.dma_start(out=cdo[:, D:2 * D], in_=ckb[:, :]), 16)

                vdst = self.vc_in.ap()
                for tc in range(NDC):
                    spar = tc % 2
                    @self.T
                    def _(t):
                        if tc < 2:
                            t.wait_ge(self.sem("sw"), wv_ld)
                        Rp[spar].wait_w(self, t)
                        ii = None
                        for dc in range(NDC):
                            lhs = xn_sb[:, dc * TLOC + tc * 128: dc * TLOC + (tc + 1) * 128]
                            for js in range(D // 512):
                                ii = t.matmul(PP[spar][:, js * 512:(js + 1) * 512], lhs,
                                              wvb[:, dc * 2048 + js * 512: dc * 2048 + (js + 1) * 512],
                                              start=(dc == 0), stop=(dc == NDC - 1))
                        self.inc("smm", ii)
                    pm = self.val("smm")

                    @self.V
                    def _(v):
                        v.wait_ge(self.sem("smm"), pm)
                        Rstg[spar].wait_w(self, v)
                        self.inc("sdv", v.tensor_add(
                            stg[:, spar * TLOC:(spar + 1) * TLOC],
                            PP[spar][:, :], bvb[:, :]))
                    ev = self.val("sdv")
                    Rp[spar].rd("sdv", ev)

                    @self.S
                    def _(sy):
                        self.qdrain(sy, 'sx')
                        sy.wait_ge(self.sem("sdv"), ev)
                        self.inc("sx", sy.dma_start(
                            out=vdst[tc * 128:(tc + 1) * 128, :],
                            in_=stg[:, spar * TLOC:(spar + 1) * TLOC]), 16)
                        if "dbg_v" in self.dbg_t:
                            self.inc("sx", sy.dma_start(
                                out=self.dbg_t["dbg_v"][tc * 128:(tc + 1) * 128, :],
                                in_=stg[:, spar * TLOC:(spar + 1) * TLOC]), 16)
                    Rstg[spar].rd("sx", self.val("sx"))
            self.phase_snap = self.snap()

    def gather_phase(self):
        kv_stored = self.val("sx")

        @self.G
        def _(g):
            self.qdrain(g, 'sw')
            g.wait_ge(self.sem("sx"), kv_stored)
            self.inc("scc", g.collective_compute(
                "AllGather", OP.bypass,
                replica_groups=[[0, 1, 2, 3], [4, 5, 6, 7]],
                ins=[self.kc_in.ap().opt()], outs=[self.kg.ap().opt()]))
            self.inc("scc", g.collective_compute(
                "AllGather", OP.bypass,
                replica_groups=[[0, 1, 2, 3], [4, 5, 6, 7]],
                ins=[self.vc_in.ap().opt()], outs=[self.vg.ap().opt()]))
        self.phase_snap = self.snap()

    def attn_phase(self):
        nc = self.nc
        self.phase_barrier()
        barrier = self.phase_snap
        kgap, vgap = self.kg.ap(), self.vg.ap()
        with ExitStack() as cctx:
            sb = lambda name, shape, dt: cctx.enter_context(nc.sbuf_tensor(name, shape, dt))
            k_sb = sb("k_sb", [128, 2 * SEQ], BF16)
            vt_sb = sb("vt_sb", [128, 2 * SEQ], BF16)
            o_sb = sb("o_sb", [128, NDC * TLOC], BF16)
            q_sb = sb("q_sb", [128, 4 * 512], BF16)
            pt_sb = sb("pt_sb", [128, 2 * 1024], BF16)
            lrow = sb("lrow", [1, 1024], F32)
            lb_sb = sb("lb_sb", [128, 1024], F32)
            wo_b = sb("wo_b", [128, 2 * 2048], BF16)
            ost = sb("ost", [128, 2 * 512], F32)

            with ExitStack() as px:
                S2 = [px.enter_context(nc.psum_tensor(f"S{i}", [128, 1024], F32))
                      for i in range(2)]
                O2 = [px.enter_context(nc.psum_tensor(f"O{i}", [128, 512], F32))
                      for i in range(2)]
                L2 = [px.enter_context(nc.psum_tensor(f"L{i}", [1, 512], F32))
                      for i in range(2)]
                RS = [Res(), Res()]
                RO = [Res(), Res()]
                RL = [Res(), Res()]
                Rk = [Res(), Res()]
                Rvt = [Res(), Res()]
                Rq = [Res() for _ in range(4)]
                Rpt = [Res(), Res()]
                Rlb = Res()
                Ro = Res()

                kv_ready = {}

                def load_kv(h):
                    par = h % 2
                    @self.S
                    def _(sy):
                        self.qdrain(sy, 'sx')
                        if h == 0:
                            self.wait_snap(sy, barrier)
                            sy.wait_ge(self.sem("scc"), self.val("scc"))
                        Rk[par].wait_w(self, sy)
                        for r in range(GRP):
                            self.inc("sx", sy.dma_start(
                                out=k_sb[:, par * SEQ + r * TLOC: par * SEQ + (r + 1) * TLOC],
                                in_=kgap[r * D + h * 128: r * D + (h + 1) * 128, :]), 16)
                    kx = self.val("sx")

                    @self.G
                    def _(g):
                        qn = self.gq_begin(g)
                        if h == 0:
                            self.wait_snap(g, barrier)
                            g.wait_ge(self.sem("scc"), self.val("scc"))
                        Rvt[par].wait_w(self, g)
                        self.inc(qn, g.dma_start(
                            out=vt_sb[:, par * SEQ: (par + 1) * SEQ]
                            .rearrange("p (kc j) -> p kc j", kc=NKC),
                            in_=vgap[:, h * 128:(h + 1) * 128]
                            .rearrange("(kc p) j -> p kc j", p=128)), 16)
                        kv_ready[h] = (kx, qn, self.val(qn))

                load_kv(0)

                q_ready = {}

                def load_q(h, qt):
                    slot = qt % 4
                    @self.S
                    def _(sy):
                        self.qdrain(sy, 'sx')
                        Rq[slot].wait_w(self, sy)
                        self.inc("sx", sy.dma_start(
                            out=q_sb[:, slot * 512:(slot + 1) * 512],
                            in_=self.q_dr[h * 128:(h + 1) * 128, qt * 512:(qt + 1) * 512]), 16)
                    q_ready[(h, qt)] = self.val("sx")

                load_q(0, 0)
                load_q(0, 1)

                NH = self.lim.get("nheads", HEADS)
                for h in range(NH):
                    par = h % 2
                    if h + 1 < NH:
                        load_kv(h + 1)
                    for qp in range(2):
                        qt0, qt1 = 2 * qp, 2 * qp + 1
                        s0, s1 = qt0 % 4, qt1 % 4
                        if not (h == 0 and qp == 0):
                            load_q(h, qt0)
                            load_q(h, qt1)
                        NKCL = self.lim.get("nkc", NKC)
                        exp_ev = {}

                        def emit_S(kc):
                            sp = kc % 2
                            @self.T
                            def _(t):
                                if kc == 0:
                                    t.wait_ge(self.sem("sx"), q_ready[(h, qt0)])
                                    t.wait_ge(self.sem("sx"), q_ready[(h, qt1)])
                                    t.wait_ge(self.sem("sx"), kv_ready[h][0])
                                    t.wait_ge(self.sem(kv_ready[h][1]), kv_ready[h][2])
                                RS[sp].wait_w(self, t)
                                lhsk = k_sb[:, par * SEQ + kc * 128: par * SEQ + (kc + 1) * 128]
                                t.matmul(S2[sp][:, 0:512], lhsk, q_sb[:, s0 * 512:(s0 + 1) * 512],
                                         start=True, stop=True)
                                ii = t.matmul(S2[sp][:, 512:1024], lhsk,
                                              q_sb[:, s1 * 512:(s1 + 1) * 512],
                                              start=True, stop=True)
                                self.inc("smm", ii)
                            smv = self.val("smm")

                            @self.A
                            def _(a):
                                a.wait_ge(self.sem("smm"), smv)
                                Rpt[kc % 2].wait_w(self, a)
                                self.inc("sac", a.activation(
                                    out=pt_sb[:, (kc % 2) * 1024:(kc % 2 + 1) * 1024],
                                    in_=S2[sp][:, :], func=AF.Exp))
                            exp_ev[kc] = self.val("sac")
                            RS[sp].rd("sac", exp_ev[kc])

                        def emit_AV(kc):
                            po = (kc % 2) * 1024
                            @self.T
                            def _(t):
                                t.wait_ge(self.sem("sac"), exp_ev[kc])
                                if kc == 0:
                                    RO[0].wait_w(self, t)
                                    RO[1].wait_w(self, t)
                                    RL[0].wait_w(self, t)
                                    RL[1].wait_w(self, t)
                                lhsv = vt_sb[:, par * SEQ + kc * 128: par * SEQ + (kc + 1) * 128]
                                t.matmul(O2[0][:, :], lhsv, pt_sb[:, po: po + 512],
                                         start=(kc == 0), stop=(kc == NKCL - 1))
                                t.matmul(O2[1][:, :], lhsv, pt_sb[:, po + 512: po + 1024],
                                         start=(kc == 0), stop=(kc == NKCL - 1))
                                t.matmul(L2[0][0:1, :], self.ones_bf[:, :],
                                         pt_sb[:, po: po + 512],
                                         start=(kc == 0), stop=(kc == NKCL - 1))
                                ii = t.matmul(L2[1][0:1, :], self.ones_bf[:, :],
                                              pt_sb[:, po + 512: po + 1024],
                                              start=(kc == 0), stop=(kc == NKCL - 1))
                                self.inc("smm", ii)
                            av = self.val("smm")
                            Rpt[kc % 2].rd("smm", av)
                            Rq[s0].rd("smm", av)
                            Rq[s1].rd("smm", av)
                            Rk[par].rd("smm", av)
                            Rvt[par].rd("smm", av)

                        emit_S(0)
                        for kc in range(1, NKCL):
                            emit_S(kc)
                            emit_AV(kc - 1)
                        emit_AV(NKCL - 1)
                        av_all = self.val("smm")

                        @self.V
                        def _(v):
                            v.wait_ge(self.sem("smm"), av_all)
                            v.tensor_copy(lrow[0:1, 0:512], L2[0][0:1, :])
                            v.tensor_copy(lrow[0:1, 512:1024], L2[1][0:1, :])
                            self.inc("sdv", v.reciprocal(lrow[0:1, :], lrow[0:1, :]))
                        lr = self.val("sdv")
                        RL[0].rd("sdv", lr)
                        RL[1].rd("sdv", lr)

                        @self.T
                        def _(t):
                            t.wait_ge(self.sem("sdv"), lr)
                            RS[1].wait_w(self, t)
                            ii = None
                            for ts in range(2):
                                ii = t.matmul(S2[1][:, ts * 512:(ts + 1) * 512], self.onesf[:, :],
                                              lrow[0:1, ts * 512:(ts + 1) * 512],
                                              start=True, stop=True)
                            self.inc("smm", ii)
                        lbm = self.val("smm")

                        @self.A
                        def _(a):
                            a.wait_ge(self.sem("smm"), lbm)
                            Rlb.wait_w(self, a)
                            self.inc("sac", a.activation(out=lb_sb[:, :], in_=S2[1][:, :],
                                                         func=AF.Copy))
                        lbe = self.val("sac")
                        RS[1].rd("sac", lbe)

                        @self.V
                        def _(v):
                            v.wait_ge(self.sem("sac"), lbe)
                            if self.lim.get("nol"):
                                v.tensor_copy(
                                    o_sb[:, h * TLOC + qt0 * 512: h * TLOC + (qt0 + 1) * 512],
                                    O2[0][:, :])
                                self.inc("sdv", v.tensor_copy(
                                    o_sb[:, h * TLOC + qt1 * 512: h * TLOC + (qt1 + 1) * 512],
                                    O2[1][:, :]))
                            else:
                                v.tensor_tensor(
                                    o_sb[:, h * TLOC + qt0 * 512: h * TLOC + (qt0 + 1) * 512],
                                    O2[0][:, :], lb_sb[:, 0:512], OP.mult)
                                self.inc("sdv", v.tensor_tensor(
                                    o_sb[:, h * TLOC + qt1 * 512: h * TLOC + (qt1 + 1) * 512],
                                    O2[1][:, :], lb_sb[:, 512:1024], OP.mult))
                        oev = self.val("sdv")
                        RO[0].rd("sdv", oev)
                        RO[1].rd("sdv", oev)
                        Rlb.rd("sdv", oev)
                o_done = self.val("sdv")

                if self.lim.get("ptdump"):
                    pdo = nc.dram_tensor("ptdump", [128, 2 * 1024], BF16, kind="ExternalOutput").ap()
                    sdo = nc.dram_tensor("sdump", [128, 1024], F32, kind="ExternalOutput").ap()
                    @self.V
                    def _(v):
                        v.wait_ge(self.sem("sdv"), o_done)
                        self.inc("sdv", v.tensor_copy(lb_sb[:, :], S2[0][:, :]))
                    sde = self.val("sdv")
                    @self.S
                    def _(sy):
                        self.qdrain(sy, 'sx')
                        sy.wait_ge(self.sem("sdv"), sde)
                        self.inc("sx", sy.dma_start(out=pdo, in_=pt_sb[:, :]), 16)
                        self.inc("sx", sy.dma_start(out=sdo, in_=lb_sb[:, :]), 16)

                if self.lim.get("odump"):
                    odo = nc.dram_tensor("odump", [D, TLOC], BF16, kind="ExternalOutput").ap()
                    @self.S
                    def _(sy):
                        self.qdrain(sy, 'sx')
                        sy.wait_ge(self.sem("sdv"), o_done)
                        for jc in range(NDC):
                            self.inc("sx", sy.dma_start(
                                out=odo[jc * 128:(jc + 1) * 128, :],
                                in_=o_sb[:, jc * TLOC:(jc + 1) * TLOC]), 16)

                Rwo = [Res(), Res()]
                Rost = [Res(), Res()]
                wo_ready = {}

                def load_wo(do):
                    par = do % 2
                    @self.G
                    def _(g):
                        qn = self.gq_begin(g)
                        Rwo[par].wait_w(self, g)
                        self.inc(qn, g.dma_start(out=wo_b[:, par * 2048:(par + 1) * 2048],
                                                 in_=self.wo_p[do]), 16)
                        wo_ready[do] = (qn, self.val(qn))

                load_wo(0)
                load_wo(1)

                for do in range(NDC):
                    par = do % 2
                    for qt in range(4):
                        opar = qt % 2
                        @self.T
                        def _(t):
                            t.wait_ge(self.sem(wo_ready[do][0]), wo_ready[do][1])
                            if do == 0 and qt == 0:
                                t.wait_ge(self.sem("sdv"), o_done)
                            RO[opar].wait_w(self, t)
                            ii = None
                            for jc in range(NDC):
                                lhs = wo_b[:, par * 2048 + jc * 128: par * 2048 + (jc + 1) * 128]
                                ii = t.matmul(O2[opar][:, :], lhs,
                                              o_sb[:, jc * TLOC + qt * 512: jc * TLOC + (qt + 1) * 512],
                                              start=(jc == 0), stop=(jc == NDC - 1))
                            self.inc("smm", ii)
                        wmm = self.val("smm")
                        Ro.rd("smm", wmm)
                        Rwo[par].rd("smm", wmm)

                        sslot = (do * 4 + qt) % 2
                        @self.A
                        def _(a):
                            a.wait_ge(self.sem("smm"), wmm)
                            Rost[sslot].wait_w(self, a)
                            self.inc("sac", a.activation(
                                out=ost[:, sslot * 512:(sslot + 1) * 512],
                                in_=O2[opar][:, :], func=AF.Identity,
                                bias=self.bias_t[:, 3 * NDC + do: 3 * NDC + do + 1]))
                        wev = self.val("sac")
                        RO[opar].rd("sac", wev)

                        @self.G
                        def _(g):
                            self.qdrain(g, 'sw')
                            g.wait_ge(self.sem("sac"), wev)
                            self.inc("sw", g.dma_start(
                                out=self.x_res[do * 128:(do + 1) * 128, qt * 512:(qt + 1) * 512],
                                in_=ost[:, sslot * 512:(sslot + 1) * 512],
                                accum_op=OP.add), 16)
                        Rost[sslot].rd("sw", self.val("sw"))
                    if do + 2 < NDC:
                        load_wo(do + 2)
            self.phase_snap = self.snap()

    def final_wait(self):
        @self.S
        def _(sy):
            self.qdrain(sy, 'sx')
            sy.wait_ge(self.sem("sx"), self.val("sx"))
            sy.wait_ge(self.sem("sw"), self.val("sw"))


def _pack_inputs(inputs):
    f32 = np.float32
    i = inputs
    pk4 = lambda WT: np.ascontiguousarray(
        WT.reshape(NDC, 128, NDC, 128).transpose(2, 1, 0, 3).reshape(NDC, 128, NDC * 128))
    d = {}
    d["wq_p"] = pk4(np.asarray(i["L0_attn_Wq"], f32).T.copy())
    d["wk_p"] = pk4(np.asarray(i["L0_attn_Wk"], f32).T.copy())
    d["wv_p"] = np.ascontiguousarray(np.asarray(i["L0_attn_Wv"], f32).T.reshape(NDC, 128, D))
    d["wo_p"] = pk4(np.asarray(i["L0_attn_Wo"], f32).T.copy())
    bt = np.zeros((128, 4 * NDC), f32)
    for c, nm in enumerate(("L0_attn_bq", "L0_attn_bk", "L0_attn_bv", "L0_attn_bo")):
        bt[:, c * NDC:(c + 1) * NDC] = np.asarray(i[nm], f32).reshape(NDC, 128).T
    d["biases"] = bt
    d["bv_r"] = np.asarray(i["L0_attn_bv"], f32).reshape(1, D).copy()
    d["bq_r"] = np.asarray(i["L0_attn_bq"], f32).reshape(1, D).copy()
    d["bk_r"] = np.asarray(i["L0_attn_bk"], f32).reshape(1, D).copy()
    hl = np.asarray(i["L0_attn_helix_rotation"], f32)
    sp = np.asarray(i["L0_attn_spiral_weight"], f32)
    d["helix_t"] = np.ascontiguousarray(hl.reshape(NDC, 128).T)
    d["helix_r"] = hl.reshape(1, D).copy()
    d["spiral_t"] = np.ascontiguousarray(np.broadcast_to(sp[None, :], (128, NDC))).astype(f32)
    d["spiral_r"] = np.repeat(sp, 128).reshape(1, D).copy()
    d["gwT_p"] = np.ascontiguousarray(np.asarray(i["L0_ffn_gate_w"], f32).T.reshape(NDC, 128, NE))
    d["rb"] = np.asarray(i["L0_ffn_router_bias"], f32).reshape(NE, 1).copy()
    W1 = np.asarray(i["L0_ffn_W1"], f32)
    W3 = np.asarray(i["L0_ffn_W3"], f32)
    W2 = np.asarray(i["L0_ffn_W2"], f32)
    d["w1_p"] = np.ascontiguousarray(
        W1.reshape(NE, NDC, 128, NIC, 128).transpose(0, 3, 2, 1, 4).reshape(NE, NIC, 128, NDC * 128))
    d["w3_p"] = np.ascontiguousarray(
        W3.reshape(NE, NDC, 128, NIC, 128).transpose(0, 3, 2, 1, 4).reshape(NE, NIC, 128, NDC * 128))
    d["w2_p"] = np.ascontiguousarray(
        W2.reshape(NE, NIC, 128, NDC, 128).transpose(0, 3, 2, 1, 4).reshape(NE, NDC, 128, NIC * 128))
    sw1T = np.asarray(i["L0_ffn_sw1"], f32).T.copy()
    sw3T = np.asarray(i["L0_ffn_sw3"], f32).T.copy()
    sw2T = np.asarray(i["L0_ffn_sw2"], f32).T.copy()
    d["sw1_p"] = np.ascontiguousarray(
        sw1T.reshape(NDC, 128, NHC, 128).transpose(2, 1, 0, 3).reshape(NHC, 128, NDC * 128))
    d["sw3_p"] = np.ascontiguousarray(
        sw3T.reshape(NDC, 128, NHC, 128).transpose(2, 1, 0, 3).reshape(NHC, 128, NDC * 128))
    d["sw2_p"] = np.ascontiguousarray(
        sw2T.reshape(NHC, 128, NDC, 128).transpose(2, 1, 0, 3).reshape(NDC, 128, NHC * 128))
    d["identf"] = np.eye(128, dtype=np.float32)
    c = np.zeros((1, 8), np.float32); c[0, :len(CONSTS)] = CONSTS
    d["consts"] = c
    return d


_NC_CACHE = {}


def _get_nc(dbg=(), phases=("ffn1", "qkv", "gather", "attn", "ffn2"), lim=None):
    key = (tuple(sorted(dbg)), tuple(phases), tuple(sorted((lim or {}).items())))
    if key not in _NC_CACHE:
        _NC_CACHE[key] = Builder(bass.Bass("TRN2", target_bir_lowering=False,
                                           detect_race_conditions=False), dbg,
                                 phases, lim).build()
    return _NC_CACHE[key]


def kernel(**inputs):
    x = np.asarray(inputs["L0_attn_x"], np.float32)
    shared = _pack_inputs(inputs)
    nc = _get_nc()
    in_maps = []
    for c in range(8):
        seq, t0 = c // GRP, (c % GRP) * TLOC
        m = dict(shared)
        m["xT"] = np.ascontiguousarray(x[seq, t0:t0 + TLOC, :].T)
        in_maps.append(m)
    res = run_bass_kernel_spmd(nc, in_maps, core_ids=list(range(8)))
    out = np.empty((2, SEQ, D), np.float32)
    for c in range(8):
        seq, t0 = c // GRP, (c % GRP) * TLOC
        out[seq, t0:t0 + TLOC, :] = res.results[c]["outT"].T
    return out
